# version 05
# speedup vs baseline: 1.9178x; 1.0457x over previous
import numpy as np
import ml_dtypes
from contextlib import ExitStack
import concourse.bass as bass
import concourse.mybir as mybir

F32 = mybir.dt.float32
BF16 = mybir.dt.bfloat16
I32 = mybir.dt.int32
AF = mybir.ActivationFunctionType
ALU = mybir.AluOpType
AX = mybir.AxisListType

T = 4096
D = 1024
H = 16
DH = 64
G = 512
NG = T // G
FFN = 4096
TSEL = 1024
EPS_LN = 1e-5
EPS_RMS = 1e-6
NEG = -1e9

ENGINES = ("sync", "vector", "scalar", "tensor", "gpsimd")


class Sem:
    __slots__ = ("name", "v", "h")
    def __init__(self, name):
        self.name = name
        self.v = 0
        self.h = None


class Prog:
    def __init__(self):
        self.steps = {e: [] for e in ENGINES}
        self.sems = []

    def sem(self, name):
        s = Sem(name)
        self.sems.append(s)
        return s

    def add(self, eng, fn=None, wait=(), inc=(), is_dma=False):
        wait = tuple(t for t in wait if t is not None)
        incs = []
        for s, n in inc:
            s.v += n
            incs.append((s, n, s.v))
        self.steps[eng].append((fn, tuple(wait), tuple(incs), is_dma))
        return [v for s, n, v in incs]

    def barrier(self, name):
        bs = self.sem("bar_" + name)
        for e in ENGINES:
            self.add(e, None, inc=[(bs, 1)])
        v = bs.v
        for e in ENGINES:
            self.add(e, None, wait=[(bs, v)])
        return bs

    def emit(self, block, sim_edges=False):
        if sim_edges and not hasattr(self, "ord"):
            self.ord = {e: Sem("ord_" + e) for e in ENGINES}
        def make(engname):
            def run(eng):
                osem = self.ord[engname] if sim_edges else None
                ocnt = 0
                prev_edge = None
                for fn, waits, incs, is_dma in self.steps[engname]:
                    if sim_edges and not is_dma and fn is not None and prev_edge:
                        eng.wait_ge(prev_edge[0].h, prev_edge[1])
                    for s, v in waits:
                        eng.wait_ge(s.h, v)
                    r = fn(eng) if fn is not None else None
                    if incs:
                        if r is None:
                            r = eng.nop()
                        for s, n, tv in incs:
                            r = r.then_inc(s.h, n)
                    if sim_edges and not is_dma and fn is not None:
                        if incs:
                            prev_edge = (incs[0][0], incs[0][2])
                        else:
                            ocnt += 1
                            r.then_inc(osem.h, 1)
                            prev_edge = (osem, ocnt)
            return run
        block.sync(make("sync"))
        block.vector(make("vector"))
        block.scalar(make("scalar"))
        block.tensor(make("tensor"))
        block.gpsimd(make("gpsimd"))



def host_consts():
    c = {}
    ey = np.eye(128)
    c["identity"] = ey.astype(ml_dtypes.bfloat16)
    iu = np.arange(128)
    c["SUge"] = (iu[None, :] >= iu[:, None]).astype(ml_dtypes.bfloat16)
    c["SUst"] = (iu[None, :] > iu[:, None]).astype(ml_dtypes.bfloat16)
    sh = np.zeros((128, 2048), np.float32)
    for kt in range(4):
        for f in range(512):
            r = f + 1 - 128 * kt
            if r <= 0:
                sh[0, 512 * kt + f] = NEG
            elif r < 128:
                sh[r, 512 * kt + f] = NEG
    c["Sh4"] = sh.astype(ml_dtypes.bfloat16)
    pp = (1024 * (np.arange(128)[:, None] // 32) + np.arange(1024)[None, :]).astype(np.float32)
    c["pos_packed"] = pp
    inv = (1.0 / 10000.0 ** (np.arange(32, dtype=np.float64) / 32.0))
    c["inv_col"] = np.tile(inv.reshape(32, 1), (4, 1)).astype(np.float32)
    slopes = (2.0 ** (-np.linspace(1.0, 8.0, H))).astype(np.float64)
    c["slopes_col"] = np.zeros((128, 1), np.float32)
    c["slopes_col"][:H, 0] = slopes.astype(np.float32)
    c["iota_row"] = np.arange(4096, dtype=np.float32).reshape(1, 4096)
    c["iota16"] = np.tile(np.arange(4096, dtype=np.float32)[None, :], (16, 1))
    c["iotaP_col"] = np.arange(128, dtype=np.float32).reshape(128, 1)
    c["ones_col_bf"] = np.ones((128, 1), ml_dtypes.bfloat16)
    c["ones16"] = np.ones((16, 512), ml_dtypes.bfloat16)
    c["iota1024_2d"] = np.tile(np.arange(1024, dtype=np.float32)[None, :], (128, 1))
    c["iota4096_2d"] = np.tile(np.arange(4096, dtype=np.float32)[None, :], (128, 1))
    c["hi128_row"] = (128.0 * np.arange(32, dtype=np.float32)).reshape(1, 32)
    inv64 = 1.0 / 10000.0 ** (np.arange(32, dtype=np.float64) / 32.0)
    aa, rr = np.meshgrid(np.arange(4), np.arange(32), indexing="ij")
    cpk = np.zeros((128, 1024), np.float64)
    spk = np.zeros((128, 1024), np.float64)
    for a in range(4):
        for b in range(2):
            pos = 1024.0 * a + 512.0 * b + np.arange(512)
            ang = inv64[:, None] * pos[None, :]
            cpk[32 * a:32 * a + 32, 512 * b:512 * b + 512] = np.cos(ang)
            spk[32 * a:32 * a + 32, 512 * b:512 * b + 512] = np.sin(ang)
    c["cos_pk"] = cpk.astype(np.float32)
    c["sin_pk"] = spk.astype(np.float32)
    slopes64 = 2.0 ** (-np.linspace(1.0, 8.0, H))
    iota = np.arange(T, dtype=np.float64)
    def hml(v):
        hi = v.astype(ml_dtypes.bfloat16).astype(np.float64)
        r1 = v - hi
        mid = r1.astype(ml_dtypes.bfloat16).astype(np.float64)
        lo = (r1 - mid).astype(ml_dtypes.bfloat16).astype(np.float64)
        return hi, mid, lo
    kv = slopes64[:, None] * iota[None, :]
    qv = -kv
    for nm, v in [("kaug", kv), ("qaug", qv)]:
        parts = hml(v)
        rows = np.zeros((NG * 3, H * G), np.float64)
        for g in range(NG):
            for k in range(3):
                rows[3 * g + k] = parts[k][:, g * G:(g + 1) * G].reshape(-1)
        c[nm] = rows.astype(ml_dtypes.bfloat16)
    return c


def permute_w_rope(w):
    cols = []
    for g in range(4):
        for h in range(4 * g, 4 * g + 4):
            cols.extend(range(h * 64, h * 64 + 32))
        for h in range(4 * g, 4 * g + 4):
            cols.extend(range(h * 64 + 32, h * 64 + 64))
    return np.ascontiguousarray(w[:, cols])


def split_qkv(wqkv, bqkv):
    Wr = wqkv.reshape(1024, H, 3, DH)
    br = bqkv.reshape(H, 3, DH)
    out = {}
    for i, r in enumerate("qkv"):
        out["W" + r] = np.ascontiguousarray(Wr[:, :, i, :].reshape(1024, 1024))
        out["b" + r] = np.ascontiguousarray(br[:, i, :].reshape(1024))
    return out



"""Stage-1 program builder. Phase A (attn block + LN2) first.

Layout: feature-major activations [d on partitions (8 tiles of 128), tokens free].
"""
import numpy as np
import ml_dtypes
from contextlib import ExitStack
import concourse.bass as bass
import concourse.mybir as mybir

AP = bass.AP
C0 = 20.0


class Ctx:
    def __init__(self, nc):
        self.nc = nc
        self.p = Prog()
        self.es = ExitStack()
        self.t = {}
        p = self.p
        self.sV = p.sem("sV"); self.sS = p.sem("sS")
        self.sT = p.sem("sT"); self.sG = p.sem("sG")
        NPOOL = 12
        self.dpool = {q: [p.sem(f"d_{q}_{i}") for i in range(NPOOL)]
                      for q in ("sync", "gpsimd", "scalar")}
        self.dcnt = {q: 0 for q in self.dpool}
        self.duse = {}
        self.grd = {}
        self.outs = []

    def sb(self, name, shape, dt):
        t = self.es.enter_context(self.nc.sbuf_tensor(name, shape, dt))
        self.t[name] = t
        return t

    def psum(self, name, shape=(128, 512), dt=F32):
        t = self.es.enter_context(self.nc.psum_tensor(name, list(shape), dt))
        self.t[name] = t
        return t

    def dma(self, queue, out, in_, wait=()):
        pool = self.dpool[queue]
        sem = pool[self.dcnt[queue] % len(pool)]
        self.dcnt[queue] += 1
        prev = self.duse.get(sem.name, 0)
        w = [t for t in wait if t is not None] + ([(sem, 16 * prev)] if prev else [])
        self.duse[sem.name] = prev + 1
        self.p.add(queue, lambda e, o=out, i=in_: e.dma_start(out=o, in_=i),
                   wait=w, inc=[(sem, 16)], is_dma=True)
        return (sem, 16 * (prev + 1))

    def join(self, engine, tokens):
        es_ = {"vector": self.sV, "scalar": self.sS, "tensor": self.sT,
               "gpsimd": self.sG}[engine]
        best = {}
        for t in tokens:
            if t is None:
                continue
            s_, v_ = t
            if v_ > best.get(s_.name, (None, -1))[1]:
                best[s_.name] = (s_, v_)
        v = self.p.add(engine, None, wait=list(best.values()), inc=[(es_, 1)])[0]
        return (es_, v)


def wtile_ap(Wdram, f, ncols=128, nkt=8, col0=0):
    ld = Wdram.shape[1]
    return AP(Wdram, col0 + f * ncols, [[ld, 128], [128 * ld, nkt], [1, ncols]])


def build_stage1(n_groups=NG, upto="A"):
    nc = bass.Bass("TRN2", target_bir_lowering=False, detect_race_conditions=False)
    c = Ctx(nc)
    p = c.p
    grd = c.grd

    def din(name, shape, dt=F32):
        return nc.dram_tensor(name, shape, dt, kind="ExternalInput").ap()
    def dout(name, shape, dt=F32):
        return nc.dram_tensor(name, shape, dt, kind="ExternalOutput").ap()

    HC_ALL = host_consts()
    NEEDED = ["SUge", "Sh4", "ones_col_bf", "cos_pk", "sin_pk"]
    DRAMONLY = ["ones16", "kaug", "qaug"]
    HC = {n: HC_ALL[n] for n in NEEDED + DRAMONLY}
    XT = din("xT", [D, T])
    WD = {n: din(n, [D, D]) for n in ["Wk", "Wq", "Wv", "Wo"]}
    BCOLS = {n: din(n, [128, 8]) for n in ["bk", "bq", "bv", "bo"]}
    BVREP = din("bvrep", [128, D])
    CONSTS = {n: din("c_" + n, list(v.shape),
                     BF16 if v.dtype == ml_dtypes.bfloat16 else F32)
              for n, v in HC.items()}
    X1T = dout("x1T", [D, T])
    HIDT_O = dout("hidT", [D, T], BF16)
    W1 = {n: din(n, shp) for n, shp in [
        ("wq1p", [D, D]), ("wk1p", [D, D]), ("wv1", [D, D]), ("wo1", [D, D]),
        ("wg1", [D, FFN]), ("wu1", [D, FFN]), ("wd1", [FFN, D]),
        ("prior_w", [D, 2 * D])]}
    GCOLS = {n: din(n, [128, 8]) for n in ["g11", "g21", "gpn"]}
    PROCT = dout("procT", [D, T])
    DST = dout("dst", [1, T]); DCH = dout("dch", [1, T])
    DSA = dout("dsa", [1, 1]); DSB = dout("dsb", [1, 1])
    DBG_QT = dout("dbg_qt", [70, H * G], BF16)
    DBG_KT = dout("dbg_kt", [70, H * G], BF16)
    DBG_VG = dout("dbg_vg", [128, 4 * H * 65], BF16)
    DBG_OT = dout("dbg_ot", [D, G], BF16)
    DBG_PS = dout("dbg_ps", [128, G])
    DBG_PS2 = dout("dbg_ps2", [128, G])

    wsc = {n: nc.dram_tensor("sc_" + n, [D, D], BF16) for n in WD}
    for n, shp in [("Wq1", [D, D]), ("Wk1", [D, D]), ("Wv1", [D, D]), ("Wo1", [D, D]),
                   ("Wg1", [D, FFN]), ("Wu1", [D, FFN]), ("Wd1", [FFN, D]),
                   ("PW", [D, 2 * D])]:
        wsc[n] = nc.dram_tensor("sc_" + n, shp, BF16)
    KT1d = nc.dram_tensor("sc_KT1", [64, H * T], BF16)
    V1d = nc.dram_tensor("sc_V1", [128, H * 32 * 65], BF16)
    KTab = nc.dram_tensor("sc_KTab", [70, H * T], BF16)
    Vab = nc.dram_tensor("sc_Vab", [128, H * 32 * 65], BF16)
    BNC = nc.dram_tensor("sc_bnc", [16, G], BF16)

    with c.es:
        cst = {n: c.sb("k_" + n, list(HC[n].shape),
                       BF16 if HC[n].dtype == ml_dtypes.bfloat16 else F32)
               for n in NEEDED}
        onesr = c.sb("onesr", [1, 128], BF16)
        xg = [c.sb(f"xg{i}", [128, G], F32) for i in range(8)]
        xnbf = [c.sb(f"xnbf{i}", [128, G], BF16) for i in range(8)]
        sqb = [c.sb(f"sqb{i}", [128, G], BF16) for i in range(2)]
        tmpf = [c.sb(f"tmpf{i}", [128, G], F32) for i in range(2)]
        rows = {n: c.sb("r_" + n, [1, G], F32) for n in
                ["m", "var", "rstd", "recip", "sc1"]}
        rows["e2"] = rows["sc1"]
        rows_bf = {n: c.sb("rb_" + n, [1, G], BF16) for n in ["m", "rstd", "recip"]}
        eps1 = c.sb("eps1", [1, 1], F32)
        eps2 = c.sb("eps2", [1, 1], F32)
        dsa = c.sb("dsa_s", [1, 1], F32)
        dsb = c.sb("dsb_s", [1, 1], F32)
        h1bf = [c.sb(f"h1bf{i}", [128, G], BF16) for i in range(8)]
        sgbf = c.sb("sgbf", [128, G], BF16)
        swig = c.sb("swig", [128, 32 * G], BF16)
        mubf = [c.sb(f"mubf{i}", [128, G], BF16) for i in range(8)]
        resbf = [c.sb(f"resbf{i}", [128, G], BF16) for i in range(8)]
        cosrep = c.sb("cosrep", [128, G], F32)
        sinrep = c.sb("sinrep", [128, G], F32)
        rtmp = c.sb("rtmp", [128, G], F32)
        ebuf = c.sb("ebuf", [128, G], F32)
        c0t = c.sb("c0t", [128, 1], F32)
        bcol = {n: c.sb("bc_" + n, [128, 8], F32) for n in BCOLS}
        for n in ["g11", "g21", "gpn"]:
            bcol[n] = c.sb("bc_" + n, [128, 8], F32)
        wr = [c.sb(f"wr{i}", [128, 2048], BF16) for i in range(4)]
        QTa = c.sb("QTa", [70, H * G], BF16)
        KTs = c.sb("KTs", [70, H * G], BF16)
        Vg = c.sb("Vg", [128, 4 * H * 65], BF16)
        expS = [c.sb(f"expS{i}", [128, G], BF16) for i in range(4)]
        recb = c.sb("recb", [64, G], F32)
        OTbf = [c.sb(f"OTbf{i}", [128, G], BF16) for i in range(8)]
        hidg = [c.sb(f"hidg{i}", [128, G], BF16) for i in range(8)]
        bvb = c.sb("bvb", [128, 1024], F32)
        pb = [c.psum(f"pb{i}") for i in range(8)]

        sV, sS, sT, sG = c.sV, c.sS, c.sT, c.sG

        toks = []
        for n, t in cst.items():
            toks.append(c.dma("sync", t[:], CONSTS[n]))
        for n in BCOLS:
            toks.append(c.dma("sync", bcol[n][:], BCOLS[n]))
        for n in ["g11", "g21", "gpn"]:
            toks.append(c.dma("sync", bcol[n][:], GCOLS[n]))
        toks.append(c.dma("sync", bvb[:], BVREP))
        wtoks = []
        srcmap = dict(WD)
        srcmap.update({"Wq1": W1["wq1p"], "Wk1": W1["wk1p"], "Wv1": W1["wv1"],
                       "Wo1": W1["wo1"], "Wg1": W1["wg1"], "Wu1": W1["wu1"],
                       "Wd1": W1["wd1"], "PW": W1["prior_w"]})
        for n in wsc:
            wtoks.append(c.dma("gpsimd", wsc[n].ap(), srcmap[n]))
        v_wcast = c.join("gpsimd", wtoks)

        p.add("vector", lambda e: e.memset(onesr[:], 1.0))
        p.add("vector", lambda e: e.memset(eps1[:], EPS_LN))
        p.add("vector", lambda e: e.memset(eps2[:], EPS_RMS))
        p.add("vector", lambda e: e.memset(dsa[:], 0.0))
        p.add("vector", lambda e: e.memset(dsb[:], 0.0))
        p.add("vector", lambda e: e.memset(c0t[:], -C0))
        p.add("vector", lambda e: e.memset(
            AP(Vg, 64, [[4 * H * 65, 128], [65, 4 * H], [1, 1]]), 1.0))
        for k in range(3):
            toks.append(c.dma("sync",
                  AP(KTs, (67 + k) * (H * G), [[H * G, 1], [1, H * G]]),
                  AP(CONSTS["ones16"].tensor, 0, [[H * G, 1], [1, H * G]])))
            toks.append(c.dma("sync",
                  AP(QTa, (64 + k) * (H * G), [[H * G, 1], [1, H * G]]),
                  AP(CONSTS["ones16"].tensor, 0, [[H * G, 1], [1, H * G]])))
        v_cl = c.join("vector", toks)
        v_init = p.add("vector", None, wait=[v_cl], inc=[(sV, 1)])[0]
        p.add("vector", lambda e: e.tensor_scalar(
            bcol["bq"][:], bcol["bq"][:], 0.125, None, ALU.mult), wait=[v_cl])
        pass

        def ln_grp(src_aps, dst_aps, rms=False, gains=None, eps_v=None,
                   wait_first=(), wait_per_kt=None):
            vc = []
            for kt in range(8):
                w = list(wait_first if kt == 0 else ())
                if wait_per_kt:
                    w = w + [wait_per_kt[kt]]
                vc.append(p.add("vector", lambda e, kt=kt: e.tensor_copy(
                    xnbf[kt][:], src_aps[kt]), wait=w, inc=[(sV, 1)])[0])
            if not rms:
                for kt in range(8):
                    p.add("tensor", lambda e, kt=kt: e.matmul(
                        pb[0][0:1, :], cst["ones_col_bf"][:], xnbf[kt][:],
                        start=(kt == 0), stop=(kt == 7)),
                        wait=([(sV, vc[kt]), grd.get("bank0")]
                              if kt == 0 else [(sV, vc[kt])]),
                        inc=([(sT, 1)] if kt == 7 else ()))
                t_sum = sT.v
            sqv = []
            sqmm = {}
            for kt in range(8):
                sqv.append(p.add("vector", lambda e, kt=kt: e.tensor_tensor(
                    sqb[kt % 2][:], xnbf[kt][:], xnbf[kt][:], ALU.mult),
                    wait=[(sV, vc[kt])] + ([(sT, sqmm[kt - 2])] if kt >= 2 else []),
                    inc=[(sV, 1)])[0])
                sqmm[kt] = p.add("tensor", lambda e, kt=kt: e.matmul(
                    pb[1][0:1, :], cst["ones_col_bf"][:], sqb[kt % 2][:],
                    start=(kt == 0), stop=(kt == 7)),
                    wait=([(sV, sqv[kt]), grd.get("bank1"), grd.get("bank0")]
                          if kt == 0 else [(sV, sqv[kt])]),
                    inc=[(sT, 1)])[0]
            t_sq = sT.v
            grd["xnbf_cons"] = (sT, t_sq)
            if rms:
                v_v = p.add("vector", lambda e: e.tensor_scalar(
                    rows["var"][:], pb[1][0:1, :], 1.0 / D, None, ALU.mult),
                    wait=[(sT, t_sq)], inc=[(sV, 1)])[0]
            else:
                p.add("vector", lambda e: e.tensor_scalar(
                    rows["m"][:], pb[0][0:1, :], 1.0 / D, None, ALU.mult),
                    wait=[(sT, t_sum), grd.get("rowm_free")])
                p.add("vector", lambda e: e.tensor_scalar(
                    rows["e2"][:], pb[1][0:1, :], 1.0 / D, None, ALU.mult),
                    wait=[(sT, t_sq)])
                p.add("vector", lambda e: e.tensor_tensor(
                    rows["var"][:], rows["m"][:], rows["m"][:], ALU.mult))
                v_v = p.add("vector", lambda e: e.tensor_tensor(
                    rows["var"][:], rows["e2"][:], rows["var"][:], ALU.subtract),
                    inc=[(sV, 1)])[0]
            s1 = p.add("scalar", lambda e, ev=eps_v: e.activation(
                rows["sc1"][:], rows["var"][:], AF.Ln, bias=ev, scale=1.0),
                wait=[(sV, v_v)], inc=[(sS, 1)])[0]
            s2 = p.add("scalar", lambda e: e.activation(
                rows["rstd"][:], rows["sc1"][:], AF.Exp, bias=0.0, scale=-0.5),
                wait=[(sS, s1)], inc=[(sS, 1)])[0]
            v_rb = p.add("vector", lambda e: e.tensor_copy(
                rows_bf["rstd"][:], rows["rstd"][:]),
                wait=[(sS, s2), grd.get("lnrows")],
                inc=[(sV, 1)])[0]
            if not rms:
                v_mb = p.add("vector", lambda e: e.tensor_copy(
                    rows_bf["m"][:], rows["m"][:]), inc=[(sV, 1)])[0]
                p.add("tensor", lambda e: e.matmul(
                    pb[0][:], onesr[:], rows_bf["m"][:], start=True, stop=True),
                    wait=[(sV, v_mb)])
            p.add("tensor", lambda e: e.matmul(
                pb[1][:], onesr[:], rows_bf["rstd"][:], start=True, stop=True),
                wait=[(sV, v_rb)], inc=[(sT, 1)])
            t_bc = sT.v
            grd["lnrows"] = (sT, t_bc)
            out_v = []
            for kt in range(8):
                if rms:
                    out_v.append(p.add("vector", lambda e, kt=kt:
                        e.scalar_tensor_tensor(dst_aps[kt], src_aps[kt],
                                               bcol[gains][:, kt:kt + 1], pb[1][:],
                                               ALU.mult, ALU.mult),
                        wait=([(sT, t_bc)] if kt == 0 else ()), inc=[(sV, 1)])[0])
                else:
                    p.add("vector", lambda e, kt=kt: e.tensor_tensor(
                        tmpf[0][:], src_aps[kt], pb[0][:], ALU.subtract),
                        wait=([(sT, t_bc)] if kt == 0 else ()))
                    out_v.append(p.add("vector", lambda e, kt=kt: e.tensor_tensor(
                        dst_aps[kt], tmpf[0][:], pb[1][:], ALU.mult),
                        inc=[(sV, 1)])[0])
            grd["bank0"] = (sV, sV.v)
            grd["bank1"] = (sV, sV.v)
            return out_v

        def emit_attn(g, KTsc_d, Vsc_d, QTbuf, nrows, bias_tile, d_kst, d_vst,
                      qk_extra_waits, tag):
            LAG = 3
            for hs in range(4):
                h0 = hs * 4
                av_started = [False] * 4
                pending = []

                def emit_av(rec):
                    (hl, kt, ppe, s_exp, rv, lwv, first, last, closes,
                     strip_first) = rec
                    p.add("tensor", lambda e, hl=hl, kt=kt, ppe=ppe,
                          first=first, last=last, rv=rv: e.matmul(
                              pb[hl][0:65, :],
                              AP(wr[rv], (hl * 4 + kt) * 65,
                                 [[2048, 128], [1, 65]]),
                              expS[ppe][:],
                              start=first, stop=last),
                          wait=[(sS, s_exp)] + ([lwv] if strip_first else [])
                          + ([grd.get(f"av{hl}"), grd.get(f"bank{hl}")]
                             if first else []),
                          inc=[(sT, 1)])
                    grd[f"expS{ppe}"] = (sT, sT.v)
                    if closes is not None:
                        grd[f"wr{closes}"] = (sT, sT.v)

                tileno = 0
                for ks in range(g + 1):
                    rk, lwk = wload(AP(KTsc_d, h0 * T + ks * G,
                                       [[H * T, nrows], [T, 4], [1, G]]),
                                    extra=[d_kst], rows=nrows)
                    rv, lwv = wload(AP(Vsc_d, h0 * 32 * 65 + 4 * ks * 65,
                                       [[H * 32 * 65, 128], [32 * 65, 4], [1, 260]]),
                                    extra=[d_vst], cols=1040)
                    crossing = (ks == g)
                    strip_new = True
                    for hl in range(4):
                        for kt in range(4):
                            bqk = 4 + tileno % 4
                            wq_ = [grd.get(f"bank{bqk}")]
                            if strip_new:
                                wq_ = wq_ + [lwk]
                            if hs == 0 and ks == 0 and tileno == 0:
                                wq_ = wq_ + qk_extra_waits
                            if crossing:
                                p.add("tensor", lambda e, kt=kt, bqk=bqk: e.matmul(
                                    pb[bqk][:], cst["SUge"][:],
                                    cst["Sh4"][:, kt * G:(kt + 1) * G],
                                    start=True, stop=False), wait=wq_)
                            p.add("tensor", lambda e, hl=hl, kt=kt, bqk=bqk, h0=h0,
                                  nrows=nrows, crossing=crossing, rk=rk: e.matmul(
                                      pb[bqk][:],
                                      wr[rk][0:nrows,
                                             (hl * 4 + kt) * 128:(hl * 4 + kt + 1) * 128],
                                      QTbuf[0:nrows, (h0 + hl) * G:(h0 + hl + 1) * G],
                                      start=(not crossing), stop=True),
                                  wait=(wq_ if not crossing else ()),
                                  inc=[(sT, 1)])
                            t_qk = sT.v
                            ppe = tileno % 4
                            s_exp = p.add("scalar", lambda e, ppe=ppe, bqk=bqk,
                                          bias_tile=bias_tile:
                                          e.activation(expS[ppe][:], pb[bqk][:],
                                                       AF.Exp, bias=bias_tile[:],
                                                       scale=1.0),
                                          wait=[(sT, t_qk),
                                                grd.get(f"expS{ppe}")]
                                          + ([(sV, v_init)] if tileno == 0 else []),
                                          inc=[(sS, 1)])[0]
                            grd[f"bank{bqk}"] = (sS, s_exp)
                            first = not av_started[hl]
                            av_started[hl] = True
                            last = (ks == g and kt == 3)
                            closes = rv if (hl == 3 and kt == 3) else None
                            pending.append((hl, kt, ppe, s_exp, rv, lwv,
                                            first, last, closes, strip_new))
                            strip_new = False
                            tileno += 1
                            if len(pending) > LAG:
                                emit_av(pending.pop(0))
                    grd[f"wr{rk}"] = (sT, sT.v)
                while pending:
                    emit_av(pending.pop(0))
                t_av_done = sT.v
                for hl in range(4):
                    h = h0 + hl
                    s3 = p.add("scalar", lambda e, hl=hl: e.activation(
                        rows["recip"][:], pb[hl][64:65, :], AF.Ln,
                        bias=0.0, scale=1.0),
                        wait=[(sT, t_av_done), grd.get("reciprow")],
                        inc=[(sS, 1)])[0]
                    v4 = p.add("scalar", lambda e: e.activation(
                        rows_bf["recip"][:], rows["recip"][:], AF.Exp,
                        bias=0.0, scale=-1.0), inc=[(sS, 1)])[0]
                    bb = 6 + (hl % 2)
                    t_bc = p.add("tensor", lambda e, bb=bb: e.matmul(
                        pb[bb][0:64, :], onesr[0:1, 0:64], rows_bf["recip"][0:1, :],
                        start=True, stop=True),
                        wait=[(sS, v4), grd.get(f"bank{bb}")],
                        inc=[(sT, 1)])[0]
                    grd["reciprow"] = (sT, t_bc)
                    v4b = p.add("vector", lambda e, bb=bb: e.tensor_copy(
                        recb[0:64, :], pb[bb][0:64, :]),
                        wait=[(sT, t_bc)], inc=[(sV, 1)])[0]
                    grd[f"bank{bb}"] = (sV, v4b)
                    v5 = p.add("vector", lambda e, h=h, hl=hl: e.tensor_tensor(
                        OTbf[h // 2][(h % 2) * 64:(h % 2) * 64 + 64, :],
                        pb[hl][0:64, :], recb[0:64, :], ALU.mult),
                        inc=[(sV, 1)])[0]
                    grd[f"av{hl}"] = (sV, v5)
                    grd[f"bank{hl}"] = (sV, v5)
            grd["qta_aug_free"] = (sT, sT.v)
            return sV.v

        wstate = {"k": 0}

        def wload(src_ap, extra=(), rows=128, cols=2048):
            rix = wstate["k"] % 4
            wstate["k"] += 1
            q = "sync" if rix % 2 == 0 else "gpsimd"
            tok = c.dma(q, wr[rix][0:rows, 0:cols], src_ap,
                        wait=[v_wcast, grd.get(f"wr{rix}")] + [t for t in extra if t])
            return rix, tok

        def wpair_ap(Wd, fp):
            ld = Wd.shape[1]
            return AP(Wd, fp * 256, [[ld, 128], [128 * ld, 8], [1, 256]])

        for g in range(n_groups):
            gl = g * G
            xw = []
            for kt in range(8):
                w = [grd.get("xg"), grd.get("xg_dma")] if kt == 0 else ()
                xw.append(c.dma("sync", xg[kt][:],
                                XT[kt * 128:(kt + 1) * 128, gl:gl + G], wait=w))
            vln = ln_grp([xg[i][:] for i in range(8)],
                         [xnbf[i][:] for i in range(8)],
                         rms=False, eps_v=eps1[:],
                         wait_first=[grd.get("xnbf_cons")],
                         wait_per_kt=[xw[i] for i in range(8)])
            v_ln = vln[-1]

            for ri, (role, Wn, bcn, dst_, scl) in enumerate(
                    [("k", "Wk", "bk", KTs, 1.0), ("q", "Wq", "bq", QTa, 0.125)]):
                for fp in range(4):
                    rix, lw = wload(wpair_ap(wsc[Wn], fp))
                    for sub in range(2):
                        f = 2 * fp + sub
                        b = 2 + (ri * 8 + f) % 2
                        for i in range(8):
                            p.add("tensor", lambda e, i=i, rix=rix, sub=sub, b=b:
                                  e.matmul(
                                      pb[b][:], wr[rix][:, i * 256 + sub * 128:
                                                        i * 256 + sub * 128 + 128],
                                      xnbf[i][:],
                                      start=(i == 0), stop=(i == 7)),
                                wait=([lw, (sV, v_ln), grd.get(f"bank{b}")]
                                      if i == 0 else ()),
                                inc=([(sT, 1)] if i == 7 else ()))
                        t_mm = sT.v
                        for half in range(2):
                            hh = 2 * f + half
                            p.add("scalar", lambda e, f=f, half=half, hh=hh, dst_=dst_,
                                  bcn=bcn, scl=scl, b=b: e.activation(
                                      dst_[0:64, hh * G:(hh + 1) * G],
                                      pb[b][half * 64:half * 64 + 64, :], AF.Identity,
                                      bias=bcol[bcn][half * 64:half * 64 + 64, f:f + 1],
                                      scale=scl),
                                  wait=[(sT, t_mm), grd.get("kts_free")],
                                  inc=[(sS, 1)])
                        grd[f"bank{b}"] = (sS, sS.v)
                    grd[f"wr{rix}"] = (sT, t_mm)
                if role == "k":
                    s_k_evac = sS.v
                else:
                    s_q_evac = sS.v

            for half in range(2):
                vch = []
                for q_ in range(2):
                    vch.append(wload(AP(wsc["Wv"],
                                        half * 512 + q_ * 4 * 128 * D,
                                        [[D, 128], [128 * D, 4], [1, 512]])))
                for tt in range(4):
                    b = 2 + (half * 4 + tt) % 2
                    for i in range(8):
                        rix, lw = vch[i // 4]
                        p.add("tensor", lambda e, i=i, tt=tt, rix=rix, b=b: e.matmul(
                            pb[b][:], xnbf[i][:, tt * 128:(tt + 1) * 128],
                            wr[rix][:, (i % 4) * 512:(i % 4) * 512 + 512],
                            start=(i == 0), stop=(i == 7)),
                            wait=([vch[0][1], vch[1][1], (sV, v_ln),
                                   grd.get(f"bank{b}")]
                                  if i == 0 else ()),
                            inc=([(sT, 1)] if i == 7 else ()))
                    t_v = sT.v
                    v_ev = p.add("vector", lambda e, tt=tt, half=half, b=b:
                                 e.tensor_tensor(
                        AP(Vg, (half * 8 * 4 + tt) * 65,
                           [[4 * H * 65, 128], [4 * 65, 8], [1, 64]]),
                        pb[b][:], bvb[:, half * 512:(half + 1) * 512], ALU.add),
                        wait=[(sT, t_v), (sV, v_init),
                              grd.get("vg_free")], inc=[(sV, 1)])[0]
                    grd[f"bank{b}"] = (sV, v_ev)
                for rix, lw in vch:
                    grd[f"wr{rix}"] = (sT, sT.v)
            v_v_evac = sV.v

            ktoks, qtoks = [], []
            for k in range(3):
                ktoks.append(c.dma("sync",
                    AP(KTs, (64 + k) * (H * G), [[H * G, 1], [1, H * G]]),
                    AP(CONSTS["kaug"].tensor, (3 * g + k) * (H * G),
                       [[H * G, 1], [1, H * G]]),
                    wait=[grd.get("kts_free")]))
                qtoks.append(c.dma("sync",
                    AP(QTa, (67 + k) * (H * G), [[H * G, 1], [1, H * G]]),
                    AP(CONSTS["qaug"].tensor, (3 * g + k) * (H * G),
                       [[H * G, 1], [1, H * G]]),
                    wait=[grd.get("qta_aug_free")]))
            d_kaug = c.join("gpsimd", ktoks)
            d_qaug = c.join("gpsimd", qtoks)

            d_kst = c.dma("gpsimd",
                          AP(KTab, gl, [[H * T, 70], [T, H], [1, G]]),
                          AP(KTs, 0, [[H * G, 70], [G, H], [1, G]]),
                          wait=[(sS, s_k_evac), d_kaug, v_cl])
            d_vst = c.dma("gpsimd",
                          AP(Vab, 4 * g * 65,
                             [[H * 32 * 65, 128], [32 * 65, H], [1, 260]]),
                          AP(Vg, 0, [[4 * H * 65, 128], [260, H], [1, 260]]),
                          wait=[(sV, v_v_evac)])
            grd["vg_free"] = d_vst
            grd["kts_free"] = d_kst


            v_attn_done = emit_attn(
                g, KTab, Vab, QTa, 70, c0t, d_kst, d_vst,
                [(sS, s_q_evac), d_qaug, v_cl], "ab")

            if g == n_groups - 1:
                c.outs.append(c.dma("gpsimd", DBG_QT, QTa[:], wait=[d_qaug, (sS, s_q_evac), v_cl]))
                c.outs.append(c.dma("gpsimd", DBG_KT, KTs[:], wait=[d_kaug, (sS, s_k_evac), v_cl]))
                c.outs.append(c.dma("gpsimd", DBG_VG, Vg[:], wait=[(sV, v_v_evac)]))
                for i in range(8):
                    c.outs.append(c.dma("gpsimd", DBG_OT[i * 128:(i + 1) * 128, :],
                          OTbf[i][:], wait=[(sV, v_attn_done)]))
            x1v = []
            for fp in range(4):
                rix, lw = wload(wpair_ap(wsc["Wo"], fp))
                for sub in range(2):
                    f = 2 * fp + sub
                    b = 2 + f % 2
                    for i in range(8):
                        p.add("tensor", lambda e, i=i, rix=rix, sub=sub, b=b: e.matmul(
                            pb[b][:], wr[rix][:, i * 256 + sub * 128:
                                              i * 256 + sub * 128 + 128],
                            OTbf[i][:],
                            start=(i == 0), stop=(i == 7)),
                            wait=([lw, (sV, v_attn_done), grd.get(f"bank{b}")]
                                  if i == 0 else ()),
                            inc=([(sT, 1)] if i == 7 else ()))
                    t_mm = sT.v
                    s_ev = p.add("scalar", lambda e, f=f, b=b: e.activation(
                        tmpf[1][:], pb[b][:], AF.Identity,
                        bias=bcol["bo"][:, f:f + 1], scale=1.0),
                        wait=[(sT, t_mm), grd.get("tmpf1")], inc=[(sS, 1)])[0]
                    grd[f"bank{b}"] = (sS, s_ev)
                    v_x1 = p.add("vector", lambda e, f=f: e.tensor_tensor(
                        xg[f][:], tmpf[1][:], xg[f][:], ALU.add),
                        wait=[(sS, s_ev), grd.get("x1gv"),
                              grd.get("x1gd")], inc=[(sV, 1)])[0]
                    grd["pb2w"] = (sV, v_x1)
                    grd["tmpf1"] = (sV, v_x1)
                    x1v.append(v_x1)
                    d_x1o = c.dma("gpsimd",
                                  X1T[f * 128:(f + 1) * 128, gl:gl + G], xg[f][:],
                                  wait=[(sV, v_x1)])
                    c.outs.append(d_x1o)
                    grd[f"x1od{f}"] = d_x1o
                grd[f"wr{rix}"] = (sT, t_mm)

            vln2 = ln_grp([xg[i][:] for i in range(8)],
                          [hidg[i][:] for i in range(8)],
                          rms=False, eps_v=eps1[:])
            htoks = []
            for kt in range(8):
                htoks.append(c.dma("gpsimd",
                      HIDT_O[kt * 128:(kt + 1) * 128, gl:gl + G],
                      hidg[kt][:], wait=[(sV, vln2[kt])]))
            c.outs.extend(htoks)
            grd["xg"] = (sV, vln2[-1])
            grd["x1gv"] = (sV, vln2[-1])
            grd["x1gd"] = c.join("gpsimd", htoks)

            pq = 32 * (g // 2)
            co = (g % 2) * G
            rv = []
            for q4 in range(4):
                p.add("vector", lambda e, q4=q4, pq=pq, co=co: e.tensor_copy(
                    cosrep[q4 * 32:(q4 + 1) * 32, :],
                    cst["cos_pk"][pq:pq + 32, co:co + G][0:32, :]),
                    wait=([grd.get("trig")] if q4 == 0 else ()))
                rv.append(p.add("vector", lambda e, q4=q4, pq=pq, co=co: e.tensor_copy(
                    sinrep[q4 * 32:(q4 + 1) * 32, :],
                    cst["sin_pk"][pq:pq + 32, co:co + G][0:32, :]),
                    inc=[(sV, 1)])[0])
            v_trig = rv[-1]

            vr1 = ln_grp([hidg[i][:] for i in range(8)],
                         [h1bf[i][:] for i in range(8)],
                         rms=True, gains="g11", eps_v=eps2[:],
                         wait_first=[grd.get("h1bf_free")])
            v_r1 = vr1[-1]

            for ri, (role, Wn, dst_, scl) in enumerate(
                    [("k", "Wk1", KTs, 1.0), ("q", "Wq1", QTa, 0.125)]):
                for j4 in range(4):
                    pA, pB = ((2, 3) if (ri * 4 + j4) % 2 == 0 else (6, 7))
                    rix, lw = wload(wpair_ap(wsc[Wn], j4))
                    for i in range(8):
                        p.add("tensor", lambda e, i=i, rix=rix, pA=pA: e.matmul(
                            pb[pA][:], wr[rix][:, i * 256:i * 256 + 128], h1bf[i][:],
                            start=(i == 0), stop=(i == 7)),
                            wait=([lw, (sV, v_r1), grd.get(f"bank{pA}")]
                                  if i == 0 else ()),
                            inc=([(sT, 1)] if i == 7 else ()))
                    tA = sT.v
                    for i in range(8):
                        p.add("tensor", lambda e, i=i, rix=rix, pB=pB: e.matmul(
                            pb[pB][:], wr[rix][:, i * 256 + 128:i * 256 + 256],
                            h1bf[i][:],
                            start=(i == 0), stop=(i == 7)),
                            wait=([grd.get(f"bank{pB}")] if i == 0 else ()),
                            inc=([(sT, 1)] if i == 7 else ()))
                    tB = sT.v
                    grd[f"wr{rix}"] = (sT, tB)
                    p.add("vector", lambda e, pA=pA: e.tensor_tensor(
                        tmpf[0][:], pb[pA][:], cosrep[:], ALU.mult),
                        wait=[(sT, tB), (sV, v_trig), grd.get("ropebuf")])
                    p.add("vector", lambda e, pB=pB: e.tensor_tensor(
                        tmpf[1][:], pb[pB][:], sinrep[:], ALU.mult))
                    p.add("vector", lambda e: e.tensor_tensor(
                        tmpf[0][:], tmpf[0][:], tmpf[1][:], ALU.subtract))
                    p.add("vector", lambda e, pA=pA: e.tensor_tensor(
                        rtmp[:], pb[pA][:], sinrep[:], ALU.mult))
                    p.add("vector", lambda e, pB=pB: e.tensor_tensor(
                        tmpf[1][:], pb[pB][:], cosrep[:], ALU.mult))
                    vrp = p.add("vector", lambda e: e.tensor_tensor(
                        rtmp[:], rtmp[:], tmpf[1][:], ALU.add), inc=[(sV, 1)])[0]
                    grd[f"bank{pA}"] = (sV, vrp)
                    grd[f"bank{pB}"] = (sV, vrp)
                    for m in range(4):
                        hh = 4 * j4 + m
                        p.add("scalar", lambda e, m=m, hh=hh, dst_=dst_, scl=scl:
                              e.activation(dst_[0:32, hh * G:(hh + 1) * G],
                                           tmpf[0][32 * m:32 * m + 32, :],
                                           AF.Identity, bias=0.0, scale=scl),
                              wait=([(sV, vrp), grd.get("kts_free" if dst_ is KTs
                                                        else "qta_aug_free")]
                                    if m == 0 else ()),
                              inc=[(sS, 1)])
                        p.add("scalar", lambda e, m=m, hh=hh, dst_=dst_, scl=scl:
                              e.activation(dst_[32:64, hh * G:(hh + 1) * G],
                                           rtmp[32 * m:32 * m + 32, :],
                                           AF.Identity, bias=0.0, scale=scl),
                              inc=[(sS, 1)])
                    grd["ropebuf"] = (sS, sS.v)
                if role == "k":
                    s_k1 = sS.v
                else:
                    s_q1 = sS.v

            for half in range(2):
                vch = []
                for q_ in range(2):
                    vch.append(wload(AP(wsc["Wv1"],
                                        half * 512 + q_ * 4 * 128 * D,
                                        [[D, 128], [128 * D, 4], [1, 512]])))
                for tt_ in range(4):
                    b = 2 + (half * 4 + tt_) % 2
                    for i in range(8):
                        rix, lw = vch[i // 4]
                        p.add("tensor", lambda e, i=i, tt_=tt_, rix=rix, b=b: e.matmul(
                            pb[b][:], h1bf[i][:, tt_ * 128:(tt_ + 1) * 128],
                            wr[rix][:, (i % 4) * 512:(i % 4) * 512 + 512],
                            start=(i == 0), stop=(i == 7)),
                            wait=([vch[0][1], vch[1][1], (sV, v_r1),
                                   grd.get(f"bank{b}")]
                                  if i == 0 else ()),
                            inc=([(sT, 1)] if i == 7 else ()))
                    t_v = sT.v
                    v_ev = p.add("vector", lambda e, tt_=tt_, half=half, b=b:
                                 e.tensor_copy(
                        AP(Vg, (half * 8 * 4 + tt_) * 65,
                           [[4 * H * 65, 128], [4 * 65, 8], [1, 64]]),
                        pb[b][:]),
                        wait=[(sT, t_v), grd.get("vg_free")], inc=[(sV, 1)])[0]
                    grd[f"bank{b}"] = (sV, v_ev)
                for rix, lw in vch:
                    grd[f"wr{rix}"] = (sT, sT.v)
            v_v1 = sV.v

            d_kst1 = c.dma("gpsimd",
                           AP(KT1d, gl, [[H * T, 64], [T, H], [1, G]]),
                           AP(KTs, 0, [[H * G, 64], [G, H], [1, G]]),
                           wait=[(sS, s_k1)])
            d_vst1 = c.dma("gpsimd",
                           AP(V1d, 4 * g * 65,
                              [[H * 32 * 65, 128], [32 * 65, H], [1, 260]]),
                           AP(Vg, 0, [[4 * H * 65, 128], [260, H], [1, 260]]),
                           wait=[(sV, v_v1)])
            grd["vg_free"] = d_vst1
            grd["kts_free"] = d_kst1

            v_attn1 = emit_attn(g, KT1d, V1d, QTa, 64, c0t, d_kst1, d_vst1,
                                [(sS, s_q1), v_cl], "d1")

            for fp in range(4):
                rix, lw = wload(wpair_ap(wsc["Wo1"], fp))
                for sub in range(2):
                    f = 2 * fp + sub
                    b = 2 + f % 2
                    for i in range(8):
                        p.add("tensor", lambda e, i=i, rix=rix, sub=sub, b=b: e.matmul(
                            pb[b][:], wr[rix][:, i * 256 + sub * 128:
                                              i * 256 + sub * 128 + 128],
                            OTbf[i][:],
                            start=(i == 0), stop=(i == 7)),
                            wait=([lw, (sV, v_attn1), grd.get(f"bank{b}")]
                                  if i == 0 else ()),
                            inc=([(sT, 1)] if i == 7 else ()))
                    t_mm = sT.v
                    s_ev = p.add("scalar", lambda e, f=f, b=b: e.activation(
                        tmpf[1][:], pb[b][:], AF.Identity, bias=0.0, scale=1.0),
                        wait=[(sT, t_mm), grd.get("tmpf1")], inc=[(sS, 1)])[0]
                    grd[f"bank{b}"] = (sS, s_ev)
                    v_x2 = p.add("vector", lambda e, f=f: e.tensor_tensor(
                        xg[f][:], tmpf[1][:], hidg[f][:], ALU.add),
                        wait=[(sS, s_ev), grd.get("xg"), grd.get("x1gd"),
                              grd.get(f"x1od{f}")],
                        inc=[(sV, 1)])[0]
                    grd["pb2"] = (sV, v_x2)
                    grd["tmpf1"] = (sV, v_x2)
                grd[f"wr{rix}"] = (sT, t_mm)
            v_x2_done = sV.v
            grd["xg"] = (sV, v_x2_done)

            vr2 = ln_grp([xg[i][:] for i in range(8)],
                         [h1bf[i][:] for i in range(8)],
                         rms=True, gains="g21", eps_v=eps2[:],
                         wait_first=[grd.get("h1bf_free")])
            v_r2 = vr2[-1]

            for fop in range(16):
                rg, lwg = wload(wpair_ap(wsc["Wg1"], fop))
                ru, lwu = wload(wpair_ap(wsc["Wu1"], fop))
                for sub in range(2):
                    fo = 2 * fop + sub
                    bG = 2 if fo % 2 == 0 else 6
                    bU = 3 if fo % 2 == 0 else 7
                    for i in range(8):
                        p.add("tensor", lambda e, i=i, rg=rg, sub=sub, bG=bG: e.matmul(
                            pb[bG][:], wr[rg][:, i * 256 + sub * 128:
                                             i * 256 + sub * 128 + 128],
                            h1bf[i][:],
                            start=(i == 0), stop=(i == 7)),
                            wait=([lwg, (sV, v_r2), grd.get(f"bank{bG}")]
                                  if i == 0 else ()),
                            inc=([(sT, 1)] if i == 7 else ()))
                    tG = sT.v
                    for i in range(8):
                        p.add("tensor", lambda e, i=i, ru=ru, sub=sub, bU=bU: e.matmul(
                            pb[bU][:], wr[ru][:, i * 256 + sub * 128:
                                             i * 256 + sub * 128 + 128],
                            h1bf[i][:],
                            start=(i == 0), stop=(i == 7)),
                            wait=([lwu, grd.get(f"bank{bU}")] if i == 0 else ()),
                            inc=([(sT, 1)] if i == 7 else ()))
                    tU = sT.v
                    s_si = p.add("scalar", lambda e, bG=bG: e.activation(
                        sgbf[:], pb[bG][:], AF.Silu, bias=0.0, scale=1.0),
                        wait=[(sT, tG), grd.get("sgbf")], inc=[(sS, 1)])[0]
                    v_sw = p.add("vector", lambda e, fo=fo, bU=bU: e.tensor_tensor(
                        swig[:, fo * G:(fo + 1) * G], sgbf[:], pb[bU][:], ALU.mult),
                        wait=[(sS, s_si), (sT, tU), grd.get("swig_free")],
                        inc=[(sV, 1)])[0]
                    grd["sgbf"] = (sV, v_sw)
                    grd[f"bank{bG}"] = (sS, s_si)
                    grd[f"bank{bU}"] = (sV, v_sw)
                grd[f"wr{rg}"] = (sT, tG)
                grd[f"wr{ru}"] = (sT, tU)
            v_swig = sV.v

            for f in range(8):
                wch = []
                for q_ in range(2):
                    wch.append(wload(AP(wsc["Wd1"],
                                        f * 128 + q_ * 16 * 128 * D,
                                        [[D, 128], [128 * D, 16], [1, 128]])))
                b = 2 + f % 2
                for i in range(32):
                    rix, lw = wch[i // 16]
                    p.add("tensor", lambda e, i=i, rix=rix, b=b: e.matmul(
                        pb[b][:], wr[rix][:, (i % 16) * 128:(i % 16) * 128 + 128],
                        swig[:, i * G:(i + 1) * G],
                        start=(i == 0), stop=(i == 31)),
                        wait=([wch[0][1], wch[1][1], (sV, v_swig),
                               grd.get(f"bank{b}")]
                              if i == 0 else ()),
                        inc=([(sT, 1)] if i == 31 else ()))
                t_mm = sT.v
                for rix, lw in wch:
                    grd[f"wr{rix}"] = (sT, t_mm)
                s_ev = p.add("scalar", lambda e, b=b: e.activation(
                    tmpf[1][:], pb[b][:], AF.Identity, bias=0.0, scale=1.0),
                    wait=[(sT, t_mm), grd.get("tmpf1")], inc=[(sS, 1)])[0]
                grd[f"bank{b}"] = (sS, s_ev)
                v_pr = p.add("vector", lambda e, f=f: e.tensor_tensor(
                    xg[f][:], tmpf[1][:], xg[f][:], ALU.add),
                    wait=[(sS, s_ev)], inc=[(sV, 1)])[0]
                grd["pb2"] = (sV, v_pr)
                grd["tmpf1"] = (sV, v_pr)
                c.outs.append(c.dma("gpsimd",
                    PROCT[f * 128:(f + 1) * 128, gl:gl + G], xg[f][:],
                    wait=[(sV, v_pr)]))
            grd["swig_free"] = (sT, sT.v)
            v_proc = sV.v
            d_proc = c.join("gpsimd", c.outs[-8:])
            grd["xg_dma"] = d_proc

            vph = ln_grp([hidg[i][:] for i in range(8)],
                         [h1bf[i][:] for i in range(8)],
                         rms=True, gains="gpn", eps_v=eps2[:],
                         wait_first=[grd.get("h1bf_free")])
            v_ph = vph[-1]
            for fp in range(4):
                rix, lw = wload(wpair_ap(wsc["PW"], fp))
                for sub in range(2):
                    f = 2 * fp + sub
                    b = 2 + f % 2
                    for i in range(8):
                        p.add("tensor", lambda e, i=i, rix=rix, sub=sub, b=b: e.matmul(
                            pb[b][:], wr[rix][:, i * 256 + sub * 128:
                                              i * 256 + sub * 128 + 128],
                            h1bf[i][:],
                            start=(i == 0), stop=(i == 7)),
                            wait=([lw, (sV, v_ph), grd.get(f"bank{b}")]
                                  if i == 0 else ()),
                            inc=([(sT, 1)] if i == 7 else ()))
                    t_mm = sT.v
                    s_mu = p.add("scalar", lambda e, f=f, b=b: e.activation(
                        mubf[f][:], pb[b][:], AF.Identity, bias=0.0, scale=1.0),
                        wait=[(sT, t_mm), grd.get("mubf_free")], inc=[(sS, 1)])[0]
                    grd[f"bank{b}"] = (sS, s_mu)
                grd[f"wr{rix}"] = (sT, t_mm)
            s_mu_done = sS.v

            for kt in range(8):
                v_res = p.add("vector", lambda e, kt=kt: e.tensor_tensor(
                    tmpf[0][:], xg[kt][:], hidg[kt][:], ALU.subtract),
                    wait=[(sV, v_proc)] if kt == 0 else ())
                p.add("vector", lambda e, kt=kt: e.tensor_copy(
                    resbf[kt][:], tmpf[0][:]),
                    wait=[grd.get("resbf_free")] if kt == 0 else ())
                vsq = p.add("vector", lambda e, kt=kt: e.tensor_tensor(
                    sqb[kt % 2][:], tmpf[0][:], tmpf[0][:], ALU.mult),
                    wait=[grd.get(f"dsq{kt % 2}")], inc=[(sV, 1)])[0]
                tq = p.add("tensor", lambda e, kt=kt: e.matmul(
                    pb[5][0:1, :], cst["ones_col_bf"][:], sqb[kt % 2][:],
                    start=(kt == 0), stop=(kt == 7)),
                    wait=[(sV, vsq)] + ([grd.get("pb5")] if kt == 0 else []),
                    inc=[(sT, 1)])[0]
                grd[f"dsq{kt % 2}"] = (sT, tq)
            t_dst = sT.v
            grd["xg"] = (sV, sV.v)
            v_dstrow = p.add("vector", lambda e: e.tensor_scalar(
                rows["m"][:], pb[5][0:1, :], 1.0 / D, None, ALU.mult),
                wait=[(sT, t_dst), grd.get("dstrow_free")], inc=[(sV, 1)])[0]
            grd["pb5"] = (sV, v_dstrow)
            d_dstdma = c.dma("gpsimd", DST[0:1, gl:gl + G], rows["m"][:],
                             wait=[(sV, v_dstrow)])
            c.outs.append(d_dstdma)
            v_ds1 = p.add("vector", lambda e: e.tensor_reduce(
                rows["sc1"][0:1, 0:1], rows["m"][:], AX.X, ALU.add),
                wait=[grd.get("sumacc")], inc=[(sV, 1)])[0]
            v_ds2 = p.add("vector", lambda e: e.tensor_tensor(
                dsa[:], dsa[:], rows["sc1"][0:1, 0:1], ALU.add),
                wait=[(sV, v_ds1)], inc=[(sV, 1)])[0]
            grd["sumacc"] = (sV, v_ds2)
            grd["dstrow_free"] = (sV, v_ds2)

            for f in range(8):
                if f % 2 == 0:
                    rix, lw = wload(wpair_ap(wsc["PW"], 4 + f // 2))
                b = 2 + f % 2
                for i in range(8):
                    p.add("tensor", lambda e, i=i, rix=rix, f=f, b=b: e.matmul(
                        pb[b][:], wr[rix][:, i * 256 + (f % 2) * 128:
                                          i * 256 + (f % 2) * 128 + 128],
                        h1bf[i][:],
                        start=(i == 0), stop=(i == 7)),
                        wait=([lw, (sV, v_ph), grd.get(f"bank{b}")] if i == 0 else ()),
                        inc=([(sT, 1)] if i == 7 else ()))
                t_lv = sT.v
                if f % 2 == 1:
                    grd[f"wr{rix}"] = (sT, t_lv)
                s_e = p.add("scalar", lambda e, b=b: e.activation(
                    ebuf[:], pb[b][:], AF.Exp, bias=0.0, scale=-1.0),
                    wait=[(sT, t_lv), grd.get("ebuf_free")], inc=[(sS, 1)])[0]
                p.add("vector", lambda e, f=f: e.tensor_tensor(
                    tmpf[0][:], resbf[f][:], mubf[f][:], ALU.subtract),
                    wait=[(sS, s_mu_done), (sS, s_e)])
                p.add("vector", lambda e: e.tensor_tensor(
                    tmpf[0][:], tmpf[0][:], tmpf[0][:], ALU.mult))
                p.add("vector", lambda e: e.tensor_scalar(
                    tmpf[0][:], tmpf[0][:], 1.0, None, ALU.add))
                p.add("vector", lambda e: e.tensor_tensor(
                    tmpf[0][:], tmpf[0][:], ebuf[:], ALU.mult))
                p.add("vector", lambda e, b=b: e.tensor_tensor(
                    tmpf[0][:], tmpf[0][:], pb[b][:], ALU.add))
                vq = p.add("vector", lambda e, f=f: e.tensor_copy(
                    sqb[f % 2][:], tmpf[0][:]),
                    wait=[grd.get(f"dsq{f % 2}")], inc=[(sV, 1)])[0]
                grd["ebuf_free"] = (sV, vq)
                grd[f"bank{b}"] = (sV, vq)
                tq = p.add("tensor", lambda e, f=f: e.matmul(
                    pb[5][0:1, :], cst["ones_col_bf"][:], sqb[f % 2][:],
                    start=(f == 0), stop=(f == 7)),
                    wait=[(sV, vq)] + ([grd.get("pb5")] if f == 0 else []),
                    inc=[(sT, 1)])[0]
                grd[f"dsq{f % 2}"] = (sT, tq)
            t_dch = sT.v
            v_dchrow = p.add("vector", lambda e: e.tensor_scalar(
                rows["m"][:], pb[5][0:1, :], 0.5 / D, -0.5, ALU.mult, ALU.add),
                wait=[(sT, t_dch), grd.get("dstrow_free"), d_dstdma],
                inc=[(sV, 1)])[0]
            grd["pb5"] = (sV, v_dchrow)
            d_dchdma = c.dma("gpsimd", DCH[0:1, gl:gl + G], rows["m"][:],
                             wait=[(sV, v_dchrow)])
            c.outs.append(d_dchdma)
            grd["rowm_free"] = d_dchdma
            v_dc1 = p.add("vector", lambda e: e.tensor_reduce(
                rows["sc1"][0:1, 0:1], rows["m"][:], AX.X, ALU.add),
                wait=[grd.get("sumacc")], inc=[(sV, 1)])[0]
            v_dc2 = p.add("vector", lambda e: e.tensor_tensor(
                dsb[:], dsb[:], rows["sc1"][0:1, 0:1], ALU.add),
                wait=[(sV, v_dc1)], inc=[(sV, 1)])[0]
            grd["sumacc"] = (sV, v_dc2)
            grd["dstrow_free"] = (sV, v_dc2)
            grd["h1bf_free"] = (sT, sT.v)
            grd["mubf_free"] = (sV, sV.v)
            grd["resbf_free"] = (sV, sV.v)
            grd["trig"] = (sT, sT.v)

        tv = sT.v
        p.add("tensor", lambda e: e.matmul(pb[6][:], cst["SUge"][:],
                                           cst["Sh4"][:, 0:G], start=True, stop=False),
              wait=[(sV, sV.v), (sS, sS.v), (sT, tv)])
        t_f = p.add("tensor", lambda e: e.matmul(pb[6][:], KTs[:, 0:128],
                                                 QTa[:, 0:G], start=False, stop=True),
                    inc=[(sT, 1)])[0]
        v_f = p.add("vector", lambda e: e.tensor_copy(tmpf[0][:], pb[6][:]),
                    wait=[(sT, t_f)], inc=[(sV, 1)])[0]
        c.outs.append(c.dma("gpsimd", DBG_PS2, tmpf[0][:], wait=[(sV, v_f)]))
        c.outs.append(c.dma("gpsimd", DSA, dsa[:], wait=[(sV, sV.v)]))
        c.outs.append(c.dma("gpsimd", DSB, dsb[:], wait=[(sV, sV.v)]))
        c.join("gpsimd", c.outs)
        p.barrier("end")
        import os
        sim_edges = bool(int(os.environ.get("MK_SIM_EDGES", "0")))
        if sim_edges:
            _ENG = ENGINES
            c.p.emit_pre = None
        for s_ in p.sems:
            s_.h = c.es.enter_context(nc.semaphore(s_.name))
        if sim_edges:
            c.p.ord = {e: type(p.sems[0])("ord_" + e) for e in _ENG}
            for s_ in c.p.ord.values():
                s_.h = c.es.enter_context(nc.semaphore(s_.name))
            c.p.emit(c.es.enter_context(nc.Block()), sim_edges=True)
        else:
            c.p.emit(c.es.enter_context(nc.Block()))
    return nc, c


def stage1_inputs(I, b):
    qkv = split_qkv(I["L0_attn_W_qkv"], I["L0_attn_b_qkv"])
    colshape = lambda v: np.ascontiguousarray(v.reshape(8, 128).T)
    m = {
        "xT": np.ascontiguousarray(I["L0_attn_X"][b].T),
        "Wk": qkv["Wk"], "Wq": qkv["Wq"], "Wv": qkv["Wv"],
        "Wo": I["L0_attn_W_out"],
        "bk": colshape(qkv["bk"]), "bq": colshape(qkv["bq"]),
        "bv": colshape(qkv["bv"]), "bo": colshape(I["L0_attn_b_out"]),
        "bvrep": np.tile(qkv["bv"].reshape(1, D), (128, 1)),
        "wq1p": permute_w_rope(I["L0_ffn_wq1"]),
        "wk1p": permute_w_rope(I["L0_ffn_wk1"]),
        "wv1": I["L0_ffn_wv1"], "wo1": I["L0_ffn_wo1"],
        "wg1": I["L0_ffn_wg1"], "wu1": I["L0_ffn_wu1"], "wd1": I["L0_ffn_wd1"],
        "prior_w": I["L0_ffn_prior_w"],
        "g11": colshape(I["L0_ffn_ln1_1"]), "g21": colshape(I["L0_ffn_ln2_1"]),
        "gpn": colshape(I["L0_ffn_prior_norm_w"]),
    }
    for n, v in host_consts().items():
        m["c_" + n] = np.asarray(v)
    return m



"""Stage-2 builder: gating z, top-k selection, gather, decoder-2, scatter, output.

Inputs per core: x1T, procT [D,T] f32; dst, dch [1,T]; allsums [1,16]
(8x dsa then 8x dsb, host-concatenated layout); dec2 weights; consts.
Output: outT [D, T] f32 (host transposes).
"""
import numpy as np
import ml_dtypes
from contextlib import ExitStack
import concourse.bass as bass
import concourse.mybir as mybir

AP = bass.AP
C0 = 20.0
T2 = TSEL
NG2 = T2 // G
KSEL = 1024
CW_HI = 6.28125
CW_MID = float(np.float32(2 * np.pi - 6.28125))
CW_LO = float(np.float32((2 * np.pi - 6.28125) - np.float64(np.float32(2 * np.pi - 6.28125))))
INV2PI = float(np.float32(1.0 / (2 * np.pi)))
PI = float(np.pi)
BIG = 1e30


def host_consts2():
    c = {}
    iu = np.arange(128)
    c["SUge"] = (iu[None, :] >= iu[:, None]).astype(ml_dtypes.bfloat16)
    c["SUstT"] = (iu[None, :] > iu[:, None]).astype(ml_dtypes.bfloat16)
    sh = np.zeros((128, 2048), np.float32)
    for kt in range(4):
        for f in range(512):
            r = f + 1 - 128 * kt
            if r <= 0:
                sh[0, 512 * kt + f] = NEG
            elif r < 128:
                sh[r, 512 * kt + f] = NEG
    c["Sh4"] = sh.astype(ml_dtypes.bfloat16)
    c["ones_col_bf"] = np.ones((128, 1), ml_dtypes.bfloat16)
    c["onesr_f"] = np.ones((1, 128), np.float32)
    c["iota1024_2d"] = np.tile(np.arange(1024, dtype=np.float32)[None, :], (128, 1))
    c["iotaP_bf"] = np.arange(128, dtype=np.float64).astype(ml_dtypes.bfloat16).reshape(128, 1)
    c["hi32_bf"] = np.tile((128.0 * np.arange(32, dtype=np.float64)
                            ).astype(ml_dtypes.bfloat16).reshape(1, 32), (128, 1))
    c["idf"] = np.eye(128, dtype=np.float32)
    c["inv_row"] = (1.0 / 10000.0 ** (np.arange(32, dtype=np.float64) / 32.0)
                    ).astype(np.float32).reshape(1, 32)
    return c


def build_stage2():
    nc = bass.Bass("TRN2", target_bir_lowering=False, detect_race_conditions=False)
    c = Ctx(nc)
    p = c.p
    grd = c.grd
    sV, sS, sT, sG = c.sV, c.sS, c.sT, c.sG

    def din(name, shape, dt=F32):
        return nc.dram_tensor(name, shape, dt, kind="ExternalInput")
    def dout(name, shape, dt=F32):
        return nc.dram_tensor(name, shape, dt, kind="ExternalOutput")

    HC = host_consts2()
    X1T = din("x1T", [D, T])
    PROCT = din("procT", [D, T])
    DSTI = din("dst", [1, T]); DCHI = din("dch", [1, T])
    ALLS = din("allsums", [1, 16])
    W2 = {n: din(n, shp) for n, shp in [
        ("wq2p", [D, D]), ("wk2p", [D, D]), ("wv2", [D, D]), ("wo2", [D, D]),
        ("wg2", [D, FFN]), ("wu2", [D, FFN]), ("wd2", [FFN, D])]}
    GC = {n: din(n, [128, 8]) for n in ["g12", "g22"]}
    CONSTS = {n: din("c_" + n, list(v.shape),
                     BF16 if v.dtype == ml_dtypes.bfloat16 else F32)
              for n, v in HC.items()}
    OUT = dout("outT", [D, T])
    DBG_SEL = dout("dbg_selidx", [1, T2])
    DBG_SC = dout("dbg_scores", [1, T2])
    DBG_Y = dout("dbg_y", [D, T2])
    DBG_SELT = dout("dbg_selT", [D, T2], BF16)

    wsc = {n: nc.dram_tensor("sc2_" + n, list(W2[n].shape), BF16) for n in W2}
    KT2d = nc.dram_tensor("sc2_KT", [64, H * T2], BF16)
    V2d = nc.dram_tensor("sc2_V", [128, H * 8 * 65], BF16)
    PROCN = nc.dram_tensor("sc2_procN", [T, D], BF16)
    BNZ = nc.dram_tensor("sc2_bnz", [1, T], F32)
    BNR = nc.dram_tensor("sc2_bnr", [1, T], F32)
    BNS = nc.dram_tensor("sc2_bns", [1, T2], F32)

    with c.es:
        cst = {n: c.sb("k_" + n, list(v.shape),
                       BF16 if v.dtype == ml_dtypes.bfloat16 else F32)
               for n, v in HC.items()}
        gcol = {n: c.sb("gc_" + n, [128, 8], F32) for n in GC}
        zcol = c.sb("zcol", [128, 32], F32)
        zTbf = c.sb("zTbf", [128, 32], BF16)
        rkT = c.sb("rkT", [128, 32], F32)
        selTc = c.sb("selTc", [128, 8], F32)
        A32 = c.sb("A32", [32, 128], F32)
        m32 = {n: c.sb("m_" + n, [128, 32], F32) for n in
               ["ge", "eq", "gt", "pre", "pre2", "msel", "rank"]}
        col1 = c.sb("col1", [128, 1], F32)
        col2 = c.sb("col2", [128, 1], F32)
        col3 = c.sb("col3", [128, 1], F32)
        colbf = c.sb("colbf", [128, 1], BF16)
        s11 = {n: c.sb("s_" + n, [1, 1], F32) for n in
               ["m1", "m2", "lo", "hi", "mid", "pred", "theta", "kk", "t1", "t2"]}
        PTt = c.sb("PTt", [128, 1024], BF16)
        selT = [c.sb(f"selT{i}", [128, T2], BF16) for i in range(8)]
        x2g = [c.sb(f"x2g{i}", [128, G], F32) for i in range(8)]
        h1bf = [c.sb(f"h1bf{i}", [128, G], BF16) for i in range(8)]
        sgbf = c.sb("sgbf", [128, G], BF16)
        swig = c.sb("swig", [128, 32 * G], BF16)
        QTa = c.sb("QTa", [64, H * G], BF16)
        kpiece = c.sb("kpiece", [64, 4 * G], BF16)
        Vg = c.sb("Vg", [128, 4 * H * 65], BF16)
        KTld = c.sb("KTld", [64, 4 * G], BF16)
        Vld = c.sb("Vld", [128, 4 * 4 * 65], BF16)
        expS = [c.sb(f"expS{i}", [128, G], BF16) for i in range(4)]
        OTbf = [c.sb(f"OTbf{i}", [128, G], BF16) for i in range(8)]
        recb = c.sb("recb", [64, G], F32)
        rows = {n: c.sb("r_" + n, [1, G], F32) for n in ["var", "rstd", "recip", "sc1"]}
        rows_bf = {n: c.sb("rb_" + n, [1, G], BF16) for n in ["rstd", "recip"]}
        onesr = c.sb("onesr", [1, 128], BF16)
        eps2 = c.sb("eps2", [1, 1], F32)
        c0t = c.sb("c0t", [128, 1], F32)
        wtile = c.sb("wtile", [128, 1024], BF16)
        wcols = c.sb("wcols", [128, 4096], BF16)
        tmpf = [c.sb(f"tmpf{i}", [128, G], F32) for i in range(2)]
        sqb = [c.sb(f"sqb{i}", [128, G], BF16) for i in range(2)]
        xnbf = [c.sb(f"xnbf{i}", [128, G], BF16) for i in range(8)]
        ang = c.sb("ang", [32, G], F32)
        angk = c.sb("angk", [32, G], F32)
        angi = c.sb("angi", [32, G], I32)
        cosrep = c.sb("cosrep", [128, G], F32)
        sinrep = c.sb("sinrep", [128, G], F32)
        rtmp = c.sb("rtmp", [128, G], F32)
        srow = c.sb("srow", [1, T2], F32)
        rowB = c.sb("rowB", [1, T2], F32)
        srep = c.sb("srep", [128, G], F32)
        deltaN = [c.sb(f"deltaN{i}", [128, D], BF16) for i in range(8)]
        projt = wtile
        trb = sqb[0]
        pb = [c.psum(f"pb{i}") for i in range(8)]

        def VS(fn, wait=()):
            v = p.add("vector", fn, wait=[w for w in wait if w], inc=[(sV, 1)])[0]
            return (sV, v)

        def TS(fn, wait=()):
            v = p.add("tensor", fn, wait=[w for w in wait if w], inc=[(sT, 1)])[0]
            return (sT, v)

        def SC(fn, wait=()):
            v = p.add("scalar", fn, wait=[w for w in wait if w], inc=[(sS, 1)])[0]
            return (sS, v)

        toks = [c.dma("sync", cst[n][:], CONSTS[n].ap()) for n in cst]
        toks += [c.dma("sync", gcol[n][:], GC[n].ap()) for n in gcol]
        toks.append(c.dma("sync", srow[0:1, 0:16], ALLS.ap()))
        v_cl = c.join("vector", toks)
        wtoks = [c.dma("gpsimd", wsc[n].ap(), W2[n].ap()) for n in W2]
        v_wcast = c.join("gpsimd", wtoks)
        e0 = VS(lambda e: e.memset(onesr[:], 1.0), [v_cl])
        e0 = VS(lambda e: e.memset(eps2[:], EPS_RMS), [e0])
        e0 = VS(lambda e: e.memset(c0t[:], -C0), [e0])
        e0 = VS(lambda e: e.memset(
            AP(Vg, 64, [[4 * H * 65, 128], [65, 4 * H], [1, 1]]), 1.0), [e0])
        v_init = e0

        pn_toks = []
        for f in range(8):
            for jt8 in range(8):
                ld = c.dma("sync", tmpf[0][:],
                           AP(PROCT, f * 128 * T + jt8 * 512, [[T, 128], [1, 512]]),
                           wait=[grd.get("tmpf0")])
                tw = None
                for q in range(4):
                    tw = TS(lambda e, q=q: e.transpose(
                        pb[5][:, q * 128:(q + 1) * 128],
                        tmpf[0][:, q * 128:(q + 1) * 128], cst["idf"][:]),
                        [ld, v_cl, grd.get("tr5")] if q == 0 else [])
                ev = VS(lambda e: e.tensor_copy(trb[:], pb[5][:]),
                        [tw, grd.get("trb")])
                grd["tr5"] = ev
                grd["tmpf0"] = ev
                st = c.dma("gpsimd",
                           AP(PROCN, jt8 * 512 * D + f * 128,
                              [[D, 128], [128 * D, 4], [1, 128]]),
                           AP(trb, 0, [[G, 128], [128, 4], [1, 128]]),
                           wait=[ev])
                grd["trb"] = st
                pn_toks.append(st)
        d_procn = c.join("gpsimd", pn_toks)

        e1 = VS(lambda e: e.tensor_reduce(s11["m1"][:], srow[0:1, 0:8], AX.X, ALU.add), [v_cl])
        e1 = VS(lambda e: e.tensor_scalar(s11["m1"][:], s11["m1"][:], 1.0 / (8 * T), None, ALU.mult), [e1])
        e1 = VS(lambda e: e.tensor_reduce(s11["m2"][:], srow[0:1, 8:16], AX.X, ALU.add), [e1])
        e1 = VS(lambda e: e.tensor_scalar(s11["m2"][:], s11["m2"][:], 1.0 / (8 * T), None, ALU.mult), [e1])

        def bcast11(src11, dstcol, wait, bank=6):
            tb = TS(lambda e, bank=bank: e.matmul(
                pb[bank][:, 0:1], cst["onesr_f"][:], src11, start=True, stop=True),
                list(wait) + [grd.get(f"bc{bank}")])
            ev = VS(lambda e, bank=bank: e.tensor_copy(dstcol, pb[bank][:, 0:1]), [tb])
            grd[f"bc{bank}"] = ev
            return ev

        e2_ = bcast11(s11["m1"][:], col1[:], [e1])
        e2_ = bcast11(s11["m2"][:], col2[:], [e2_])
        dl1 = c.dma("sync", zcol[:], AP(DSTI, 0, [[32, 128], [1, 32]]))
        dl2 = c.dma("sync", m32["ge"][:], AP(DCHI, 0, [[32, 128], [1, 32]]))
        e3 = VS(lambda e: e.tensor_scalar(zcol[:], zcol[:], col1[:], None, ALU.subtract),
                [dl1, e2_])
        e3 = VS(lambda e: e.tensor_scalar(m32["ge"][:], m32["ge"][:], col2[:], None, ALU.subtract),
                [dl2, e3])
        e3 = VS(lambda e: e.tensor_tensor(zcol[:], zcol[:], m32["ge"][:], ALU.add), [e3])
        dz = c.dma("gpsimd", BNZ.ap(), zcol[:], wait=[e3])
        dza = c.dma("sync", A32[:], AP(BNZ, 0, [[128, 32], [1, 128]]), wait=[dz])
        tzt = TS(lambda e: e.transpose(pb[4][:, 0:32], A32[:], cst["idf"][0:32, 0:32]),
                 [dza, v_cl, grd.get("bc4")])
        ezt = VS(lambda e: e.tensor_copy(zTbf[:], pb[4][:, 0:32]), [tzt])
        grd["bc4"] = ezt

        def colmax(srccol, dst11, wait):
            tb = TS(lambda e: e.transpose(pb[4][0:1, 0:128], srccol, cst["idf"][:]),
                    list(wait) + [grd.get("bc4")])
            ev = VS(lambda e: e.tensor_copy(rowB[0:1, 0:128], pb[4][0:1, 0:128]), [tb])
            grd["bc4"] = ev
            ev = VS(lambda e: e.tensor_reduce(dst11, rowB[0:1, 0:128], AX.X, ALU.max), [ev])
            return ev

        em = VS(lambda e: e.tensor_reduce(col3[:], zcol[:], AX.X, ALU.max), [e3])
        em = colmax(col3[:], s11["hi"][:], [em])
        em2 = VS(lambda e: e.tensor_scalar(m32["eq"][:], zcol[:], -1.0, None, ALU.mult), [e3])
        em2 = VS(lambda e: e.tensor_reduce(col3[:], m32["eq"][:], AX.X, ALU.max), [em2, em])
        em2 = colmax(col3[:], s11["lo"][:], [em2])
        ew = VS(lambda e: e.tensor_scalar(s11["lo"][:], s11["lo"][:], -1.0, -1.0,
                                          ALU.mult, ALU.add), [em2])

        for it in range(40):
            ew = VS(lambda e: e.tensor_tensor(s11["mid"][:], s11["lo"][:], s11["hi"][:], ALU.add), [ew])
            ew = VS(lambda e: e.tensor_scalar(s11["mid"][:], s11["mid"][:], 0.5, None, ALU.mult), [ew])
            ew = bcast11(s11["mid"][:], col1[:], [ew])
            ew = VS(lambda e: e.tensor_scalar(m32["ge"][:], zcol[:], col1[:], None, ALU.is_gt), [ew])
            ew = VS(lambda e: e.tensor_reduce(col2[:], m32["ge"][:], AX.X, ALU.add), [ew])
            ew = VS(lambda e: e.tensor_copy(colbf[:], col2[:]), [ew])
            tc = TS(lambda e: e.matmul(pb[7][0:1, 0:1], colbf[:], cst["ones_col_bf"][:],
                                       start=True, stop=True), [ew, grd.get("bc7")])
            ew = VS(lambda e: e.tensor_scalar(s11["pred"][:], pb[7][0:1, 0:1], KSEL - 0.5,
                                              None, ALU.is_gt), [tc])
            grd["bc7"] = ew
            ew = VS(lambda e: e.tensor_tensor(s11["t1"][:], s11["mid"][:], s11["lo"][:], ALU.subtract), [ew])
            ew = VS(lambda e: e.tensor_tensor(s11["t1"][:], s11["t1"][:], s11["pred"][:], ALU.mult), [ew])
            ew = VS(lambda e: e.tensor_tensor(s11["lo"][:], s11["lo"][:], s11["t1"][:], ALU.add), [ew])
            ew = VS(lambda e: e.tensor_tensor(s11["t2"][:], s11["hi"][:], s11["mid"][:], ALU.subtract), [ew])
            ew = VS(lambda e: e.tensor_tensor(s11["t2"][:], s11["t2"][:], s11["pred"][:], ALU.mult), [ew])
            ew = VS(lambda e: e.tensor_tensor(s11["hi"][:], s11["mid"][:], s11["t2"][:], ALU.add), [ew])

        ew = bcast11(s11["hi"][:], col1[:], [ew])
        ew = VS(lambda e: e.tensor_scalar(m32["ge"][:], zcol[:], col1[:], None, ALU.is_gt), [ew])
        ew = VS(lambda e: e.tensor_scalar(m32["ge"][:], m32["ge"][:], BIG, None, ALU.mult), [ew])
        ew = VS(lambda e: e.tensor_tensor(m32["ge"][:], zcol[:], m32["ge"][:], ALU.subtract), [ew])
        ew = VS(lambda e: e.tensor_reduce(col3[:], m32["ge"][:], AX.X, ALU.max), [ew])
        ew = colmax(col3[:], s11["theta"][:], [ew])
        ew = bcast11(s11["theta"][:], col1[:], [ew])
        ew = VS(lambda e: e.tensor_scalar(m32["gt"][:], zcol[:], col1[:], None, ALU.is_gt), [ew])
        ew = VS(lambda e: e.tensor_scalar(m32["eq"][:], zcol[:], col1[:], None, ALU.is_equal), [ew])
        ew = VS(lambda e: e.tensor_reduce(col2[:], m32["gt"][:], AX.X, ALU.add), [ew])
        ew = VS(lambda e: e.tensor_copy(colbf[:], col2[:]), [ew])
        tc = TS(lambda e: e.matmul(pb[7][0:1, 0:1], colbf[:], cst["ones_col_bf"][:],
                                   start=True, stop=True), [ew, grd.get("bc7")])
        ew = VS(lambda e: e.tensor_scalar(s11["kk"][:], pb[7][0:1, 0:1], -1.0, KSEL,
                                          ALU.mult, ALU.add), [tc])
        grd["bc7"] = ew

        def prefix_excl(src, dst, tmp, wprev):
            w = VS(lambda e: e.tensor_reduce(col2[:], src[:], AX.X, ALU.add), [wprev])
            w = VS(lambda e: e.tensor_copy(colbf[:], col2[:]), [w])
            tb2 = TS(lambda e: e.matmul(pb[7][:, 0:1], cst["SUstT"][:], colbf[:],
                                        start=True, stop=True), [w, grd.get("bc7")])
            w = VS(lambda e: e.tensor_copy(tmp[:], src[:]), [tb2])
            for k in [1, 2, 4, 8, 16]:
                w = VS(lambda e, k=k: e.tensor_copy(dst[:], tmp[:]), [w])
                w = VS(lambda e, k=k: e.tensor_tensor(
                    dst[:, k:32], dst[:, k:32], tmp[:, 0:32 - k], ALU.add), [w])
                w = VS(lambda e: e.tensor_copy(tmp[:], dst[:]), [w])
            w = VS(lambda e: e.tensor_tensor(dst[:], tmp[:], src[:], ALU.subtract), [w])
            w = VS(lambda e: e.tensor_copy(col3[:], pb[7][:, 0:1]), [w])
            grd["bc7"] = w
            w = VS(lambda e: e.tensor_scalar(dst[:], dst[:], col3[:], None, ALU.add), [w])
            return w

        ew = prefix_excl(m32["eq"], m32["pre"], m32["pre2"], ew)
        ew = bcast11(s11["kk"][:], col1[:], [ew])
        ew = VS(lambda e: e.tensor_scalar(m32["pre"][:], m32["pre"][:], col1[:], None, ALU.is_lt), [ew])
        ew = VS(lambda e: e.tensor_tensor(m32["pre"][:], m32["pre"][:], m32["eq"][:], ALU.mult), [ew])
        ew = VS(lambda e: e.tensor_tensor(m32["msel"][:], m32["gt"][:], m32["pre"][:], ALU.add), [ew])
        ew = prefix_excl(m32["msel"], m32["rank"], m32["pre2"], ew)
        ew = VS(lambda e: e.tensor_tensor(m32["rank"][:], m32["rank"][:], m32["msel"][:], ALU.mult), [ew])
        ew = VS(lambda e: e.tensor_scalar(m32["pre"][:], m32["msel"][:], -4096.0, 4096.0,
                                          ALU.mult, ALU.add), [ew])
        ew = VS(lambda e: e.tensor_tensor(m32["rank"][:], m32["rank"][:], m32["pre"][:], ALU.add), [ew])
        drk = c.dma("gpsimd", BNR.ap(), m32["rank"][:], wait=[ew])
        dra = c.dma("sync", A32[:], AP(BNR, 0, [[128, 32], [1, 128]]), wait=[drk])
        trt = TS(lambda e: e.transpose(pb[4][:, 0:32], A32[:], cst["idf"][0:32, 0:32]),
                 [dra, grd.get("bc4")])
        erk = VS(lambda e: e.tensor_copy(rkT[:], pb[4][:, 0:32]), [trt])
        grd["bc4"] = erk

        for jt in range(32):
            eb = VS(lambda e, jt=jt: e.tensor_scalar(
                PTt[:], cst["iota1024_2d"][:], rkT[:, jt:jt + 1], None, ALU.is_equal),
                [erk, ezt] + ([grd.get("ptt")] if jt else []))
            for ch in range(2):
                TS(lambda e, jt=jt, ch=ch: e.matmul(
                    pb[ch][0:1, :], cst["hi32_bf"][:, jt:jt + 1],
                    PTt[:, ch * 512:(ch + 1) * 512],
                    start=(jt == 0), stop=False), [eb])
                TS(lambda e, jt=jt, ch=ch: e.matmul(
                    pb[ch][0:1, :], cst["iotaP_bf"][:],
                    PTt[:, ch * 512:(ch + 1) * 512],
                    start=False, stop=(jt == 31)), [])
                TS(lambda e, jt=jt, ch=ch: e.matmul(
                    pb[2 + ch][0:1, :], zTbf[:, jt:jt + 1],
                    PTt[:, ch * 512:(ch + 1) * 512],
                    start=(jt == 0), stop=(jt == 31)), [])
            grd["ptt"] = (sT, sT.v)
        t_selmm = (sT, sT.v)
        ew = VS(lambda e: e.tensor_copy(srow[0:1, 0:512], pb[0][0:1, :]), [t_selmm])
        ew = VS(lambda e: e.tensor_copy(srow[0:1, 512:1024], pb[1][0:1, :]), [ew])
        dsl = c.dma("gpsimd", BNS.ap(), srow[0:1, 0:T2], wait=[ew])
        dbg1 = c.dma("gpsimd", DBG_SEL.ap(), srow[0:1, 0:T2], wait=[ew])
        c.outs.append(dbg1)
        ez = VS(lambda e: e.tensor_copy(rowB[0:1, 0:512], pb[2][0:1, :]), [ew])
        ez = VS(lambda e: e.tensor_copy(rowB[0:1, 512:1024], pb[3][0:1, :]), [ez])
        v_evac = ez
        sc1_ = SC(lambda e: e.activation(srow[0:1, 0:T2], rowB[0:1, 0:T2], AF.Exp,
                                         bias=0.0, scale=-1.0), [ez, dsl, dbg1])
        ez = VS(lambda e: e.tensor_scalar(srow[0:1, 0:T2], srow[0:1, 0:T2], 1.0, None, ALU.add), [sc1_])
        ez = VS(lambda e: e.reciprocal(srow[0:1, 0:T2], srow[0:1, 0:T2]), [ez])
        v_scores = ez
        c.outs.append(c.dma("gpsimd", DBG_SC.ap(), srow[0:1, 0:T2], wait=[ez]))
        dsa_ = c.dma("sync", A32[0:8, :], AP(BNS, 0, [[128, 8], [1, 128]]), wait=[dsl])
        tsc = TS(lambda e: e.transpose(pb[4][:, 0:8], A32[0:8, :], cst["idf"][0:8, 0:8]),
                 [dsa_, grd.get("bc4")])
        esc = VS(lambda e: e.tensor_copy(selTc[:], pb[4][:, 0:8]), [tsc])
        grd["bc4"] = esc
        dsel2 = c.dma("sync", rowB[0:1, 0:T2], BNS.ap(), wait=[dsl, sc1_])

        pjbuf = [wtile, wcols]
        for ch in range(2):
            for jt in range(32):
                pj = pjbuf[jt % 2]
                ldp = c.dma("sync", pj[:, 0:1024],
                            AP(PROCN, jt * 128 * D, [[D, 128], [1, D]]),
                            wait=[d_procn, grd.get(f"projt{jt % 2}")])
                eb = VS(lambda e, jt=jt: e.tensor_scalar(
                    PTt[:], cst["iota1024_2d"][:], rkT[:, jt:jt + 1], None, ALU.is_equal),
                    [grd.get("ptt"), erk])
                for f in range(8):
                    TS(lambda e, jt=jt, f=f, ch=ch, pj=pj: e.matmul(
                        pb[f][:], pj[:, f * 128:(f + 1) * 128],
                        PTt[:, ch * 512:(ch + 1) * 512],
                        start=(jt == 0), stop=(jt == 31)),
                       ([ldp, eb, v_evac, esc, grd.get(f"gat{f}")] if jt == 0
                        else [ldp, eb]))
                grd["ptt"] = (sT, sT.v)
                grd[f"projt{jt % 2}"] = (sT, sT.v)
            for f in range(8):
                ev = VS(lambda e, f=f, ch=ch: e.tensor_copy(
                    selT[f][:, ch * 512:(ch + 1) * 512], pb[f][:]), [(sT, sT.v)])
                grd[f"gat{f}"] = ev
        v_selT = (sV, sV.v)
        for f in range(8):
            c.outs.append(c.dma("gpsimd", AP(DBG_SELT, f * 128 * T2, [[T2, 128], [1, T2]]),
                                selT[f][:], wait=[v_selT]))
        grd["pb01"] = v_selT
        grd["pb4"] = v_selT
        grd["pbb6"] = v_selT
        grd["pbb7"] = v_selT
        grd["pb2"] = v_selT
        grd["pb3"] = v_selT
        grd["wtile"] = grd["projt0"]
        grd["wcols"] = grd["projt1"]

        def ln_rms(src_aps, dst_aps, gains, wait_first):
            vc = []
            for kt in range(8):
                w = [wait_first] if kt == 0 else []
                vc.append(p.add("vector", lambda e, kt=kt: e.tensor_copy(
                    xnbf[kt][:], src_aps[kt]), wait=[x for x in w if x],
                    inc=[(sV, 1)])[0])
            sqmm = {}
            for kt in range(8):
                vsq = p.add("vector", lambda e, kt=kt: e.tensor_tensor(
                    sqb[kt % 2][:], xnbf[kt][:], xnbf[kt][:], ALU.mult),
                    wait=[(sV, vc[kt])] + ([(sT, sqmm[kt - 2])] if kt >= 2 else []),
                    inc=[(sV, 1)])[0]
                sqmm[kt] = p.add("tensor", lambda e, kt=kt: e.matmul(
                    pb[1][0:1, :], cst["ones_col_bf"][:], sqb[kt % 2][:],
                    start=(kt == 0), stop=(kt == 7)),
                    wait=[(sV, vsq)] + ([grd.get("pb01")] if kt == 0 else []),
                    inc=[(sT, 1)])[0]
            t_sq = sqmm[7]
            w = VS(lambda e: e.tensor_scalar(rows["var"][:], pb[1][0:1, :],
                                             1.0 / D, None, ALU.mult), [(sT, t_sq)])
            s1_ = SC(lambda e: e.activation(rows["sc1"][:], rows["var"][:], AF.Ln,
                                            bias=eps2[:], scale=1.0), [w])
            s2_ = SC(lambda e: e.activation(rows["rstd"][:], rows["sc1"][:], AF.Exp,
                                            bias=0.0, scale=-0.5), [s1_])
            w = VS(lambda e: e.tensor_copy(rows_bf["rstd"][:], rows["rstd"][:]),
                   [s2_, grd.get("lnrows")])
            tb_ = TS(lambda e: e.matmul(pb[1][:], onesr[:], rows_bf["rstd"][:],
                                        start=True, stop=True), [w])
            grd["lnrows"] = tb_
            out_v = None
            for kt in range(8):
                out_v = p.add("vector", lambda e, kt=kt: e.scalar_tensor_tensor(
                    dst_aps[kt], src_aps[kt], gcol[gains][:, kt:kt + 1], pb[1][:],
                    ALU.mult, ALU.mult),
                    wait=([tb_] if kt == 0 else ()), inc=[(sV, 1)])[0]
            grd["pb01"] = (sV, out_v)
            return (sV, out_v)

        for g in range(NG2):
            gl = g * G
            ta = TS(lambda e, gl=gl: e.matmul(
                pb[4][0:32, :], cst["inv_row"][:], rowB[0:1, gl:gl + G],
                start=True, stop=True), [dsel2, grd.get("pb4")])
            ea = VS(lambda e: e.tensor_copy(ang[:], pb[4][0:32, :]), [ta, grd.get("trig")])
            grd["pb4"] = ea
            ea = VS(lambda e: e.tensor_scalar(angk[:], ang[:], INV2PI, None, ALU.mult), [ea])
            ea = VS(lambda e: e.tensor_copy(angi[:], angk[:]), [ea])
            ea = VS(lambda e: e.tensor_copy(angk[:], angi[:]), [ea])
            ea = VS(lambda e: e.tensor_scalar(rtmp[0:32, :], angk[:], CW_HI, None, ALU.mult), [ea])
            ea = VS(lambda e: e.tensor_tensor(ang[:], ang[:], rtmp[0:32, :], ALU.subtract), [ea])
            ea = VS(lambda e: e.tensor_scalar(rtmp[0:32, :], angk[:], CW_MID, None, ALU.mult), [ea])
            ea = VS(lambda e: e.tensor_tensor(ang[:], ang[:], rtmp[0:32, :], ALU.subtract), [ea])
            ea = VS(lambda e: e.tensor_scalar(rtmp[0:32, :], angk[:], CW_LO, None, ALU.mult), [ea])
            ea = VS(lambda e: e.tensor_tensor(ang[:], ang[:], rtmp[0:32, :], ALU.subtract), [ea])
            es_ = SC(lambda e: e.activation(sinrep[0:32, :], ang[:], AF.Sin,
                                            bias=0.0, scale=1.0), [ea])
            ea = VS(lambda e: e.tensor_scalar(angk[:], ang[:], PI / 2, None, ALU.add), [ea])
            ea = VS(lambda e: e.tensor_scalar(rtmp[0:32, :], angk[:], PI, None, ALU.is_gt), [ea])
            ea = VS(lambda e: e.tensor_scalar(rtmp[0:32, :], rtmp[0:32, :], 2 * PI, None, ALU.mult), [ea])
            ea = VS(lambda e: e.tensor_tensor(angk[:], angk[:], rtmp[0:32, :], ALU.subtract), [ea])
            ec_ = SC(lambda e: e.activation(cosrep[0:32, :], angk[:], AF.Sin,
                                            bias=0.0, scale=1.0), [ea, es_])
            et = ec_
            for q in range(1, 4):
                et = VS(lambda e, q=q: e.tensor_copy(
                    cosrep[q * 32:(q + 1) * 32, :], cosrep[0:32, :]), [et])
                et = VS(lambda e, q=q: e.tensor_copy(
                    sinrep[q * 32:(q + 1) * 32, :], sinrep[0:32, :]), [et])
            v_trig = et

            v_r1_ = ln_rms([selT[i][:, gl:gl + G] for i in range(8)],
                           [h1bf[i][:] for i in range(8)], "g12",
                           grd.get("h1bf_free") or v_selT)

            for ri, (role, Wn, scl) in enumerate([("k", "wk2p", 1.0),
                                                  ("q", "wq2p", 0.125)]):
                for j4 in range(4):
                    pA, pB = ((2, 3) if (ri * 4 + j4) % 2 == 0 else (6, 7))
                    lwA = c.dma("sync", wtile[:], wtile_ap(wsc[Wn], 2 * j4),
                                wait=[v_wcast, grd.get("wtile")])
                    lwB = c.dma("sync", wcols[:, 0:1024], wtile_ap(wsc[Wn], 2 * j4 + 1),
                                wait=[v_wcast, grd.get("wcols")])
                    for i in range(8):
                        p.add("tensor", lambda e, i=i, pA=pA: e.matmul(
                            pb[pA][:], wtile[:, i * 128:(i + 1) * 128], h1bf[i][:],
                            start=(i == 0), stop=(i == 7)),
                            wait=([lwA, v_r1_, grd.get(f"bank{pA}")] if i == 0 else ()),
                            inc=([(sT, 1)] if i == 7 else ()))
                    tA = sT.v
                    for i in range(8):
                        p.add("tensor", lambda e, i=i, pB=pB: e.matmul(
                            pb[pB][:], wcols[:, i * 128:(i + 1) * 128], h1bf[i][:],
                            start=(i == 0), stop=(i == 7)),
                            wait=([lwB, grd.get(f"bank{pB}")] if i == 0 else ()),
                            inc=([(sT, 1)] if i == 7 else ()))
                    tB = sT.v
                    grd["wtile"] = (sT, tA)
                    grd["wcols"] = (sT, tB)
                    ew_ = VS(lambda e, pA=pA: e.tensor_tensor(
                        tmpf[0][:], pb[pA][:], cosrep[:], ALU.mult),
                        [(sT, tB), v_trig, grd.get("ropebuf")])
                    ew_ = VS(lambda e, pB=pB: e.tensor_tensor(
                        tmpf[1][:], pb[pB][:], sinrep[:], ALU.mult), [ew_])
                    ew_ = VS(lambda e: e.tensor_tensor(tmpf[0][:], tmpf[0][:], tmpf[1][:],
                                                       ALU.subtract), [ew_])
                    ew_ = VS(lambda e, pA=pA: e.tensor_tensor(
                        rtmp[:], pb[pA][:], sinrep[:], ALU.mult), [ew_])
                    ew_ = VS(lambda e, pB=pB: e.tensor_tensor(
                        tmpf[1][:], pb[pB][:], cosrep[:], ALU.mult), [ew_])
                    vrp = VS(lambda e: e.tensor_tensor(rtmp[:], rtmp[:], tmpf[1][:],
                                                       ALU.add), [ew_])
                    grd[f"bank{pA}"] = vrp
                    grd[f"bank{pB}"] = vrp
                    dst_ = QTa if role == "q" else kpiece
                    sv0 = None
                    for m in range(4):
                        hh = 4 * j4 + m
                        o0 = hh * G if role == "q" else m * G
                        w0 = ([vrp, grd.get("qta2_free" if role == "q" else "kp_free")]
                              if m == 0 else ())
                        sv0 = p.add("scalar", lambda e, m=m, o0=o0, dst_=dst_, scl=scl:
                                    e.activation(dst_[0:32, o0:o0 + G],
                                                 tmpf[0][32 * m:32 * m + 32, :],
                                                 AF.Identity, bias=0.0, scale=scl),
                                    wait=w0, inc=[(sS, 1)])[0]
                        sv0 = p.add("scalar", lambda e, m=m, o0=o0, dst_=dst_, scl=scl:
                                    e.activation(dst_[32:64, o0:o0 + G],
                                                 rtmp[32 * m:32 * m + 32, :],
                                                 AF.Identity, bias=0.0, scale=scl),
                                    inc=[(sS, 1)])[0]
                    grd["ropebuf"] = (sS, sv0)
                    if role == "k":
                        dk = c.dma("gpsimd",
                                   AP(KT2d, (4 * j4) * T2 + gl,
                                      [[H * T2, 64], [T2, 4], [1, G]]),
                                   AP(kpiece, 0, [[4 * G, 64], [G, 4], [1, G]]),
                                   wait=[(sS, sv0)])
                        grd["kp_free"] = dk
                        grd.setdefault("kst_toks", []).append(dk)
                if role == "q":
                    s_q2 = sS.v
            d_kst2 = c.join("gpsimd", grd.pop("kst_toks"))

            for half in range(2):
                lw = c.dma("sync", wcols[:],
                           AP(wsc["wv2"], half * 512, [[D, 128], [128 * D, 8], [1, 512]]),
                           wait=[v_wcast, grd.get("wcols")])
                for tt_ in range(4):
                    b = 2 + (half * 4 + tt_) % 2
                    for i in range(8):
                        p.add("tensor", lambda e, i=i, tt_=tt_, b=b: e.matmul(
                            pb[b][:], h1bf[i][:, tt_ * 128:(tt_ + 1) * 128],
                            wcols[:, i * 512:(i + 1) * 512],
                            start=(i == 0), stop=(i == 7)),
                            wait=([lw, v_r1_, grd.get(f"bank{b}")] if i == 0 else ()),
                            inc=([(sT, 1)] if i == 7 else ()))
                    t_v = sT.v
                    v_ev = VS(lambda e, tt_=tt_, half=half, b=b: e.tensor_copy(
                        AP(Vg, (half * 8 * 4 + tt_) * 65,
                           [[4 * H * 65, 128], [4 * 65, 8], [1, 64]]),
                        pb[b][:]), [(sT, t_v), grd.get("vg2_free"), v_init])
                    grd[f"bank{b}"] = v_ev
                grd["wcols"] = (sT, sT.v)
            v_v2 = (sV, sV.v)
            d_vst2 = c.dma("gpsimd",
                           AP(V2d, 4 * g * 65,
                              [[H * 8 * 65, 128], [8 * 65, H], [1, 260]]),
                           AP(Vg, 0, [[4 * H * 65, 128], [260, H], [1, 260]]),
                           wait=[v_v2])
            grd["vg2_free"] = d_vst2

            LAG = 3
            for hs in range(4):
                h0 = hs * 4
                av_started = [False] * 4
                pending = []

                def emit_av2(rec):
                    hl, kt, ppe, s_exp, lwv, first, last, strip_first = rec
                    p.add("tensor", lambda e, hl=hl, kt=kt, ppe=ppe,
                          first=first, last=last: e.matmul(
                              pb[hl][0:65, :],
                              AP(Vld, (hl * 4 + kt) * 65,
                                 [[4 * 4 * 65, 128], [1, 65]]),
                              expS[ppe][:],
                              start=first, stop=last),
                          wait=[(sS, s_exp)] + ([lwv] if strip_first else [])
                          + ([grd.get(f"av{hl}"), grd.get(f"bank{hl}")]
                             if first else []),
                          inc=[(sT, 1)])
                    grd[f"expS{ppe}"] = (sT, sT.v)

                tileno = 0
                for ks in range(g + 1):
                    lwk = c.dma("scalar", KTld[:],
                                AP(KT2d, h0 * T2 + ks * G,
                                   [[H * T2, 64], [T2, 4], [1, G]]),
                                wait=[d_kst2, grd.get("ktld2")])
                    lwv = c.dma("scalar", Vld[:],
                                AP(V2d, h0 * 8 * 65 + 4 * ks * 65,
                                   [[H * 8 * 65, 128], [8 * 65, 4], [1, 260]]),
                                wait=[d_vst2, grd.get("vld2")])
                    crossing = (ks == g)
                    strip_new = True
                    for hl in range(4):
                        for kt in range(4):
                            bqk = 4 + tileno % 4
                            wq_ = [grd.get(f"bank{bqk}")]
                            if strip_new:
                                wq_ = wq_ + [lwk]
                            if hs == 0 and tileno == 0:
                                wq_ = wq_ + [(sS, s_q2)]
                            if crossing:
                                p.add("tensor", lambda e, kt=kt, bqk=bqk: e.matmul(
                                    pb[bqk][:], cst["SUge"][:],
                                    cst["Sh4"][:, kt * G:(kt + 1) * G],
                                    start=True, stop=False), wait=wq_)
                            p.add("tensor", lambda e, hl=hl, kt=kt, bqk=bqk, h0=h0,
                                  crossing=crossing: e.matmul(
                                      pb[bqk][:],
                                      KTld[:, (hl * 4 + kt) * 128:(hl * 4 + kt + 1) * 128],
                                      QTa[:, (h0 + hl) * G:(h0 + hl + 1) * G],
                                      start=(not crossing), stop=True),
                                  wait=(wq_ if not crossing else ()),
                                  inc=[(sT, 1)])
                            t_qk = sT.v
                            ppe = tileno % 4
                            s_exp = p.add("scalar", lambda e, ppe=ppe, bqk=bqk:
                                          e.activation(expS[ppe][:], pb[bqk][:],
                                                       AF.Exp, bias=c0t[:], scale=1.0),
                                          wait=[(sT, t_qk),
                                                grd.get(f"expS{ppe}")]
                                          + ([v_init] if tileno == 0 else []),
                                          inc=[(sS, 1)])[0]
                            grd[f"bank{bqk}"] = (sS, s_exp)
                            first = not av_started[hl]
                            av_started[hl] = True
                            last = (ks == g and kt == 3)
                            pending.append((hl, kt, ppe, s_exp, lwv, first, last,
                                            strip_new))
                            strip_new = False
                            tileno += 1
                            if len(pending) > LAG:
                                emit_av2(pending.pop(0))
                    while pending:
                        emit_av2(pending.pop(0))
                    grd["ktld2"] = (sT, sT.v)
                    grd["vld2"] = (sT, sT.v)
                t_av_done = sT.v
                for hl in range(4):
                    h = h0 + hl
                    s3 = p.add("scalar", lambda e, hl=hl: e.activation(
                        rows["recip"][:], pb[hl][64:65, :], AF.Ln,
                        bias=0.0, scale=1.0),
                        wait=[(sT, t_av_done), grd.get("reciprow")],
                        inc=[(sS, 1)])[0]
                    v4 = p.add("scalar", lambda e: e.activation(
                        rows_bf["recip"][:], rows["recip"][:], AF.Exp,
                        bias=0.0, scale=-1.0), inc=[(sS, 1)])[0]
                    bb = 6 + (hl % 2)
                    t_bc = p.add("tensor", lambda e, bb=bb: e.matmul(
                        pb[bb][0:64, :], onesr[0:1, 0:64], rows_bf["recip"][0:1, :],
                        start=True, stop=True),
                        wait=[(sS, v4), grd.get(f"pbb{bb}"), grd.get(f"bank{bb}")],
                        inc=[(sT, 1)])[0]
                    grd["reciprow"] = (sT, t_bc)
                    v4b = p.add("vector", lambda e, bb=bb: e.tensor_copy(
                        recb[0:64, :], pb[bb][0:64, :]),
                        wait=[(sT, t_bc)], inc=[(sV, 1)])[0]
                    grd[f"pbb{bb}"] = (sV, v4b)
                    grd[f"bank{bb}"] = (sV, v4b)
                    v5 = p.add("vector", lambda e, h=h, hl=hl: e.tensor_tensor(
                        OTbf[h // 2][(h % 2) * 64:(h % 2) * 64 + 64, :],
                        pb[hl][0:64, :], recb[0:64, :], ALU.mult),
                        inc=[(sV, 1)])[0]
                    grd[f"av{hl}"] = (sV, v5)
                    grd[f"bank{hl}"] = (sV, v5)
            v_attn2 = (sV, sV.v)
            grd["qta2_free"] = (sT, sT.v)

            for f in range(8):
                b = 2 + f % 2
                lw = c.dma("sync", wtile[:], wtile_ap(wsc["wo2"], f),
                           wait=[grd.get("wtile")])
                for i in range(8):
                    p.add("tensor", lambda e, i=i, b=b: e.matmul(
                        pb[b][:], wtile[:, i * 128:(i + 1) * 128], OTbf[i][:],
                        start=(i == 0), stop=(i == 7)),
                        wait=([lw, v_attn2, grd.get(f"bank{b}")] if i == 0 else ()),
                        inc=([(sT, 1)] if i == 7 else ()))
                t_mm = sT.v
                grd["wtile"] = (sT, t_mm)
                s_ev = SC(lambda e, b=b: e.activation(tmpf[1][:], pb[b][:], AF.Identity,
                                                      bias=0.0, scale=1.0),
                          [(sT, t_mm), grd.get("tmpf1")])
                grd[f"bank{b}"] = s_ev
                v_x2 = VS(lambda e, f=f, gl=gl: e.tensor_tensor(
                    x2g[f][:], tmpf[1][:], selT[f][:, gl:gl + G], ALU.add),
                    [s_ev, grd.get("x2g")])
                grd["tmpf1"] = v_x2
            v_x2d = (sV, sV.v)

            tb = TS(lambda e, gl=gl: e.matmul(
                pb[6][:], cst["onesr_f"][:], srow[0:1, gl:gl + G],
                start=True, stop=True), [v_scores, grd.get("pbb6")])
            v_srep = VS(lambda e: e.tensor_copy(srep[:], pb[6][:]),
                        [tb, grd.get("srep_free")])
            grd["pbb6"] = v_srep

            v_r2_ = ln_rms([x2g[i][:] for i in range(8)],
                           [h1bf[i][:] for i in range(8)], "g22", v_x2d)

            for fo in range(32):
                bG = 2 if fo % 2 == 0 else 6
                bU = 3 if fo % 2 == 0 else 7
                lwg = c.dma("sync", wtile[:], wtile_ap(wsc["wg2"], fo),
                            wait=[v_wcast, grd.get("wtile")])
                lwu = c.dma("sync", wcols[:, 0:1024], wtile_ap(wsc["wu2"], fo),
                            wait=[v_wcast, grd.get("wcols")])
                for i in range(8):
                    p.add("tensor", lambda e, i=i, bG=bG: e.matmul(
                        pb[bG][:], wtile[:, i * 128:(i + 1) * 128], h1bf[i][:],
                        start=(i == 0), stop=(i == 7)),
                        wait=([lwg, v_r2_, grd.get(f"bank{bG}")] if i == 0 else ()),
                        inc=([(sT, 1)] if i == 7 else ()))
                tG = sT.v
                for i in range(8):
                    p.add("tensor", lambda e, i=i, bU=bU: e.matmul(
                        pb[bU][:], wcols[:, i * 128:(i + 1) * 128], h1bf[i][:],
                        start=(i == 0), stop=(i == 7)),
                        wait=([lwu, grd.get(f"bank{bU}")] if i == 0 else ()),
                        inc=([(sT, 1)] if i == 7 else ()))
                tU = sT.v
                grd["wtile"] = (sT, tG)
                grd["wcols"] = (sT, tU)
                s_si = SC(lambda e, bG=bG: e.activation(sgbf[:], pb[bG][:], AF.Silu,
                                                        bias=0.0, scale=1.0),
                          [(sT, tG), grd.get("sgbf")])
                v_sw = VS(lambda e, fo=fo, bU=bU: e.tensor_tensor(
                    swig[:, fo * G:(fo + 1) * G], sgbf[:], pb[bU][:], ALU.mult),
                    [s_si, (sT, tU), grd.get("swig_free")])
                grd["sgbf"] = v_sw
                grd[f"bank{bG}"] = s_si
                grd[f"bank{bU}"] = v_sw
            v_swig = (sV, sV.v)

            for f in range(8):
                b = 2 + f % 2
                lw = c.dma("sync", wcols[:],
                           AP(wsc["wd2"], f * 128, [[D, 128], [128 * D, 32], [1, 128]]),
                           wait=[grd.get("wcols")])
                for i in range(32):
                    p.add("tensor", lambda e, i=i, b=b: e.matmul(
                        pb[b][:], wcols[:, i * 128:(i + 1) * 128],
                        swig[:, i * G:(i + 1) * G],
                        start=(i == 0), stop=(i == 31)),
                        wait=([lw, v_swig, grd.get(f"bank{b}")] if i == 0 else ()),
                        inc=([(sT, 1)] if i == 31 else ()))
                t_mm = sT.v
                grd["wcols"] = (sT, t_mm)
                s_ev = SC(lambda e, b=b: e.activation(tmpf[1][:], pb[b][:], AF.Identity,
                                                      bias=0.0, scale=1.0),
                          [(sT, t_mm), grd.get("tmpf1")])
                grd[f"bank{b}"] = s_ev
                v_y = VS(lambda e, f=f: e.tensor_tensor(
                    rtmp[:], tmpf[1][:], x2g[f][:], ALU.add),
                    [s_ev, grd.get("rtmp_y")])
                grd["pb2"] = v_y
                grd["tmpf1"] = v_y
                dby = c.dma("gpsimd", AP(DBG_Y, f * 128 * T2 + gl, [[T2, 128], [1, G]]),
                            rtmp[:], wait=[v_y])
                c.outs.append(dby)
                vd = VS(lambda e, f=f, gl=gl: e.tensor_tensor(
                    rtmp[:], rtmp[:], selT[f][:, gl:gl + G], ALU.subtract),
                    [dby, v_srep])
                vd = VS(lambda e: e.tensor_tensor(
                    rtmp[:], rtmp[:], srep[:], ALU.mult), [vd])
                tw = None
                for q in range(4):
                    tw = TS(lambda e, q=q: e.transpose(
                        pb[5][:, q * 128:(q + 1) * 128],
                        rtmp[:, q * 128:(q + 1) * 128], cst["idf"][:]),
                        [vd, grd.get("tr5d")] if q == 0 else [])
                ev = None
                for q in range(4):
                    ev = VS(lambda e, q=q, g=g, f=f: e.tensor_copy(
                        deltaN[4 * g + q][:, f * 128:(f + 1) * 128],
                        pb[5][:, q * 128:(q + 1) * 128]), [tw] if q == 0 else [])
                grd["tr5d"] = ev
                grd["rtmp_y"] = ev
            grd["swig_free"] = (sT, sT.v)
            grd["h1bf_free"] = (sT, sT.v)
            grd["x2g"] = (sV, sV.v)
            grd["trig"] = (sT, sT.v)
            grd["srep_free"] = (sV, sV.v)
        v_dN = (sV, sV.v)

        oring = [tmpf[0], tmpf[1], cosrep, sinrep, rtmp, srep]
        NR = len(oring)
        octr = {"x": 0, "p": 0}

        def oload(kind, f, jc, queue):
            i = octr[kind]
            octr[kind] += 1
            buf = oring[(i % (NR // 2)) * 2 + (0 if kind == "x" else 1)]
            src = X1T if kind == "x" else PROCT
            tok = c.dma(queue, buf[:],
                        AP(src, f * 128 * T + jc * 512, [[T, 128], [1, 512]]),
                        wait=[v_dN, grd.get(f"ob{kind}{i % (NR // 2)}")])
            return buf, tok, i % (NR // 2)

        for jc in range(8):
            for it in range(8):
                eb1 = VS(lambda e, it=it, jc=jc: e.tensor_scalar(
                    col1[:], selTc[:, it:it + 1], 512.0 * jc, None, ALU.subtract),
                    [esc, v_dN, grd.get("pcol")])
                eb2 = VS(lambda e: e.tensor_scalar(
                    PTt[:, 0:512], cst["iota1024_2d"][:, 0:512], col1[:], None,
                    ALU.is_equal), [eb1, grd.get("ptt")])
                grd["pcol"] = eb2
                for f in range(8):
                    p.add("tensor", lambda e, it=it, f=f: e.matmul(
                        pb[f][:], deltaN[it][:, f * 128:(f + 1) * 128], PTt[:, 0:512],
                        start=(it == 0), stop=(it == 7)),
                        wait=([eb2] + ([grd.get(f"sc{f}")] if it == 0 else [])),
                        inc=([(sT, 1)] if f == 7 else ()))
                grd["ptt"] = (sT, sT.v)
            t_sc = (sT, sT.v)
            for f in range(8):
                bx, ldx, ix = oload("x", f, jc, "sync")
                bp, ldp2, ip = oload("p", f, jc, "scalar")
                ea1 = VS(lambda e, f=f, bx=bx: e.tensor_tensor(
                    bx[:], bx[:], pb[f][:], ALU.add), [ldx, t_sc])
                grd[f"sc{f}"] = ea1
                ea2 = VS(lambda e, f=f, bx=bx, bp=bp: e.tensor_tensor(
                    bx[:], bx[:], bp[:], ALU.add), [ldp2, ea1])
                grd[f"obp{ip}"] = ea2
                dso = c.dma("gpsimd",
                            AP(OUT, f * 128 * T + jc * 512, [[T, 128], [1, 512]]),
                            bx[:], wait=[ea2])
                grd[f"obx{ix}"] = dso
                c.outs.append(dso)

        c.join("gpsimd", c.outs)
        p.barrier("end")
        for s_ in p.sems:
            s_.h = c.es.enter_context(nc.semaphore(s_.name))
        import os
        if bool(int(os.environ.get("MK_SIM_EDGES", "0"))):
            _ENG = ENGINES; _Sem = Sem
            c.p.ord = {en: _Sem("ord_" + en) for en in _ENG}
            for s_ in c.p.ord.values():
                s_.h = c.es.enter_context(nc.semaphore(s_.name))
            c.p.emit(c.es.enter_context(nc.Block()), sim_edges=True)
        else:
            c.p.emit(c.es.enter_context(nc.Block()))
    return nc, c


def stage2_inputs(I, st1_out):
    colshape = lambda v: np.ascontiguousarray(v.reshape(8, 128).T)
    m = {
        "x1T": st1_out["x1T"], "procT": st1_out["procT"],
        "dst": st1_out["dst"], "dch": st1_out["dch"],
        "allsums": st1_out["allsums"],
        "wq2p": permute_w_rope(I["L0_ffn_wq2"]),
        "wk2p": permute_w_rope(I["L0_ffn_wk2"]),
        "wv2": I["L0_ffn_wv2"], "wo2": I["L0_ffn_wo2"],
        "wg2": I["L0_ffn_wg2"], "wu2": I["L0_ffn_wu2"], "wd2": I["L0_ffn_wd2"],
        "g12": colshape(I["L0_ffn_ln1_2"]), "g22": colshape(I["L0_ffn_ln2_2"]),
    }
    for n, v in host_consts2().items():
        m["c_" + n] = np.asarray(v)
    return m


from concourse.bass_utils import run_bass_kernel_spmd


def kernel(**inputs):
    I = {k: np.asarray(v) for k, v in inputs.items()}
    nc1, _c1 = build_stage1(n_groups=NG)
    maps1 = [stage1_inputs(I, b) for b in range(8)]
    res1 = run_bass_kernel_spmd(nc1, maps1, core_ids=list(range(8)))
    r1 = [res1.results[b] for b in range(8)]
    allsums = np.zeros((1, 16), np.float32)
    for b in range(8):
        allsums[0, b] = r1[b]["dsa"][0, 0]
        allsums[0, 8 + b] = r1[b]["dsb"][0, 0]
    nc2, _c2 = build_stage2()
    maps2 = []
    for b in range(8):
        st1 = dict(x1T=r1[b]["x1T"], procT=r1[b]["procT"],
                   dst=r1[b]["dst"], dch=r1[b]["dch"], allsums=allsums)
        maps2.append(stage2_inputs(I, st1))
    res2 = run_bass_kernel_spmd(nc2, maps2, core_ids=list(range(8)))
    out = np.empty((8, 4096, 1024), np.float32)
    for b in range(8):
        out[b] = res2.results[b]["outT"].T
    return out


# version 06 (final)
# speedup vs baseline: 1.9481x; 1.0158x over previous
import numpy as np
import ml_dtypes
from contextlib import ExitStack
import concourse.bass as bass
import concourse.mybir as mybir

F32 = mybir.dt.float32
BF16 = mybir.dt.bfloat16
I32 = mybir.dt.int32
AF = mybir.ActivationFunctionType
ALU = mybir.AluOpType
AX = mybir.AxisListType

T = 4096
D = 1024
H = 16
DH = 64
G = 512
NG = T // G
FFN = 4096
TSEL = 1024
EPS_LN = 1e-5
EPS_RMS = 1e-6
NEG = -1e9

ENGINES = ("sync", "vector", "scalar", "tensor", "gpsimd")


class Sem:
    __slots__ = ("name", "v", "h")
    def __init__(self, name):
        self.name = name
        self.v = 0
        self.h = None


class Prog:
    def __init__(self):
        self.steps = {e: [] for e in ENGINES}
        self.sems = []

    def sem(self, name):
        s = Sem(name)
        self.sems.append(s)
        return s

    def add(self, eng, fn=None, wait=(), inc=(), is_dma=False):
        wait = tuple(t for t in wait if t is not None)
        incs = []
        for s, n in inc:
            s.v += n
            incs.append((s, n, s.v))
        self.steps[eng].append((fn, tuple(wait), tuple(incs), is_dma))
        return [v for s, n, v in incs]

    def barrier(self, name):
        bs = self.sem("bar_" + name)
        for e in ENGINES:
            self.add(e, None, inc=[(bs, 1)])
        v = bs.v
        for e in ENGINES:
            self.add(e, None, wait=[(bs, v)])
        return bs

    def emit(self, block, sim_edges=False):
        if sim_edges and not hasattr(self, "ord"):
            self.ord = {e: Sem("ord_" + e) for e in ENGINES}
        def make(engname):
            def run(eng):
                osem = self.ord[engname] if sim_edges else None
                ocnt = 0
                prev_edge = None
                for fn, waits, incs, is_dma in self.steps[engname]:
                    if sim_edges and not is_dma and fn is not None and prev_edge:
                        eng.wait_ge(prev_edge[0].h, prev_edge[1])
                    for s, v in waits:
                        eng.wait_ge(s.h, v)
                    r = fn(eng) if fn is not None else None
                    if incs:
                        if r is None:
                            r = eng.nop()
                        for s, n, tv in incs:
                            r = r.then_inc(s.h, n)
                    if sim_edges and not is_dma and fn is not None:
                        if incs:
                            prev_edge = (incs[0][0], incs[0][2])
                        else:
                            ocnt += 1
                            r.then_inc(osem.h, 1)
                            prev_edge = (osem, ocnt)
            return run
        block.sync(make("sync"))
        block.vector(make("vector"))
        block.scalar(make("scalar"))
        block.tensor(make("tensor"))
        block.gpsimd(make("gpsimd"))



def host_consts():
    c = {}
    ey = np.eye(128)
    c["identity"] = ey.astype(ml_dtypes.bfloat16)
    iu = np.arange(128)
    c["SUge"] = (iu[None, :] >= iu[:, None]).astype(ml_dtypes.bfloat16)
    c["SUst"] = (iu[None, :] > iu[:, None]).astype(ml_dtypes.bfloat16)
    sh = np.zeros((128, 2048), np.float32)
    for kt in range(4):
        for f in range(512):
            r = f + 1 - 128 * kt
            if r <= 0:
                sh[0, 512 * kt + f] = NEG
            elif r < 128:
                sh[r, 512 * kt + f] = NEG
    c["Sh4"] = sh.astype(ml_dtypes.bfloat16)
    pp = (1024 * (np.arange(128)[:, None] // 32) + np.arange(1024)[None, :]).astype(np.float32)
    c["pos_packed"] = pp
    inv = (1.0 / 10000.0 ** (np.arange(32, dtype=np.float64) / 32.0))
    c["inv_col"] = np.tile(inv.reshape(32, 1), (4, 1)).astype(np.float32)
    slopes = (2.0 ** (-np.linspace(1.0, 8.0, H))).astype(np.float64)
    c["slopes_col"] = np.zeros((128, 1), np.float32)
    c["slopes_col"][:H, 0] = slopes.astype(np.float32)
    c["iota_row"] = np.arange(4096, dtype=np.float32).reshape(1, 4096)
    c["iota16"] = np.tile(np.arange(4096, dtype=np.float32)[None, :], (16, 1))
    c["iotaP_col"] = np.arange(128, dtype=np.float32).reshape(128, 1)
    c["ones_col_bf"] = np.ones((128, 1), ml_dtypes.bfloat16)
    c["ones16"] = np.ones((16, 512), ml_dtypes.bfloat16)
    c["iota1024_2d"] = np.tile(np.arange(1024, dtype=np.float32)[None, :], (128, 1))
    c["iota4096_2d"] = np.tile(np.arange(4096, dtype=np.float32)[None, :], (128, 1))
    c["hi128_row"] = (128.0 * np.arange(32, dtype=np.float32)).reshape(1, 32)
    inv64 = 1.0 / 10000.0 ** (np.arange(32, dtype=np.float64) / 32.0)
    aa, rr = np.meshgrid(np.arange(4), np.arange(32), indexing="ij")
    cpk = np.zeros((128, 1024), np.float64)
    spk = np.zeros((128, 1024), np.float64)
    for a in range(4):
        for b in range(2):
            pos = 1024.0 * a + 512.0 * b + np.arange(512)
            ang = inv64[:, None] * pos[None, :]
            cpk[32 * a:32 * a + 32, 512 * b:512 * b + 512] = np.cos(ang)
            spk[32 * a:32 * a + 32, 512 * b:512 * b + 512] = np.sin(ang)
    c["cos_pk"] = cpk.astype(np.float32)
    c["sin_pk"] = spk.astype(np.float32)
    slopes64 = 2.0 ** (-np.linspace(1.0, 8.0, H))
    iota = np.arange(T, dtype=np.float64)
    def hml(v):
        hi = v.astype(ml_dtypes.bfloat16).astype(np.float64)
        r1 = v - hi
        mid = r1.astype(ml_dtypes.bfloat16).astype(np.float64)
        lo = (r1 - mid).astype(ml_dtypes.bfloat16).astype(np.float64)
        return hi, mid, lo
    kv = slopes64[:, None] * iota[None, :]
    qv = -kv
    for nm, v in [("kaug", kv), ("qaug", qv)]:
        parts = hml(v)
        rows = np.zeros((NG * 3, H * G), np.float64)
        for g in range(NG):
            for k in range(3):
                rows[3 * g + k] = parts[k][:, g * G:(g + 1) * G].reshape(-1)
        c[nm] = rows.astype(ml_dtypes.bfloat16)
    return c


def permute_w_rope(w):
    cols = []
    for g in range(4):
        for h in range(4 * g, 4 * g + 4):
            cols.extend(range(h * 64, h * 64 + 32))
        for h in range(4 * g, 4 * g + 4):
            cols.extend(range(h * 64 + 32, h * 64 + 64))
    return np.ascontiguousarray(w[:, cols])


def split_qkv(wqkv, bqkv):
    Wr = wqkv.reshape(1024, H, 3, DH)
    br = bqkv.reshape(H, 3, DH)
    out = {}
    for i, r in enumerate("qkv"):
        out["W" + r] = np.ascontiguousarray(Wr[:, :, i, :].reshape(1024, 1024))
        out["b" + r] = np.ascontiguousarray(br[:, i, :].reshape(1024))
    return out



"""Stage-1 program builder. Phase A (attn block + LN2) first.

Layout: feature-major activations [d on partitions (8 tiles of 128), tokens free].
"""
import numpy as np
import ml_dtypes
from contextlib import ExitStack
import concourse.bass as bass
import concourse.mybir as mybir

AP = bass.AP
C0 = 20.0


class Ctx:
    def __init__(self, nc):
        self.nc = nc
        self.p = Prog()
        self.es = ExitStack()
        self.t = {}
        p = self.p
        self.sV = p.sem("sV"); self.sS = p.sem("sS")
        self.sT = p.sem("sT"); self.sG = p.sem("sG")
        NPOOL = 12
        self.dpool = {q: [p.sem(f"d_{q}_{i}") for i in range(NPOOL)]
                      for q in ("sync", "gpsimd", "scalar")}
        self.dcnt = {q: 0 for q in self.dpool}
        self.duse = {}
        self.grd = {}
        self.outs = []

    def sb(self, name, shape, dt):
        t = self.es.enter_context(self.nc.sbuf_tensor(name, shape, dt))
        self.t[name] = t
        return t

    def psum(self, name, shape=(128, 512), dt=F32):
        t = self.es.enter_context(self.nc.psum_tensor(name, list(shape), dt))
        self.t[name] = t
        return t

    def dma(self, queue, out, in_, wait=()):
        pool = self.dpool[queue]
        sem = pool[self.dcnt[queue] % len(pool)]
        self.dcnt[queue] += 1
        prev = self.duse.get(sem.name, 0)
        w = [t for t in wait if t is not None] + ([(sem, 16 * prev)] if prev else [])
        self.duse[sem.name] = prev + 1
        self.p.add(queue, lambda e, o=out, i=in_: e.dma_start(out=o, in_=i),
                   wait=w, inc=[(sem, 16)], is_dma=True)
        return (sem, 16 * (prev + 1))

    def join(self, engine, tokens):
        es_ = {"vector": self.sV, "scalar": self.sS, "tensor": self.sT,
               "gpsimd": self.sG}[engine]
        best = {}
        for t in tokens:
            if t is None:
                continue
            s_, v_ = t
            if v_ > best.get(s_.name, (None, -1))[1]:
                best[s_.name] = (s_, v_)
        v = self.p.add(engine, None, wait=list(best.values()), inc=[(es_, 1)])[0]
        return (es_, v)


def wtile_ap(Wdram, f, ncols=128, nkt=8, col0=0):
    ld = Wdram.shape[1]
    return AP(Wdram, col0 + f * ncols, [[ld, 128], [128 * ld, nkt], [1, ncols]])


def build_stage1(n_groups=NG, upto="A"):
    nc = bass.Bass("TRN2", target_bir_lowering=False, detect_race_conditions=False)
    c = Ctx(nc)
    p = c.p
    grd = c.grd

    def din(name, shape, dt=F32):
        return nc.dram_tensor(name, shape, dt, kind="ExternalInput").ap()
    def dout(name, shape, dt=F32):
        return nc.dram_tensor(name, shape, dt, kind="ExternalOutput").ap()

    HC_ALL = host_consts()
    NEEDED = ["SUge", "Sh4", "ones_col_bf", "cos_pk", "sin_pk"]
    DRAMONLY = ["ones16", "kaug", "qaug"]
    HC = {n: HC_ALL[n] for n in NEEDED + DRAMONLY}
    XT = din("xT", [D, T])
    WD = {n: din(n, [D, D]) for n in ["Wk", "Wq", "Wv", "Wo"]}
    BCOLS = {n: din(n, [128, 8]) for n in ["bk", "bq", "bv", "bo"]}
    BVREP = din("bvrep", [128, D])
    CONSTS = {n: din("c_" + n, list(v.shape),
                     BF16 if v.dtype == ml_dtypes.bfloat16 else F32)
              for n, v in HC.items()}
    X1T = dout("x1T", [D, T])
    HIDT_O = dout("hidT", [D, T], BF16)
    W1 = {n: din(n, shp) for n, shp in [
        ("wq1p", [D, D]), ("wk1p", [D, D]), ("wv1", [D, D]), ("wo1", [D, D]),
        ("wg1", [D, FFN]), ("wu1", [D, FFN]), ("wd1", [FFN, D]),
        ("prior_w", [D, 2 * D])]}
    GCOLS = {n: din(n, [128, 8]) for n in ["g11", "g21", "gpn"]}
    PROCT = dout("procT", [D, T])
    DST = dout("dst", [1, T]); DCH = dout("dch", [1, T])
    DSA = dout("dsa", [1, 1]); DSB = dout("dsb", [1, 1])
    DBG_QT = dout("dbg_qt", [70, H * G], BF16)
    DBG_KT = dout("dbg_kt", [70, H * G], BF16)
    DBG_VG = dout("dbg_vg", [128, 4 * H * 65], BF16)
    DBG_OT = dout("dbg_ot", [D, G], BF16)
    DBG_PS = dout("dbg_ps", [128, G])
    DBG_PS2 = dout("dbg_ps2", [128, G])

    wsc = {n: nc.dram_tensor("sc_" + n, [D, D], BF16) for n in WD}
    for n, shp in [("Wq1", [D, D]), ("Wk1", [D, D]), ("Wv1", [D, D]), ("Wo1", [D, D]),
                   ("Wg1", [D, FFN]), ("Wu1", [D, FFN]), ("Wd1", [FFN, D]),
                   ("PW", [D, 2 * D])]:
        wsc[n] = nc.dram_tensor("sc_" + n, shp, BF16)
    KT1d = nc.dram_tensor("sc_KT1", [64, H * T], BF16)
    V1d = nc.dram_tensor("sc_V1", [128, H * 32 * 65], BF16)
    KTab = nc.dram_tensor("sc_KTab", [70, H * T], BF16)
    Vab = nc.dram_tensor("sc_Vab", [128, H * 32 * 65], BF16)
    BNC = nc.dram_tensor("sc_bnc", [16, G], BF16)

    with c.es:
        cst = {n: c.sb("k_" + n, list(HC[n].shape),
                       BF16 if HC[n].dtype == ml_dtypes.bfloat16 else F32)
               for n in NEEDED}
        onesr = c.sb("onesr", [1, 128], BF16)
        xg = [c.sb(f"xg{i}", [128, G], F32) for i in range(8)]
        xnbf = [c.sb(f"xnbf{i}", [128, G], BF16) for i in range(8)]
        sqb = [c.sb(f"sqb{i}", [128, G], BF16) for i in range(2)]
        tmpf = [c.sb(f"tmpf{i}", [128, G], F32) for i in range(2)]
        rows = {n: c.sb("r_" + n, [1, G], F32) for n in
                ["m", "var", "rstd", "recip", "sc1"]}
        rows["e2"] = rows["sc1"]
        rows_bf = {n: c.sb("rb_" + n, [1, G], BF16) for n in ["m", "rstd", "recip"]}
        eps1 = c.sb("eps1", [1, 1], F32)
        eps2 = c.sb("eps2", [1, 1], F32)
        dsa = c.sb("dsa_s", [1, 1], F32)
        dsb = c.sb("dsb_s", [1, 1], F32)
        h1bf = [c.sb(f"h1bf{i}", [128, G], BF16) for i in range(8)]
        sgbf = c.sb("sgbf", [128, G], BF16)
        swig = c.sb("swig", [128, 32 * G], BF16)
        mubf = [c.sb(f"mubf{i}", [128, G], BF16) for i in range(8)]
        resbf = [c.sb(f"resbf{i}", [128, G], BF16) for i in range(8)]
        cosrep = c.sb("cosrep", [128, G], F32)
        sinrep = c.sb("sinrep", [128, G], F32)
        rtmp = c.sb("rtmp", [128, G], F32)
        ebuf = c.sb("ebuf", [128, G], F32)
        c0t = c.sb("c0t", [128, 1], F32)
        bcol = {n: c.sb("bc_" + n, [128, 8], F32) for n in BCOLS}
        for n in ["g11", "g21", "gpn"]:
            bcol[n] = c.sb("bc_" + n, [128, 8], F32)
        wr = [c.sb(f"wr{i}", [128, 2048], BF16) for i in range(4)]
        QTa = c.sb("QTa", [70, H * G], BF16)
        KTs = c.sb("KTs", [70, H * G], BF16)
        Vg = c.sb("Vg", [128, 4 * H * 65], BF16)
        expS = [c.sb(f"expS{i}", [128, G], BF16) for i in range(4)]
        recb = c.sb("recb", [64, G], F32)
        OTbf = [c.sb(f"OTbf{i}", [128, G], BF16) for i in range(8)]
        hidg = [c.sb(f"hidg{i}", [128, G], BF16) for i in range(8)]
        bvb = c.sb("bvb", [128, 1024], F32)
        pb = [c.psum(f"pb{i}") for i in range(8)]

        sV, sS, sT, sG = c.sV, c.sS, c.sT, c.sG

        toks = []
        for n, t in cst.items():
            toks.append(c.dma("sync", t[:], CONSTS[n]))
        for n in BCOLS:
            toks.append(c.dma("sync", bcol[n][:], BCOLS[n]))
        for n in ["g11", "g21", "gpn"]:
            toks.append(c.dma("sync", bcol[n][:], GCOLS[n]))
        toks.append(c.dma("sync", bvb[:], BVREP))
        wtoks = []
        srcmap = dict(WD)
        srcmap.update({"Wq1": W1["wq1p"], "Wk1": W1["wk1p"], "Wv1": W1["wv1"],
                       "Wo1": W1["wo1"], "Wg1": W1["wg1"], "Wu1": W1["wu1"],
                       "Wd1": W1["wd1"], "PW": W1["prior_w"]})
        for n in wsc:
            wtoks.append(c.dma("gpsimd", wsc[n].ap(), srcmap[n]))
        v_wcast = c.join("gpsimd", wtoks)

        p.add("vector", lambda e: e.memset(onesr[:], 1.0))
        p.add("vector", lambda e: e.memset(eps1[:], EPS_LN))
        p.add("vector", lambda e: e.memset(eps2[:], EPS_RMS))
        p.add("vector", lambda e: e.memset(dsa[:], 0.0))
        p.add("vector", lambda e: e.memset(dsb[:], 0.0))
        p.add("vector", lambda e: e.memset(c0t[:], -C0))
        p.add("vector", lambda e: e.memset(
            AP(Vg, 64, [[4 * H * 65, 128], [65, 4 * H], [1, 1]]), 1.0))
        for k in range(3):
            toks.append(c.dma("sync",
                  AP(KTs, (67 + k) * (H * G), [[H * G, 1], [1, H * G]]),
                  AP(CONSTS["ones16"].tensor, 0, [[H * G, 1], [1, H * G]])))
            toks.append(c.dma("sync",
                  AP(QTa, (64 + k) * (H * G), [[H * G, 1], [1, H * G]]),
                  AP(CONSTS["ones16"].tensor, 0, [[H * G, 1], [1, H * G]])))
        v_cl = c.join("vector", toks)
        v_init = p.add("vector", None, wait=[v_cl], inc=[(sV, 1)])[0]
        p.add("vector", lambda e: e.tensor_scalar(
            bcol["bq"][:], bcol["bq"][:], 0.125, None, ALU.mult), wait=[v_cl])
        pass

        def ln_grp(src_aps, dst_aps, rms=False, gains=None, eps_v=None,
                   wait_first=(), wait_per_kt=None):
            vc = []
            for kt in range(8):
                w = list(wait_first if kt == 0 else ())
                if wait_per_kt:
                    w = w + [wait_per_kt[kt]]
                vc.append(p.add("vector", lambda e, kt=kt: e.tensor_copy(
                    xnbf[kt][:], src_aps[kt]), wait=w, inc=[(sV, 1)])[0])
            if not rms:
                for kt in range(8):
                    p.add("tensor", lambda e, kt=kt: e.matmul(
                        pb[0][0:1, :], cst["ones_col_bf"][:], xnbf[kt][:],
                        start=(kt == 0), stop=(kt == 7)),
                        wait=([(sV, vc[kt]), grd.get("bank0")]
                              if kt == 0 else [(sV, vc[kt])]),
                        inc=([(sT, 1)] if kt == 7 else ()))
                t_sum = sT.v
            sqv = []
            sqmm = {}
            for kt in range(8):
                sqv.append(p.add("vector", lambda e, kt=kt: e.tensor_tensor(
                    sqb[kt % 2][:], xnbf[kt][:], xnbf[kt][:], ALU.mult),
                    wait=[(sV, vc[kt])] + ([(sT, sqmm[kt - 2])] if kt >= 2 else []),
                    inc=[(sV, 1)])[0])
                sqmm[kt] = p.add("tensor", lambda e, kt=kt: e.matmul(
                    pb[1][0:1, :], cst["ones_col_bf"][:], sqb[kt % 2][:],
                    start=(kt == 0), stop=(kt == 7)),
                    wait=([(sV, sqv[kt]), grd.get("bank1"), grd.get("bank0")]
                          if kt == 0 else [(sV, sqv[kt])]),
                    inc=[(sT, 1)])[0]
            t_sq = sT.v
            grd["xnbf_cons"] = (sT, t_sq)
            if rms:
                v_v = p.add("vector", lambda e: e.tensor_scalar(
                    rows["var"][:], pb[1][0:1, :], 1.0 / D, None, ALU.mult),
                    wait=[(sT, t_sq)], inc=[(sV, 1)])[0]
            else:
                p.add("vector", lambda e: e.tensor_scalar(
                    rows["m"][:], pb[0][0:1, :], 1.0 / D, None, ALU.mult),
                    wait=[(sT, t_sum), grd.get("rowm_free")])
                p.add("vector", lambda e: e.tensor_scalar(
                    rows["e2"][:], pb[1][0:1, :], 1.0 / D, None, ALU.mult),
                    wait=[(sT, t_sq)])
                p.add("vector", lambda e: e.tensor_tensor(
                    rows["var"][:], rows["m"][:], rows["m"][:], ALU.mult))
                v_v = p.add("vector", lambda e: e.tensor_tensor(
                    rows["var"][:], rows["e2"][:], rows["var"][:], ALU.subtract),
                    inc=[(sV, 1)])[0]
            s1 = p.add("scalar", lambda e, ev=eps_v: e.activation(
                rows["sc1"][:], rows["var"][:], AF.Ln, bias=ev, scale=1.0),
                wait=[(sV, v_v)], inc=[(sS, 1)])[0]
            s2 = p.add("scalar", lambda e: e.activation(
                rows["rstd"][:], rows["sc1"][:], AF.Exp, bias=0.0, scale=-0.5),
                wait=[(sS, s1)], inc=[(sS, 1)])[0]
            v_rb = p.add("vector", lambda e: e.tensor_copy(
                rows_bf["rstd"][:], rows["rstd"][:]),
                wait=[(sS, s2), grd.get("lnrows")],
                inc=[(sV, 1)])[0]
            if not rms:
                v_mb = p.add("vector", lambda e: e.tensor_copy(
                    rows_bf["m"][:], rows["m"][:]), inc=[(sV, 1)])[0]
                p.add("tensor", lambda e: e.matmul(
                    pb[0][:], onesr[:], rows_bf["m"][:], start=True, stop=True),
                    wait=[(sV, v_mb)])
            p.add("tensor", lambda e: e.matmul(
                pb[1][:], onesr[:], rows_bf["rstd"][:], start=True, stop=True),
                wait=[(sV, v_rb)], inc=[(sT, 1)])
            t_bc = sT.v
            grd["lnrows"] = (sT, t_bc)
            out_v = []
            for kt in range(8):
                if rms:
                    out_v.append(p.add("vector", lambda e, kt=kt:
                        e.scalar_tensor_tensor(dst_aps[kt], src_aps[kt],
                                               bcol[gains][:, kt:kt + 1], pb[1][:],
                                               ALU.mult, ALU.mult),
                        wait=([(sT, t_bc)] if kt == 0 else ()), inc=[(sV, 1)])[0])
                else:
                    p.add("vector", lambda e, kt=kt: e.tensor_tensor(
                        tmpf[0][:], src_aps[kt], pb[0][:], ALU.subtract),
                        wait=([(sT, t_bc)] if kt == 0 else ()))
                    out_v.append(p.add("vector", lambda e, kt=kt: e.tensor_tensor(
                        dst_aps[kt], tmpf[0][:], pb[1][:], ALU.mult),
                        inc=[(sV, 1)])[0])
            grd["bank0"] = (sV, sV.v)
            grd["bank1"] = (sV, sV.v)
            return out_v

        def emit_attn(g, KTsc_d, Vsc_d, QTbuf, nrows, bias_tile, d_kst, d_vst,
                      qk_extra_waits, tag):
            LAG = 3
            for hs in range(4):
                h0 = hs * 4
                av_started = [False] * 4
                pending = []

                def emit_av(rec):
                    (hl, kt, ppe, s_exp, rv, lwv, first, last, closes,
                     strip_first) = rec
                    p.add("tensor", lambda e, hl=hl, kt=kt, ppe=ppe,
                          first=first, last=last, rv=rv: e.matmul(
                              pb[hl][0:65, :],
                              AP(wr[rv], (hl * 4 + kt) * 65,
                                 [[2048, 128], [1, 65]]),
                              expS[ppe][:],
                              start=first, stop=last),
                          wait=[(sS, s_exp)] + ([lwv] if strip_first else [])
                          + ([grd.get(f"av{hl}"), grd.get(f"bank{hl}")]
                             if first else []),
                          inc=[(sT, 1)])
                    grd[f"expS{ppe}"] = (sT, sT.v)
                    if closes is not None:
                        grd[f"wr{closes}"] = (sT, sT.v)

                tileno = 0
                for ks in range(g + 1):
                    rk, lwk = wload(AP(KTsc_d, h0 * T + ks * G,
                                       [[H * T, nrows], [T, 4], [1, G]]),
                                    extra=[d_kst], rows=nrows)
                    rv, lwv = wload(AP(Vsc_d, h0 * 32 * 65 + 4 * ks * 65,
                                       [[H * 32 * 65, 128], [32 * 65, 4], [1, 260]]),
                                    extra=[d_vst], cols=1040)
                    crossing = (ks == g)
                    strip_new = True
                    for hl in range(4):
                        for kt in range(4):
                            bqk = 4 + tileno % 4
                            wq_ = [grd.get(f"bank{bqk}")]
                            if strip_new:
                                wq_ = wq_ + [lwk]
                            if hs == 0 and ks == 0 and tileno == 0:
                                wq_ = wq_ + qk_extra_waits
                            if crossing:
                                p.add("tensor", lambda e, kt=kt, bqk=bqk: e.matmul(
                                    pb[bqk][:], cst["SUge"][:],
                                    cst["Sh4"][:, kt * G:(kt + 1) * G],
                                    start=True, stop=False), wait=wq_)
                            p.add("tensor", lambda e, hl=hl, kt=kt, bqk=bqk, h0=h0,
                                  nrows=nrows, crossing=crossing, rk=rk: e.matmul(
                                      pb[bqk][:],
                                      wr[rk][0:nrows,
                                             (hl * 4 + kt) * 128:(hl * 4 + kt + 1) * 128],
                                      QTbuf[0:nrows, (h0 + hl) * G:(h0 + hl + 1) * G],
                                      start=(not crossing), stop=True),
                                  wait=(wq_ if not crossing else ()),
                                  inc=[(sT, 1)])
                            t_qk = sT.v
                            ppe = tileno % 4
                            s_exp = p.add("scalar", lambda e, ppe=ppe, bqk=bqk,
                                          bias_tile=bias_tile:
                                          e.activation(expS[ppe][:], pb[bqk][:],
                                                       AF.Exp, bias=bias_tile[:],
                                                       scale=1.0),
                                          wait=[(sT, t_qk),
                                                grd.get(f"expS{ppe}")]
                                          + ([(sV, v_init)] if tileno == 0 else []),
                                          inc=[(sS, 1)])[0]
                            grd[f"bank{bqk}"] = (sS, s_exp)
                            first = not av_started[hl]
                            av_started[hl] = True
                            last = (ks == g and kt == 3)
                            closes = rv if (hl == 3 and kt == 3) else None
                            pending.append((hl, kt, ppe, s_exp, rv, lwv,
                                            first, last, closes, strip_new))
                            strip_new = False
                            tileno += 1
                            if len(pending) > LAG:
                                emit_av(pending.pop(0))
                    grd[f"wr{rk}"] = (sT, sT.v)
                while pending:
                    emit_av(pending.pop(0))
                t_av_done = sT.v
                for hl in range(4):
                    h = h0 + hl
                    s3 = p.add("scalar", lambda e, hl=hl: e.activation(
                        rows["recip"][:], pb[hl][64:65, :], AF.Ln,
                        bias=0.0, scale=1.0),
                        wait=[(sT, t_av_done), grd.get("reciprow")],
                        inc=[(sS, 1)])[0]
                    v4 = p.add("scalar", lambda e: e.activation(
                        rows_bf["recip"][:], rows["recip"][:], AF.Exp,
                        bias=0.0, scale=-1.0), inc=[(sS, 1)])[0]
                    bb = 6 + (hl % 2)
                    t_bc = p.add("tensor", lambda e, bb=bb: e.matmul(
                        pb[bb][0:64, :], onesr[0:1, 0:64], rows_bf["recip"][0:1, :],
                        start=True, stop=True),
                        wait=[(sS, v4), grd.get(f"bank{bb}")],
                        inc=[(sT, 1)])[0]
                    grd["reciprow"] = (sT, t_bc)
                    v4b = p.add("vector", lambda e, bb=bb: e.tensor_copy(
                        recb[0:64, :], pb[bb][0:64, :]),
                        wait=[(sT, t_bc)], inc=[(sV, 1)])[0]
                    grd[f"bank{bb}"] = (sV, v4b)
                    v5 = p.add("vector", lambda e, h=h, hl=hl: e.tensor_tensor(
                        OTbf[h // 2][(h % 2) * 64:(h % 2) * 64 + 64, :],
                        pb[hl][0:64, :], recb[0:64, :], ALU.mult),
                        inc=[(sV, 1)])[0]
                    grd[f"av{hl}"] = (sV, v5)
                    grd[f"bank{hl}"] = (sV, v5)
            grd["qta_aug_free"] = (sT, sT.v)
            return sV.v

        wstate = {"k": 0}

        def wload(src_ap, extra=(), rows=128, cols=2048):
            rix = wstate["k"] % 4
            wstate["k"] += 1
            q = "sync" if rix % 2 == 0 else "gpsimd"
            tok = c.dma(q, wr[rix][0:rows, 0:cols], src_ap,
                        wait=[v_wcast, grd.get(f"wr{rix}")] + [t for t in extra if t])
            return rix, tok

        def wpair_ap(Wd, fp):
            ld = Wd.shape[1]
            return AP(Wd, fp * 256, [[ld, 128], [128 * ld, 8], [1, 256]])

        for g in range(n_groups):
            gl = g * G
            xw = []
            for kt in range(8):
                w = [grd.get("xg"), grd.get("xg_dma")] if kt == 0 else ()
                xw.append(c.dma("sync", xg[kt][:],
                                XT[kt * 128:(kt + 1) * 128, gl:gl + G], wait=w))
            vln = ln_grp([xg[i][:] for i in range(8)],
                         [xnbf[i][:] for i in range(8)],
                         rms=False, eps_v=eps1[:],
                         wait_first=[grd.get("xnbf_cons")],
                         wait_per_kt=[xw[i] for i in range(8)])
            v_ln = vln[-1]

            for ri, (role, Wn, bcn, dst_, scl) in enumerate(
                    [("k", "Wk", "bk", KTs, 1.0), ("q", "Wq", "bq", QTa, 0.125)]):
                for fp in range(4):
                    rix, lw = wload(wpair_ap(wsc[Wn], fp))
                    for sub in range(2):
                        f = 2 * fp + sub
                        b = [2, 3, 6, 7][(ri * 8 + f) % 4]
                        for i in range(8):
                            p.add("tensor", lambda e, i=i, rix=rix, sub=sub, b=b:
                                  e.matmul(
                                      pb[b][:], wr[rix][:, i * 256 + sub * 128:
                                                        i * 256 + sub * 128 + 128],
                                      xnbf[i][:],
                                      start=(i == 0), stop=(i == 7)),
                                wait=([lw, (sV, v_ln), grd.get(f"bank{b}")]
                                      if i == 0 else ()),
                                inc=([(sT, 1)] if i == 7 else ()))
                        t_mm = sT.v
                        for half in range(2):
                            hh = 2 * f + half
                            p.add("scalar", lambda e, f=f, half=half, hh=hh, dst_=dst_,
                                  bcn=bcn, scl=scl, b=b: e.activation(
                                      dst_[0:64, hh * G:(hh + 1) * G],
                                      pb[b][half * 64:half * 64 + 64, :], AF.Identity,
                                      bias=bcol[bcn][half * 64:half * 64 + 64, f:f + 1],
                                      scale=scl),
                                  wait=[(sT, t_mm), grd.get("kts_free")],
                                  inc=[(sS, 1)])
                        grd[f"bank{b}"] = (sS, sS.v)
                    grd[f"wr{rix}"] = (sT, t_mm)
                if role == "k":
                    s_k_evac = sS.v
                else:
                    s_q_evac = sS.v

            for half in range(2):
                vch = []
                for q_ in range(2):
                    vch.append(wload(AP(wsc["Wv"],
                                        half * 512 + q_ * 4 * 128 * D,
                                        [[D, 128], [128 * D, 4], [1, 512]])))
                for tt in range(4):
                    b = [2, 3, 6, 7][(half * 4 + tt) % 4]
                    for i in range(8):
                        rix, lw = vch[i // 4]
                        p.add("tensor", lambda e, i=i, tt=tt, rix=rix, b=b: e.matmul(
                            pb[b][:], xnbf[i][:, tt * 128:(tt + 1) * 128],
                            wr[rix][:, (i % 4) * 512:(i % 4) * 512 + 512],
                            start=(i == 0), stop=(i == 7)),
                            wait=([vch[0][1], vch[1][1], (sV, v_ln),
                                   grd.get(f"bank{b}")]
                                  if i == 0 else ()),
                            inc=([(sT, 1)] if i == 7 else ()))
                    t_v = sT.v
                    v_ev = p.add("vector", lambda e, tt=tt, half=half, b=b:
                                 e.tensor_tensor(
                        AP(Vg, (half * 8 * 4 + tt) * 65,
                           [[4 * H * 65, 128], [4 * 65, 8], [1, 64]]),
                        pb[b][:], bvb[:, half * 512:(half + 1) * 512], ALU.add),
                        wait=[(sT, t_v), (sV, v_init),
                              grd.get("vg_free")], inc=[(sV, 1)])[0]
                    grd[f"bank{b}"] = (sV, v_ev)
                for rix, lw in vch:
                    grd[f"wr{rix}"] = (sT, sT.v)
            v_v_evac = sV.v

            ktoks, qtoks = [], []
            for k in range(3):
                ktoks.append(c.dma("sync",
                    AP(KTs, (64 + k) * (H * G), [[H * G, 1], [1, H * G]]),
                    AP(CONSTS["kaug"].tensor, (3 * g + k) * (H * G),
                       [[H * G, 1], [1, H * G]]),
                    wait=[grd.get("kts_free")]))
                qtoks.append(c.dma("sync",
                    AP(QTa, (67 + k) * (H * G), [[H * G, 1], [1, H * G]]),
                    AP(CONSTS["qaug"].tensor, (3 * g + k) * (H * G),
                       [[H * G, 1], [1, H * G]]),
                    wait=[grd.get("qta_aug_free")]))
            d_kaug = c.join("gpsimd", ktoks)
            d_qaug = c.join("gpsimd", qtoks)

            d_kst = c.dma("gpsimd",
                          AP(KTab, gl, [[H * T, 70], [T, H], [1, G]]),
                          AP(KTs, 0, [[H * G, 70], [G, H], [1, G]]),
                          wait=[(sS, s_k_evac), d_kaug, v_cl])
            d_vst = c.dma("gpsimd",
                          AP(Vab, 4 * g * 65,
                             [[H * 32 * 65, 128], [32 * 65, H], [1, 260]]),
                          AP(Vg, 0, [[4 * H * 65, 128], [260, H], [1, 260]]),
                          wait=[(sV, v_v_evac)])
            grd["vg_free"] = d_vst
            grd["kts_free"] = d_kst


            v_attn_done = emit_attn(
                g, KTab, Vab, QTa, 70, c0t, d_kst, d_vst,
                [(sS, s_q_evac), d_qaug, v_cl], "ab")

            if g == n_groups - 1:
                c.outs.append(c.dma("gpsimd", DBG_QT, QTa[:], wait=[d_qaug, (sS, s_q_evac), v_cl]))
                c.outs.append(c.dma("gpsimd", DBG_KT, KTs[:], wait=[d_kaug, (sS, s_k_evac), v_cl]))
                c.outs.append(c.dma("gpsimd", DBG_VG, Vg[:], wait=[(sV, v_v_evac)]))
                for i in range(8):
                    c.outs.append(c.dma("gpsimd", DBG_OT[i * 128:(i + 1) * 128, :],
                          OTbf[i][:], wait=[(sV, v_attn_done)]))
            x1v = []
            for fp in range(4):
                rix, lw = wload(wpair_ap(wsc["Wo"], fp))
                for sub in range(2):
                    f = 2 * fp + sub
                    b = [2, 3, 6, 7][f % 4]
                    for i in range(8):
                        p.add("tensor", lambda e, i=i, rix=rix, sub=sub, b=b: e.matmul(
                            pb[b][:], wr[rix][:, i * 256 + sub * 128:
                                              i * 256 + sub * 128 + 128],
                            OTbf[i][:],
                            start=(i == 0), stop=(i == 7)),
                            wait=([lw, (sV, v_attn_done), grd.get(f"bank{b}")]
                                  if i == 0 else ()),
                            inc=([(sT, 1)] if i == 7 else ()))
                    t_mm = sT.v
                    s_ev = p.add("scalar", lambda e, f=f, b=b: e.activation(
                        tmpf[1][:], pb[b][:], AF.Identity,
                        bias=bcol["bo"][:, f:f + 1], scale=1.0),
                        wait=[(sT, t_mm), grd.get("tmpf1")], inc=[(sS, 1)])[0]
                    grd[f"bank{b}"] = (sS, s_ev)
                    v_x1 = p.add("vector", lambda e, f=f: e.tensor_tensor(
                        xg[f][:], tmpf[1][:], xg[f][:], ALU.add),
                        wait=[(sS, s_ev), grd.get("x1gv"),
                              grd.get("x1gd")], inc=[(sV, 1)])[0]
                    grd["pb2w"] = (sV, v_x1)
                    grd["tmpf1"] = (sV, v_x1)
                    x1v.append(v_x1)
                    d_x1o = c.dma("gpsimd",
                                  X1T[f * 128:(f + 1) * 128, gl:gl + G], xg[f][:],
                                  wait=[(sV, v_x1)])
                    c.outs.append(d_x1o)
                    grd[f"x1od{f}"] = d_x1o
                grd[f"wr{rix}"] = (sT, t_mm)

            vln2 = ln_grp([xg[i][:] for i in range(8)],
                          [hidg[i][:] for i in range(8)],
                          rms=False, eps_v=eps1[:])
            grd["xg"] = (sV, vln2[-1])
            grd["x1gv"] = (sV, vln2[-1])
            grd["x1gd"] = None

            pq = 32 * (g // 2)
            co = (g % 2) * G
            rv = []
            for q4 in range(4):
                p.add("vector", lambda e, q4=q4, pq=pq, co=co: e.tensor_copy(
                    cosrep[q4 * 32:(q4 + 1) * 32, :],
                    cst["cos_pk"][pq:pq + 32, co:co + G][0:32, :]),
                    wait=([grd.get("trig")] if q4 == 0 else ()))
                rv.append(p.add("vector", lambda e, q4=q4, pq=pq, co=co: e.tensor_copy(
                    sinrep[q4 * 32:(q4 + 1) * 32, :],
                    cst["sin_pk"][pq:pq + 32, co:co + G][0:32, :]),
                    inc=[(sV, 1)])[0])
            v_trig = rv[-1]

            vr1 = ln_grp([hidg[i][:] for i in range(8)],
                         [h1bf[i][:] for i in range(8)],
                         rms=True, gains="g11", eps_v=eps2[:],
                         wait_first=[grd.get("h1bf_free")])
            v_r1 = vr1[-1]

            for ri, (role, Wn, dst_, scl) in enumerate(
                    [("k", "Wk1", KTs, 1.0), ("q", "Wq1", QTa, 0.125)]):
                for j4 in range(4):
                    pA, pB = ((2, 3) if (ri * 4 + j4) % 2 == 0 else (6, 7))
                    rix, lw = wload(wpair_ap(wsc[Wn], j4))
                    for i in range(8):
                        p.add("tensor", lambda e, i=i, rix=rix, pA=pA: e.matmul(
                            pb[pA][:], wr[rix][:, i * 256:i * 256 + 128], h1bf[i][:],
                            start=(i == 0), stop=(i == 7)),
                            wait=([lw, (sV, v_r1), grd.get(f"bank{pA}")]
                                  if i == 0 else ()),
                            inc=([(sT, 1)] if i == 7 else ()))
                    tA = sT.v
                    for i in range(8):
                        p.add("tensor", lambda e, i=i, rix=rix, pB=pB: e.matmul(
                            pb[pB][:], wr[rix][:, i * 256 + 128:i * 256 + 256],
                            h1bf[i][:],
                            start=(i == 0), stop=(i == 7)),
                            wait=([grd.get(f"bank{pB}")] if i == 0 else ()),
                            inc=([(sT, 1)] if i == 7 else ()))
                    tB = sT.v
                    grd[f"wr{rix}"] = (sT, tB)
                    p.add("vector", lambda e, pA=pA: e.tensor_tensor(
                        tmpf[0][:], pb[pA][:], cosrep[:], ALU.mult),
                        wait=[(sT, tB), (sV, v_trig), grd.get("ropebuf")])
                    p.add("vector", lambda e, pB=pB: e.tensor_tensor(
                        tmpf[1][:], pb[pB][:], sinrep[:], ALU.mult))
                    p.add("vector", lambda e: e.tensor_tensor(
                        tmpf[0][:], tmpf[0][:], tmpf[1][:], ALU.subtract))
                    p.add("vector", lambda e, pA=pA: e.tensor_tensor(
                        rtmp[:], pb[pA][:], sinrep[:], ALU.mult))
                    p.add("vector", lambda e, pB=pB: e.tensor_tensor(
                        tmpf[1][:], pb[pB][:], cosrep[:], ALU.mult))
                    vrp = p.add("vector", lambda e: e.tensor_tensor(
                        rtmp[:], rtmp[:], tmpf[1][:], ALU.add), inc=[(sV, 1)])[0]
                    grd[f"bank{pA}"] = (sV, vrp)
                    grd[f"bank{pB}"] = (sV, vrp)
                    for m in range(4):
                        hh = 4 * j4 + m
                        p.add("scalar", lambda e, m=m, hh=hh, dst_=dst_, scl=scl:
                              e.activation(dst_[0:32, hh * G:(hh + 1) * G],
                                           tmpf[0][32 * m:32 * m + 32, :],
                                           AF.Identity, bias=0.0, scale=scl),
                              wait=([(sV, vrp), grd.get("kts_free" if dst_ is KTs
                                                        else "qta_aug_free")]
                                    if m == 0 else ()),
                              inc=[(sS, 1)])
                        p.add("scalar", lambda e, m=m, hh=hh, dst_=dst_, scl=scl:
                              e.activation(dst_[32:64, hh * G:(hh + 1) * G],
                                           rtmp[32 * m:32 * m + 32, :],
                                           AF.Identity, bias=0.0, scale=scl),
                              inc=[(sS, 1)])
                    grd["ropebuf"] = (sS, sS.v)
                if role == "k":
                    s_k1 = sS.v
                else:
                    s_q1 = sS.v

            for half in range(2):
                vch = []
                for q_ in range(2):
                    vch.append(wload(AP(wsc["Wv1"],
                                        half * 512 + q_ * 4 * 128 * D,
                                        [[D, 128], [128 * D, 4], [1, 512]])))
                for tt_ in range(4):
                    b = [2, 3, 6, 7][(half * 4 + tt_) % 4]
                    for i in range(8):
                        rix, lw = vch[i // 4]
                        p.add("tensor", lambda e, i=i, tt_=tt_, rix=rix, b=b: e.matmul(
                            pb[b][:], h1bf[i][:, tt_ * 128:(tt_ + 1) * 128],
                            wr[rix][:, (i % 4) * 512:(i % 4) * 512 + 512],
                            start=(i == 0), stop=(i == 7)),
                            wait=([vch[0][1], vch[1][1], (sV, v_r1),
                                   grd.get(f"bank{b}")]
                                  if i == 0 else ()),
                            inc=([(sT, 1)] if i == 7 else ()))
                    t_v = sT.v
                    v_ev = p.add("vector", lambda e, tt_=tt_, half=half, b=b:
                                 e.tensor_copy(
                        AP(Vg, (half * 8 * 4 + tt_) * 65,
                           [[4 * H * 65, 128], [4 * 65, 8], [1, 64]]),
                        pb[b][:]),
                        wait=[(sT, t_v), grd.get("vg_free")], inc=[(sV, 1)])[0]
                    grd[f"bank{b}"] = (sV, v_ev)
                for rix, lw in vch:
                    grd[f"wr{rix}"] = (sT, sT.v)
            v_v1 = sV.v

            d_kst1 = c.dma("gpsimd",
                           AP(KT1d, gl, [[H * T, 64], [T, H], [1, G]]),
                           AP(KTs, 0, [[H * G, 64], [G, H], [1, G]]),
                           wait=[(sS, s_k1)])
            d_vst1 = c.dma("gpsimd",
                           AP(V1d, 4 * g * 65,
                              [[H * 32 * 65, 128], [32 * 65, H], [1, 260]]),
                           AP(Vg, 0, [[4 * H * 65, 128], [260, H], [1, 260]]),
                           wait=[(sV, v_v1)])
            grd["vg_free"] = d_vst1
            grd["kts_free"] = d_kst1

            v_attn1 = emit_attn(g, KT1d, V1d, QTa, 64, c0t, d_kst1, d_vst1,
                                [(sS, s_q1), v_cl], "d1")

            for fp in range(4):
                rix, lw = wload(wpair_ap(wsc["Wo1"], fp))
                for sub in range(2):
                    f = 2 * fp + sub
                    b = [2, 3, 6, 7][f % 4]
                    for i in range(8):
                        p.add("tensor", lambda e, i=i, rix=rix, sub=sub, b=b: e.matmul(
                            pb[b][:], wr[rix][:, i * 256 + sub * 128:
                                              i * 256 + sub * 128 + 128],
                            OTbf[i][:],
                            start=(i == 0), stop=(i == 7)),
                            wait=([lw, (sV, v_attn1), grd.get(f"bank{b}")]
                                  if i == 0 else ()),
                            inc=([(sT, 1)] if i == 7 else ()))
                    t_mm = sT.v
                    s_ev = p.add("scalar", lambda e, f=f, b=b: e.activation(
                        tmpf[1][:], pb[b][:], AF.Identity, bias=0.0, scale=1.0),
                        wait=[(sT, t_mm), grd.get("tmpf1")], inc=[(sS, 1)])[0]
                    grd[f"bank{b}"] = (sS, s_ev)
                    v_x2 = p.add("vector", lambda e, f=f: e.tensor_tensor(
                        xg[f][:], tmpf[1][:], hidg[f][:], ALU.add),
                        wait=[(sS, s_ev), grd.get("xg"), grd.get("x1gd"),
                              grd.get(f"x1od{f}")],
                        inc=[(sV, 1)])[0]
                    grd["pb2"] = (sV, v_x2)
                    grd["tmpf1"] = (sV, v_x2)
                grd[f"wr{rix}"] = (sT, t_mm)
            v_x2_done = sV.v
            grd["xg"] = (sV, v_x2_done)

            vr2 = ln_grp([xg[i][:] for i in range(8)],
                         [h1bf[i][:] for i in range(8)],
                         rms=True, gains="g21", eps_v=eps2[:],
                         wait_first=[grd.get("h1bf_free")])
            v_r2 = vr2[-1]

            for fop in range(16):
                rg, lwg = wload(wpair_ap(wsc["Wg1"], fop))
                ru, lwu = wload(wpair_ap(wsc["Wu1"], fop))
                for sub in range(2):
                    fo = 2 * fop + sub
                    bG = 2 if fo % 2 == 0 else 6
                    bU = 3 if fo % 2 == 0 else 7
                    for i in range(8):
                        p.add("tensor", lambda e, i=i, rg=rg, sub=sub, bG=bG: e.matmul(
                            pb[bG][:], wr[rg][:, i * 256 + sub * 128:
                                             i * 256 + sub * 128 + 128],
                            h1bf[i][:],
                            start=(i == 0), stop=(i == 7)),
                            wait=([lwg, (sV, v_r2), grd.get(f"bank{bG}")]
                                  if i == 0 else ()),
                            inc=([(sT, 1)] if i == 7 else ()))
                    tG = sT.v
                    for i in range(8):
                        p.add("tensor", lambda e, i=i, ru=ru, sub=sub, bU=bU: e.matmul(
                            pb[bU][:], wr[ru][:, i * 256 + sub * 128:
                                             i * 256 + sub * 128 + 128],
                            h1bf[i][:],
                            start=(i == 0), stop=(i == 7)),
                            wait=([lwu, grd.get(f"bank{bU}")] if i == 0 else ()),
                            inc=([(sT, 1)] if i == 7 else ()))
                    tU = sT.v
                    s_si = p.add("scalar", lambda e, bG=bG: e.activation(
                        sgbf[:], pb[bG][:], AF.Silu, bias=0.0, scale=1.0),
                        wait=[(sT, tG), grd.get("sgbf")], inc=[(sS, 1)])[0]
                    v_sw = p.add("vector", lambda e, fo=fo, bU=bU: e.tensor_tensor(
                        swig[:, fo * G:(fo + 1) * G], sgbf[:], pb[bU][:], ALU.mult),
                        wait=[(sS, s_si), (sT, tU), grd.get("swig_free")],
                        inc=[(sV, 1)])[0]
                    grd["sgbf"] = (sV, v_sw)
                    grd[f"bank{bG}"] = (sS, s_si)
                    grd[f"bank{bU}"] = (sV, v_sw)
                grd[f"wr{rg}"] = (sT, tG)
                grd[f"wr{ru}"] = (sT, tU)
            v_swig = sV.v

            for f in range(8):
                wch = []
                for q_ in range(2):
                    wch.append(wload(AP(wsc["Wd1"],
                                        f * 128 + q_ * 16 * 128 * D,
                                        [[D, 128], [128 * D, 16], [1, 128]])))
                b = [2, 3, 6, 7][f % 4]
                for i in range(32):
                    rix, lw = wch[i // 16]
                    p.add("tensor", lambda e, i=i, rix=rix, b=b: e.matmul(
                        pb[b][:], wr[rix][:, (i % 16) * 128:(i % 16) * 128 + 128],
                        swig[:, i * G:(i + 1) * G],
                        start=(i == 0), stop=(i == 31)),
                        wait=([wch[0][1], wch[1][1], (sV, v_swig),
                               grd.get(f"bank{b}")]
                              if i == 0 else ()),
                        inc=([(sT, 1)] if i == 31 else ()))
                t_mm = sT.v
                for rix, lw in wch:
                    grd[f"wr{rix}"] = (sT, t_mm)
                s_ev = p.add("scalar", lambda e, b=b: e.activation(
                    tmpf[1][:], pb[b][:], AF.Identity, bias=0.0, scale=1.0),
                    wait=[(sT, t_mm), grd.get("tmpf1")], inc=[(sS, 1)])[0]
                grd[f"bank{b}"] = (sS, s_ev)
                v_pr = p.add("vector", lambda e, f=f: e.tensor_tensor(
                    xg[f][:], tmpf[1][:], xg[f][:], ALU.add),
                    wait=[(sS, s_ev)], inc=[(sV, 1)])[0]
                grd["pb2"] = (sV, v_pr)
                grd["tmpf1"] = (sV, v_pr)
                c.outs.append(c.dma("gpsimd",
                    PROCT[f * 128:(f + 1) * 128, gl:gl + G], xg[f][:],
                    wait=[(sV, v_pr)]))
            grd["swig_free"] = (sT, sT.v)
            v_proc = sV.v
            d_proc = c.join("gpsimd", c.outs[-8:])
            grd["xg_dma"] = d_proc

            vph = ln_grp([hidg[i][:] for i in range(8)],
                         [h1bf[i][:] for i in range(8)],
                         rms=True, gains="gpn", eps_v=eps2[:],
                         wait_first=[grd.get("h1bf_free")])
            v_ph = vph[-1]
            for fp in range(4):
                rix, lw = wload(wpair_ap(wsc["PW"], fp))
                for sub in range(2):
                    f = 2 * fp + sub
                    b = [2, 3, 6, 7][f % 4]
                    for i in range(8):
                        p.add("tensor", lambda e, i=i, rix=rix, sub=sub, b=b: e.matmul(
                            pb[b][:], wr[rix][:, i * 256 + sub * 128:
                                              i * 256 + sub * 128 + 128],
                            h1bf[i][:],
                            start=(i == 0), stop=(i == 7)),
                            wait=([lw, (sV, v_ph), grd.get(f"bank{b}")]
                                  if i == 0 else ()),
                            inc=([(sT, 1)] if i == 7 else ()))
                    t_mm = sT.v
                    s_mu = p.add("scalar", lambda e, f=f, b=b: e.activation(
                        mubf[f][:], pb[b][:], AF.Identity, bias=0.0, scale=1.0),
                        wait=[(sT, t_mm), grd.get("mubf_free")], inc=[(sS, 1)])[0]
                    grd[f"bank{b}"] = (sS, s_mu)
                grd[f"wr{rix}"] = (sT, t_mm)
            s_mu_done = sS.v

            for kt in range(8):
                v_res = p.add("vector", lambda e, kt=kt: e.tensor_tensor(
                    tmpf[0][:], xg[kt][:], hidg[kt][:], ALU.subtract),
                    wait=[(sV, v_proc)] if kt == 0 else ())
                p.add("vector", lambda e, kt=kt: e.tensor_copy(
                    resbf[kt][:], tmpf[0][:]),
                    wait=[grd.get("resbf_free")] if kt == 0 else ())
                vsq = p.add("vector", lambda e, kt=kt: e.tensor_tensor(
                    sqb[kt % 2][:], tmpf[0][:], tmpf[0][:], ALU.mult),
                    wait=[grd.get(f"dsq{kt % 2}")], inc=[(sV, 1)])[0]
                tq = p.add("tensor", lambda e, kt=kt: e.matmul(
                    pb[5][0:1, :], cst["ones_col_bf"][:], sqb[kt % 2][:],
                    start=(kt == 0), stop=(kt == 7)),
                    wait=[(sV, vsq)] + ([grd.get("pb5")] if kt == 0 else []),
                    inc=[(sT, 1)])[0]
                grd[f"dsq{kt % 2}"] = (sT, tq)
            t_dst = sT.v
            grd["xg"] = (sV, sV.v)
            v_dstrow = p.add("vector", lambda e: e.tensor_scalar(
                rows["m"][:], pb[5][0:1, :], 1.0 / D, None, ALU.mult),
                wait=[(sT, t_dst), grd.get("dstrow_free")], inc=[(sV, 1)])[0]
            grd["pb5"] = (sV, v_dstrow)
            d_dstdma = c.dma("gpsimd", DST[0:1, gl:gl + G], rows["m"][:],
                             wait=[(sV, v_dstrow)])
            c.outs.append(d_dstdma)
            v_ds1 = p.add("vector", lambda e: e.tensor_reduce(
                rows["sc1"][0:1, 0:1], rows["m"][:], AX.X, ALU.add),
                wait=[grd.get("sumacc")], inc=[(sV, 1)])[0]
            v_ds2 = p.add("vector", lambda e: e.tensor_tensor(
                dsa[:], dsa[:], rows["sc1"][0:1, 0:1], ALU.add),
                wait=[(sV, v_ds1)], inc=[(sV, 1)])[0]
            grd["sumacc"] = (sV, v_ds2)
            grd["dstrow_free"] = (sV, v_ds2)

            for f in range(8):
                if f % 2 == 0:
                    rix, lw = wload(wpair_ap(wsc["PW"], 4 + f // 2))
                b = [2, 3, 6, 7][f % 4]
                for i in range(8):
                    p.add("tensor", lambda e, i=i, rix=rix, f=f, b=b: e.matmul(
                        pb[b][:], wr[rix][:, i * 256 + (f % 2) * 128:
                                          i * 256 + (f % 2) * 128 + 128],
                        h1bf[i][:],
                        start=(i == 0), stop=(i == 7)),
                        wait=([lw, (sV, v_ph), grd.get(f"bank{b}")] if i == 0 else ()),
                        inc=([(sT, 1)] if i == 7 else ()))
                t_lv = sT.v
                if f % 2 == 1:
                    grd[f"wr{rix}"] = (sT, t_lv)
                s_e = p.add("scalar", lambda e, b=b: e.activation(
                    ebuf[:], pb[b][:], AF.Exp, bias=0.0, scale=-1.0),
                    wait=[(sT, t_lv), grd.get("ebuf_free")], inc=[(sS, 1)])[0]
                p.add("vector", lambda e, f=f: e.tensor_tensor(
                    tmpf[0][:], resbf[f][:], mubf[f][:], ALU.subtract),
                    wait=[(sS, s_mu_done), (sS, s_e)])
                p.add("vector", lambda e: e.tensor_tensor(
                    tmpf[0][:], tmpf[0][:], tmpf[0][:], ALU.mult))
                p.add("vector", lambda e: e.tensor_scalar(
                    tmpf[0][:], tmpf[0][:], 1.0, None, ALU.add))
                p.add("vector", lambda e: e.tensor_tensor(
                    tmpf[0][:], tmpf[0][:], ebuf[:], ALU.mult))
                p.add("vector", lambda e, b=b: e.tensor_tensor(
                    tmpf[0][:], tmpf[0][:], pb[b][:], ALU.add))
                vq = p.add("vector", lambda e, f=f: e.tensor_copy(
                    sqb[f % 2][:], tmpf[0][:]),
                    wait=[grd.get(f"dsq{f % 2}")], inc=[(sV, 1)])[0]
                grd["ebuf_free"] = (sV, vq)
                grd[f"bank{b}"] = (sV, vq)
                tq = p.add("tensor", lambda e, f=f: e.matmul(
                    pb[5][0:1, :], cst["ones_col_bf"][:], sqb[f % 2][:],
                    start=(f == 0), stop=(f == 7)),
                    wait=[(sV, vq)] + ([grd.get("pb5")] if f == 0 else []),
                    inc=[(sT, 1)])[0]
                grd[f"dsq{f % 2}"] = (sT, tq)
            t_dch = sT.v
            v_dchrow = p.add("vector", lambda e: e.tensor_scalar(
                rows["m"][:], pb[5][0:1, :], 0.5 / D, -0.5, ALU.mult, ALU.add),
                wait=[(sT, t_dch), grd.get("dstrow_free"), d_dstdma],
                inc=[(sV, 1)])[0]
            grd["pb5"] = (sV, v_dchrow)
            d_dchdma = c.dma("gpsimd", DCH[0:1, gl:gl + G], rows["m"][:],
                             wait=[(sV, v_dchrow)])
            c.outs.append(d_dchdma)
            grd["rowm_free"] = d_dchdma
            v_dc1 = p.add("vector", lambda e: e.tensor_reduce(
                rows["sc1"][0:1, 0:1], rows["m"][:], AX.X, ALU.add),
                wait=[grd.get("sumacc")], inc=[(sV, 1)])[0]
            v_dc2 = p.add("vector", lambda e: e.tensor_tensor(
                dsb[:], dsb[:], rows["sc1"][0:1, 0:1], ALU.add),
                wait=[(sV, v_dc1)], inc=[(sV, 1)])[0]
            grd["sumacc"] = (sV, v_dc2)
            grd["dstrow_free"] = (sV, v_dc2)
            grd["h1bf_free"] = (sT, sT.v)
            grd["mubf_free"] = (sV, sV.v)
            grd["resbf_free"] = (sV, sV.v)
            grd["trig"] = (sT, sT.v)

        tv = sT.v
        p.add("tensor", lambda e: e.matmul(pb[6][:], cst["SUge"][:],
                                           cst["Sh4"][:, 0:G], start=True, stop=False),
              wait=[(sV, sV.v), (sS, sS.v), (sT, tv)])
        t_f = p.add("tensor", lambda e: e.matmul(pb[6][:], KTs[:, 0:128],
                                                 QTa[:, 0:G], start=False, stop=True),
                    inc=[(sT, 1)])[0]
        v_f = p.add("vector", lambda e: e.tensor_copy(tmpf[0][:], pb[6][:]),
                    wait=[(sT, t_f)], inc=[(sV, 1)])[0]
        c.outs.append(c.dma("gpsimd", DBG_PS2, tmpf[0][:], wait=[(sV, v_f)]))
        c.outs.append(c.dma("gpsimd", DSA, dsa[:], wait=[(sV, sV.v)]))
        c.outs.append(c.dma("gpsimd", DSB, dsb[:], wait=[(sV, sV.v)]))
        c.join("gpsimd", c.outs)
        p.barrier("end")
        import os
        sim_edges = bool(int(os.environ.get("MK_SIM_EDGES", "0")))
        if sim_edges:
            _ENG = ENGINES
            c.p.emit_pre = None
        for s_ in p.sems:
            s_.h = c.es.enter_context(nc.semaphore(s_.name))
        if sim_edges:
            c.p.ord = {e: type(p.sems[0])("ord_" + e) for e in _ENG}
            for s_ in c.p.ord.values():
                s_.h = c.es.enter_context(nc.semaphore(s_.name))
            c.p.emit(c.es.enter_context(nc.Block()), sim_edges=True)
        else:
            c.p.emit(c.es.enter_context(nc.Block()))
    return nc, c


def stage1_inputs(I, b):
    qkv = split_qkv(I["L0_attn_W_qkv"], I["L0_attn_b_qkv"])
    colshape = lambda v: np.ascontiguousarray(v.reshape(8, 128).T)
    m = {
        "xT": np.ascontiguousarray(I["L0_attn_X"][b].T),
        "Wk": qkv["Wk"], "Wq": qkv["Wq"], "Wv": qkv["Wv"],
        "Wo": I["L0_attn_W_out"],
        "bk": colshape(qkv["bk"]), "bq": colshape(qkv["bq"]),
        "bv": colshape(qkv["bv"]), "bo": colshape(I["L0_attn_b_out"]),
        "bvrep": np.tile(qkv["bv"].reshape(1, D), (128, 1)),
        "wq1p": permute_w_rope(I["L0_ffn_wq1"]),
        "wk1p": permute_w_rope(I["L0_ffn_wk1"]),
        "wv1": I["L0_ffn_wv1"], "wo1": I["L0_ffn_wo1"],
        "wg1": I["L0_ffn_wg1"], "wu1": I["L0_ffn_wu1"], "wd1": I["L0_ffn_wd1"],
        "prior_w": I["L0_ffn_prior_w"],
        "g11": colshape(I["L0_ffn_ln1_1"]), "g21": colshape(I["L0_ffn_ln2_1"]),
        "gpn": colshape(I["L0_ffn_prior_norm_w"]),
    }
    for n, v in host_consts().items():
        m["c_" + n] = np.asarray(v)
    return m



"""Stage-2 builder: gating z, top-k selection, gather, decoder-2, scatter, output.

Inputs per core: x1T, procT [D,T] f32; dst, dch [1,T]; allsums [1,16]
(8x dsa then 8x dsb, host-concatenated layout); dec2 weights; consts.
Output: outT [D, T] f32 (host transposes).
"""
import numpy as np
import ml_dtypes
from contextlib import ExitStack
import concourse.bass as bass
import concourse.mybir as mybir

AP = bass.AP
C0 = 20.0
T2 = TSEL
NG2 = T2 // G
KSEL = 1024
CW_HI = 6.28125
CW_MID = float(np.float32(2 * np.pi - 6.28125))
CW_LO = float(np.float32((2 * np.pi - 6.28125) - np.float64(np.float32(2 * np.pi - 6.28125))))
INV2PI = float(np.float32(1.0 / (2 * np.pi)))
PI = float(np.pi)
BIG = 1e30


def host_consts2():
    c = {}
    iu = np.arange(128)
    c["SUge"] = (iu[None, :] >= iu[:, None]).astype(ml_dtypes.bfloat16)
    c["SUstT"] = (iu[None, :] > iu[:, None]).astype(ml_dtypes.bfloat16)
    sh = np.zeros((128, 2048), np.float32)
    for kt in range(4):
        for f in range(512):
            r = f + 1 - 128 * kt
            if r <= 0:
                sh[0, 512 * kt + f] = NEG
            elif r < 128:
                sh[r, 512 * kt + f] = NEG
    c["Sh4"] = sh.astype(ml_dtypes.bfloat16)
    c["ones_col_bf"] = np.ones((128, 1), ml_dtypes.bfloat16)
    c["onesr_f"] = np.ones((1, 128), np.float32)
    c["iota1024_2d"] = np.tile(np.arange(1024, dtype=np.float32)[None, :], (128, 1))
    c["iotaP_bf"] = np.arange(128, dtype=np.float64).astype(ml_dtypes.bfloat16).reshape(128, 1)
    c["hi32_bf"] = np.tile((128.0 * np.arange(32, dtype=np.float64)
                            ).astype(ml_dtypes.bfloat16).reshape(1, 32), (128, 1))
    c["idf"] = np.eye(128, dtype=np.float32)
    c["inv_row"] = (1.0 / 10000.0 ** (np.arange(32, dtype=np.float64) / 32.0)
                    ).astype(np.float32).reshape(1, 32)
    return c


def build_stage2():
    nc = bass.Bass("TRN2", target_bir_lowering=False, detect_race_conditions=False)
    c = Ctx(nc)
    p = c.p
    grd = c.grd
    sV, sS, sT, sG = c.sV, c.sS, c.sT, c.sG

    def din(name, shape, dt=F32):
        return nc.dram_tensor(name, shape, dt, kind="ExternalInput")
    def dout(name, shape, dt=F32):
        return nc.dram_tensor(name, shape, dt, kind="ExternalOutput")

    HC = host_consts2()
    X1T = din("x1T", [D, T])
    PROCT = din("procT", [D, T])
    DSTI = din("dst", [1, T]); DCHI = din("dch", [1, T])
    ALLS = din("allsums", [1, 16])
    W2 = {n: din(n, shp) for n, shp in [
        ("wq2p", [D, D]), ("wk2p", [D, D]), ("wv2", [D, D]), ("wo2", [D, D]),
        ("wg2", [D, FFN]), ("wu2", [D, FFN]), ("wd2", [FFN, D])]}
    GC = {n: din(n, [128, 8]) for n in ["g12", "g22"]}
    CONSTS = {n: din("c_" + n, list(v.shape),
                     BF16 if v.dtype == ml_dtypes.bfloat16 else F32)
              for n, v in HC.items()}
    OUT = dout("outT", [D, T])
    DBG_SEL = dout("dbg_selidx", [1, T2])
    DBG_SC = dout("dbg_scores", [1, T2])
    DBG_Y = dout("dbg_y", [D, T2])
    DBG_SELT = dout("dbg_selT", [D, T2], BF16)

    wsc = {n: nc.dram_tensor("sc2_" + n, list(W2[n].shape), BF16) for n in W2}
    KT2d = nc.dram_tensor("sc2_KT", [64, H * T2], BF16)
    V2d = nc.dram_tensor("sc2_V", [128, H * 8 * 65], BF16)
    PROCN = nc.dram_tensor("sc2_procN", [T, D], BF16)
    BNZ = nc.dram_tensor("sc2_bnz", [1, T], F32)
    BNR = nc.dram_tensor("sc2_bnr", [1, T], F32)
    BNS = nc.dram_tensor("sc2_bns", [1, T2], F32)

    with c.es:
        cst = {n: c.sb("k_" + n, list(v.shape),
                       BF16 if v.dtype == ml_dtypes.bfloat16 else F32)
               for n, v in HC.items()}
        gcol = {n: c.sb("gc_" + n, [128, 8], F32) for n in GC}
        zcol = c.sb("zcol", [128, 32], F32)
        zTbf = c.sb("zTbf", [128, 32], BF16)
        rkT = c.sb("rkT", [128, 32], F32)
        selTc = c.sb("selTc", [128, 8], F32)
        A32 = c.sb("A32", [32, 128], F32)
        m32 = {n: c.sb("m_" + n, [128, 32], F32) for n in
               ["ge", "eq", "gt", "pre", "pre2", "msel", "rank"]}
        col1 = c.sb("col1", [128, 1], F32)
        col2 = c.sb("col2", [128, 1], F32)
        col3 = c.sb("col3", [128, 1], F32)
        colbf = c.sb("colbf", [128, 1], BF16)
        s11 = {n: c.sb("s_" + n, [1, 1], F32) for n in
               ["m1", "m2", "lo", "hi", "mid", "pred", "theta", "kk", "t1", "t2"]}
        PTt = c.sb("PTt", [128, 1024], BF16)
        selT = [c.sb(f"selT{i}", [128, T2], BF16) for i in range(8)]
        x2g = [c.sb(f"x2g{i}", [128, G], F32) for i in range(8)]
        h1bf = [c.sb(f"h1bf{i}", [128, G], BF16) for i in range(8)]
        sgbf = c.sb("sgbf", [128, G], BF16)
        swig = c.sb("swig", [128, 32 * G], BF16)
        QTa = c.sb("QTa", [64, H * G], BF16)
        kpiece = c.sb("kpiece", [64, 4 * G], BF16)
        Vg = c.sb("Vg", [128, 4 * H * 65], BF16)
        KTld = c.sb("KTld", [64, 4 * G], BF16)
        Vld = c.sb("Vld", [128, 4 * 4 * 65], BF16)
        expS = [c.sb(f"expS{i}", [128, G], BF16) for i in range(4)]
        OTbf = [c.sb(f"OTbf{i}", [128, G], BF16) for i in range(8)]
        recb = c.sb("recb", [64, G], F32)
        rows = {n: c.sb("r_" + n, [1, G], F32) for n in ["var", "rstd", "recip", "sc1"]}
        rows_bf = {n: c.sb("rb_" + n, [1, G], BF16) for n in ["rstd", "recip"]}
        onesr = c.sb("onesr", [1, 128], BF16)
        eps2 = c.sb("eps2", [1, 1], F32)
        c0t = c.sb("c0t", [128, 1], F32)
        wtile = c.sb("wtile", [128, 1024], BF16)
        wcols = c.sb("wcols", [128, 4096], BF16)
        tmpf = [c.sb(f"tmpf{i}", [128, G], F32) for i in range(2)]
        sqb = [c.sb(f"sqb{i}", [128, G], BF16) for i in range(2)]
        xnbf = [c.sb(f"xnbf{i}", [128, G], BF16) for i in range(8)]
        ang = c.sb("ang", [32, G], F32)
        angk = c.sb("angk", [32, G], F32)
        angi = c.sb("angi", [32, G], I32)
        cosrep = c.sb("cosrep", [128, G], F32)
        sinrep = c.sb("sinrep", [128, G], F32)
        rtmp = c.sb("rtmp", [128, G], F32)
        srow = c.sb("srow", [1, T2], F32)
        rowB = c.sb("rowB", [1, T2], F32)
        srep = c.sb("srep", [128, G], F32)
        deltaN = [c.sb(f"deltaN{i}", [128, D], BF16) for i in range(8)]
        projt = wtile
        trb = sqb[0]
        pb = [c.psum(f"pb{i}") for i in range(8)]

        def VS(fn, wait=()):
            v = p.add("vector", fn, wait=[w for w in wait if w], inc=[(sV, 1)])[0]
            return (sV, v)

        def TS(fn, wait=()):
            v = p.add("tensor", fn, wait=[w for w in wait if w], inc=[(sT, 1)])[0]
            return (sT, v)

        def SC(fn, wait=()):
            v = p.add("scalar", fn, wait=[w for w in wait if w], inc=[(sS, 1)])[0]
            return (sS, v)

        toks = [c.dma("sync", cst[n][:], CONSTS[n].ap()) for n in cst]
        toks += [c.dma("sync", gcol[n][:], GC[n].ap()) for n in gcol]
        toks.append(c.dma("sync", srow[0:1, 0:16], ALLS.ap()))
        v_cl = c.join("vector", toks)
        wtoks = [c.dma("gpsimd", wsc[n].ap(), W2[n].ap()) for n in W2]
        v_wcast = c.join("gpsimd", wtoks)
        e0 = VS(lambda e: e.memset(onesr[:], 1.0), [v_cl])
        e0 = VS(lambda e: e.memset(eps2[:], EPS_RMS), [e0])
        e0 = VS(lambda e: e.memset(c0t[:], -C0), [e0])
        e0 = VS(lambda e: e.memset(
            AP(Vg, 64, [[4 * H * 65, 128], [65, 4 * H], [1, 1]]), 1.0), [e0])
        v_init = e0

        pn_toks = []
        for f in range(8):
            for jt8 in range(8):
                ld = c.dma("sync", tmpf[0][:],
                           AP(PROCT, f * 128 * T + jt8 * 512, [[T, 128], [1, 512]]),
                           wait=[grd.get("tmpf0")])
                tw = None
                for q in range(4):
                    tw = TS(lambda e, q=q: e.transpose(
                        pb[5][:, q * 128:(q + 1) * 128],
                        tmpf[0][:, q * 128:(q + 1) * 128], cst["idf"][:]),
                        [ld, v_cl, grd.get("tr5")] if q == 0 else [])
                ev = VS(lambda e: e.tensor_copy(trb[:], pb[5][:]),
                        [tw, grd.get("trb")])
                grd["tr5"] = ev
                grd["tmpf0"] = ev
                st = c.dma("gpsimd",
                           AP(PROCN, jt8 * 512 * D + f * 128,
                              [[D, 128], [128 * D, 4], [1, 128]]),
                           AP(trb, 0, [[G, 128], [128, 4], [1, 128]]),
                           wait=[ev])
                grd["trb"] = st
                pn_toks.append(st)
        d_procn = c.join("gpsimd", pn_toks)

        e1 = VS(lambda e: e.tensor_reduce(s11["m1"][:], srow[0:1, 0:8], AX.X, ALU.add), [v_cl])
        e1 = VS(lambda e: e.tensor_scalar(s11["m1"][:], s11["m1"][:], 1.0 / (8 * T), None, ALU.mult), [e1])
        e1 = VS(lambda e: e.tensor_reduce(s11["m2"][:], srow[0:1, 8:16], AX.X, ALU.add), [e1])
        e1 = VS(lambda e: e.tensor_scalar(s11["m2"][:], s11["m2"][:], 1.0 / (8 * T), None, ALU.mult), [e1])

        def bcast11(src11, dstcol, wait, bank=6):
            tb = TS(lambda e, bank=bank: e.matmul(
                pb[bank][:, 0:1], cst["onesr_f"][:], src11, start=True, stop=True),
                list(wait) + [grd.get(f"bc{bank}")])
            ev = VS(lambda e, bank=bank: e.tensor_copy(dstcol, pb[bank][:, 0:1]), [tb])
            grd[f"bc{bank}"] = ev
            return ev

        e2_ = bcast11(s11["m1"][:], col1[:], [e1])
        e2_ = bcast11(s11["m2"][:], col2[:], [e2_])
        dl1 = c.dma("sync", zcol[:], AP(DSTI, 0, [[32, 128], [1, 32]]))
        dl2 = c.dma("sync", m32["ge"][:], AP(DCHI, 0, [[32, 128], [1, 32]]))
        e3 = VS(lambda e: e.tensor_scalar(zcol[:], zcol[:], col1[:], None, ALU.subtract),
                [dl1, e2_])
        e3 = VS(lambda e: e.tensor_scalar(m32["ge"][:], m32["ge"][:], col2[:], None, ALU.subtract),
                [dl2, e3])
        e3 = VS(lambda e: e.tensor_tensor(zcol[:], zcol[:], m32["ge"][:], ALU.add), [e3])
        dz = c.dma("gpsimd", BNZ.ap(), zcol[:], wait=[e3])
        dza = c.dma("sync", A32[:], AP(BNZ, 0, [[128, 32], [1, 128]]), wait=[dz])
        tzt = TS(lambda e: e.transpose(pb[4][:, 0:32], A32[:], cst["idf"][0:32, 0:32]),
                 [dza, v_cl, grd.get("bc4")])
        ezt = VS(lambda e: e.tensor_copy(zTbf[:], pb[4][:, 0:32]), [tzt])
        grd["bc4"] = ezt

        def colmax(srccol, dst11, wait):
            tb = TS(lambda e: e.transpose(pb[4][0:1, 0:128], srccol, cst["idf"][:]),
                    list(wait) + [grd.get("bc4")])
            ev = VS(lambda e: e.tensor_copy(rowB[0:1, 0:128], pb[4][0:1, 0:128]), [tb])
            grd["bc4"] = ev
            ev = VS(lambda e: e.tensor_reduce(dst11, rowB[0:1, 0:128], AX.X, ALU.max), [ev])
            return ev

        em = VS(lambda e: e.tensor_reduce(col3[:], zcol[:], AX.X, ALU.max), [e3])
        em = colmax(col3[:], s11["hi"][:], [em])
        em2 = VS(lambda e: e.tensor_scalar(m32["eq"][:], zcol[:], -1.0, None, ALU.mult), [e3])
        em2 = VS(lambda e: e.tensor_reduce(col3[:], m32["eq"][:], AX.X, ALU.max), [em2, em])
        em2 = colmax(col3[:], s11["lo"][:], [em2])
        ew = VS(lambda e: e.tensor_scalar(s11["lo"][:], s11["lo"][:], -1.0, -1.0,
                                          ALU.mult, ALU.add), [em2])

        for it in range(40):
            ew = VS(lambda e: e.tensor_tensor(s11["mid"][:], s11["lo"][:], s11["hi"][:], ALU.add), [ew])
            ew = VS(lambda e: e.tensor_scalar(s11["mid"][:], s11["mid"][:], 0.5, None, ALU.mult), [ew])
            ew = bcast11(s11["mid"][:], col1[:], [ew])
            ew = VS(lambda e: e.tensor_scalar(m32["ge"][:], zcol[:], col1[:], None, ALU.is_gt), [ew])
            ew = VS(lambda e: e.tensor_reduce(col2[:], m32["ge"][:], AX.X, ALU.add), [ew])
            ew = VS(lambda e: e.tensor_copy(colbf[:], col2[:]), [ew])
            tc = TS(lambda e: e.matmul(pb[7][0:1, 0:1], colbf[:], cst["ones_col_bf"][:],
                                       start=True, stop=True), [ew, grd.get("bc7")])
            ew = VS(lambda e: e.tensor_scalar(s11["pred"][:], pb[7][0:1, 0:1], KSEL - 0.5,
                                              None, ALU.is_gt), [tc])
            grd["bc7"] = ew
            ew = VS(lambda e: e.tensor_tensor(s11["t1"][:], s11["mid"][:], s11["lo"][:], ALU.subtract), [ew])
            ew = VS(lambda e: e.tensor_tensor(s11["t1"][:], s11["t1"][:], s11["pred"][:], ALU.mult), [ew])
            ew = VS(lambda e: e.tensor_tensor(s11["lo"][:], s11["lo"][:], s11["t1"][:], ALU.add), [ew])
            ew = VS(lambda e: e.tensor_tensor(s11["t2"][:], s11["hi"][:], s11["mid"][:], ALU.subtract), [ew])
            ew = VS(lambda e: e.tensor_tensor(s11["t2"][:], s11["t2"][:], s11["pred"][:], ALU.mult), [ew])
            ew = VS(lambda e: e.tensor_tensor(s11["hi"][:], s11["mid"][:], s11["t2"][:], ALU.add), [ew])

        ew = bcast11(s11["hi"][:], col1[:], [ew])
        ew = VS(lambda e: e.tensor_scalar(m32["ge"][:], zcol[:], col1[:], None, ALU.is_gt), [ew])
        ew = VS(lambda e: e.tensor_scalar(m32["ge"][:], m32["ge"][:], BIG, None, ALU.mult), [ew])
        ew = VS(lambda e: e.tensor_tensor(m32["ge"][:], zcol[:], m32["ge"][:], ALU.subtract), [ew])
        ew = VS(lambda e: e.tensor_reduce(col3[:], m32["ge"][:], AX.X, ALU.max), [ew])
        ew = colmax(col3[:], s11["theta"][:], [ew])
        ew = bcast11(s11["theta"][:], col1[:], [ew])
        ew = VS(lambda e: e.tensor_scalar(m32["gt"][:], zcol[:], col1[:], None, ALU.is_gt), [ew])
        ew = VS(lambda e: e.tensor_scalar(m32["eq"][:], zcol[:], col1[:], None, ALU.is_equal), [ew])
        ew = VS(lambda e: e.tensor_reduce(col2[:], m32["gt"][:], AX.X, ALU.add), [ew])
        ew = VS(lambda e: e.tensor_copy(colbf[:], col2[:]), [ew])
        tc = TS(lambda e: e.matmul(pb[7][0:1, 0:1], colbf[:], cst["ones_col_bf"][:],
                                   start=True, stop=True), [ew, grd.get("bc7")])
        ew = VS(lambda e: e.tensor_scalar(s11["kk"][:], pb[7][0:1, 0:1], -1.0, KSEL,
                                          ALU.mult, ALU.add), [tc])
        grd["bc7"] = ew

        def prefix_excl(src, dst, tmp, wprev):
            w = VS(lambda e: e.tensor_reduce(col2[:], src[:], AX.X, ALU.add), [wprev])
            w = VS(lambda e: e.tensor_copy(colbf[:], col2[:]), [w])
            tb2 = TS(lambda e: e.matmul(pb[7][:, 0:1], cst["SUstT"][:], colbf[:],
                                        start=True, stop=True), [w, grd.get("bc7")])
            w = VS(lambda e: e.tensor_copy(tmp[:], src[:]), [tb2])
            for k in [1, 2, 4, 8, 16]:
                w = VS(lambda e, k=k: e.tensor_copy(dst[:], tmp[:]), [w])
                w = VS(lambda e, k=k: e.tensor_tensor(
                    dst[:, k:32], dst[:, k:32], tmp[:, 0:32 - k], ALU.add), [w])
                w = VS(lambda e: e.tensor_copy(tmp[:], dst[:]), [w])
            w = VS(lambda e: e.tensor_tensor(dst[:], tmp[:], src[:], ALU.subtract), [w])
            w = VS(lambda e: e.tensor_copy(col3[:], pb[7][:, 0:1]), [w])
            grd["bc7"] = w
            w = VS(lambda e: e.tensor_scalar(dst[:], dst[:], col3[:], None, ALU.add), [w])
            return w

        ew = prefix_excl(m32["eq"], m32["pre"], m32["pre2"], ew)
        ew = bcast11(s11["kk"][:], col1[:], [ew])
        ew = VS(lambda e: e.tensor_scalar(m32["pre"][:], m32["pre"][:], col1[:], None, ALU.is_lt), [ew])
        ew = VS(lambda e: e.tensor_tensor(m32["pre"][:], m32["pre"][:], m32["eq"][:], ALU.mult), [ew])
        ew = VS(lambda e: e.tensor_tensor(m32["msel"][:], m32["gt"][:], m32["pre"][:], ALU.add), [ew])
        ew = prefix_excl(m32["msel"], m32["rank"], m32["pre2"], ew)
        ew = VS(lambda e: e.tensor_tensor(m32["rank"][:], m32["rank"][:], m32["msel"][:], ALU.mult), [ew])
        ew = VS(lambda e: e.tensor_scalar(m32["pre"][:], m32["msel"][:], -4096.0, 4096.0,
                                          ALU.mult, ALU.add), [ew])
        ew = VS(lambda e: e.tensor_tensor(m32["rank"][:], m32["rank"][:], m32["pre"][:], ALU.add), [ew])
        drk = c.dma("gpsimd", BNR.ap(), m32["rank"][:], wait=[ew])
        dra = c.dma("sync", A32[:], AP(BNR, 0, [[128, 32], [1, 128]]), wait=[drk])
        trt = TS(lambda e: e.transpose(pb[4][:, 0:32], A32[:], cst["idf"][0:32, 0:32]),
                 [dra, grd.get("bc4")])
        erk = VS(lambda e: e.tensor_copy(rkT[:], pb[4][:, 0:32]), [trt])
        grd["bc4"] = erk

        for jt in range(32):
            eb = VS(lambda e, jt=jt: e.tensor_scalar(
                PTt[:], cst["iota1024_2d"][:], rkT[:, jt:jt + 1], None, ALU.is_equal),
                [erk, ezt] + ([grd.get("ptt")] if jt else []))
            for ch in range(2):
                TS(lambda e, jt=jt, ch=ch: e.matmul(
                    pb[ch][0:1, :], cst["hi32_bf"][:, jt:jt + 1],
                    PTt[:, ch * 512:(ch + 1) * 512],
                    start=(jt == 0), stop=False), [eb])
                TS(lambda e, jt=jt, ch=ch: e.matmul(
                    pb[ch][0:1, :], cst["iotaP_bf"][:],
                    PTt[:, ch * 512:(ch + 1) * 512],
                    start=False, stop=(jt == 31)), [])
                TS(lambda e, jt=jt, ch=ch: e.matmul(
                    pb[2 + ch][0:1, :], zTbf[:, jt:jt + 1],
                    PTt[:, ch * 512:(ch + 1) * 512],
                    start=(jt == 0), stop=(jt == 31)), [])
            grd["ptt"] = (sT, sT.v)
        t_selmm = (sT, sT.v)
        ew = VS(lambda e: e.tensor_copy(srow[0:1, 0:512], pb[0][0:1, :]), [t_selmm])
        ew = VS(lambda e: e.tensor_copy(srow[0:1, 512:1024], pb[1][0:1, :]), [ew])
        dsl = c.dma("gpsimd", BNS.ap(), srow[0:1, 0:T2], wait=[ew])
        dbg1 = c.dma("gpsimd", DBG_SEL.ap(), srow[0:1, 0:T2], wait=[ew])
        c.outs.append(dbg1)
        ez = VS(lambda e: e.tensor_copy(rowB[0:1, 0:512], pb[2][0:1, :]), [ew])
        ez = VS(lambda e: e.tensor_copy(rowB[0:1, 512:1024], pb[3][0:1, :]), [ez])
        v_evac = ez
        sc1_ = SC(lambda e: e.activation(srow[0:1, 0:T2], rowB[0:1, 0:T2], AF.Exp,
                                         bias=0.0, scale=-1.0), [ez, dsl, dbg1])
        ez = VS(lambda e: e.tensor_scalar(srow[0:1, 0:T2], srow[0:1, 0:T2], 1.0, None, ALU.add), [sc1_])
        ez = VS(lambda e: e.reciprocal(srow[0:1, 0:T2], srow[0:1, 0:T2]), [ez])
        v_scores = ez
        c.outs.append(c.dma("gpsimd", DBG_SC.ap(), srow[0:1, 0:T2], wait=[ez]))
        dsa_ = c.dma("sync", A32[0:8, :], AP(BNS, 0, [[128, 8], [1, 128]]), wait=[dsl])
        tsc = TS(lambda e: e.transpose(pb[4][:, 0:8], A32[0:8, :], cst["idf"][0:8, 0:8]),
                 [dsa_, grd.get("bc4")])
        esc = VS(lambda e: e.tensor_copy(selTc[:], pb[4][:, 0:8]), [tsc])
        grd["bc4"] = esc
        dsel2 = c.dma("sync", rowB[0:1, 0:T2], BNS.ap(), wait=[dsl, sc1_])

        pjbuf = [wtile, wcols]
        for ch in range(2):
            for jt in range(32):
                pj = pjbuf[jt % 2]
                ldp = c.dma("sync", pj[:, 0:1024],
                            AP(PROCN, jt * 128 * D, [[D, 128], [1, D]]),
                            wait=[d_procn, grd.get(f"projt{jt % 2}")])
                eb = VS(lambda e, jt=jt: e.tensor_scalar(
                    PTt[:], cst["iota1024_2d"][:], rkT[:, jt:jt + 1], None, ALU.is_equal),
                    [grd.get("ptt"), erk])
                for f in range(8):
                    TS(lambda e, jt=jt, f=f, ch=ch, pj=pj: e.matmul(
                        pb[f][:], pj[:, f * 128:(f + 1) * 128],
                        PTt[:, ch * 512:(ch + 1) * 512],
                        start=(jt == 0), stop=(jt == 31)),
                       ([ldp, eb, v_evac, esc, grd.get(f"gat{f}")] if jt == 0
                        else [ldp, eb]))
                grd["ptt"] = (sT, sT.v)
                grd[f"projt{jt % 2}"] = (sT, sT.v)
            for f in range(8):
                ev = VS(lambda e, f=f, ch=ch: e.tensor_copy(
                    selT[f][:, ch * 512:(ch + 1) * 512], pb[f][:]), [(sT, sT.v)])
                grd[f"gat{f}"] = ev
        v_selT = (sV, sV.v)
        for f in range(8):
            c.outs.append(c.dma("gpsimd", AP(DBG_SELT, f * 128 * T2, [[T2, 128], [1, T2]]),
                                selT[f][:], wait=[v_selT]))
        grd["pb01"] = v_selT
        grd["pb4"] = v_selT
        grd["pbb6"] = v_selT
        grd["pbb7"] = v_selT
        grd["pb2"] = v_selT
        grd["pb3"] = v_selT
        grd["wtile"] = grd["projt0"]
        grd["wcols"] = grd["projt1"]

        def ln_rms(src_aps, dst_aps, gains, wait_first):
            vc = []
            for kt in range(8):
                w = [wait_first] if kt == 0 else []
                vc.append(p.add("vector", lambda e, kt=kt: e.tensor_copy(
                    xnbf[kt][:], src_aps[kt]), wait=[x for x in w if x],
                    inc=[(sV, 1)])[0])
            sqmm = {}
            for kt in range(8):
                vsq = p.add("vector", lambda e, kt=kt: e.tensor_tensor(
                    sqb[kt % 2][:], xnbf[kt][:], xnbf[kt][:], ALU.mult),
                    wait=[(sV, vc[kt])] + ([(sT, sqmm[kt - 2])] if kt >= 2 else []),
                    inc=[(sV, 1)])[0]
                sqmm[kt] = p.add("tensor", lambda e, kt=kt: e.matmul(
                    pb[1][0:1, :], cst["ones_col_bf"][:], sqb[kt % 2][:],
                    start=(kt == 0), stop=(kt == 7)),
                    wait=[(sV, vsq)] + ([grd.get("pb01")] if kt == 0 else []),
                    inc=[(sT, 1)])[0]
            t_sq = sqmm[7]
            w = VS(lambda e: e.tensor_scalar(rows["var"][:], pb[1][0:1, :],
                                             1.0 / D, None, ALU.mult), [(sT, t_sq)])
            s1_ = SC(lambda e: e.activation(rows["sc1"][:], rows["var"][:], AF.Ln,
                                            bias=eps2[:], scale=1.0), [w])
            s2_ = SC(lambda e: e.activation(rows["rstd"][:], rows["sc1"][:], AF.Exp,
                                            bias=0.0, scale=-0.5), [s1_])
            w = VS(lambda e: e.tensor_copy(rows_bf["rstd"][:], rows["rstd"][:]),
                   [s2_, grd.get("lnrows")])
            tb_ = TS(lambda e: e.matmul(pb[1][:], onesr[:], rows_bf["rstd"][:],
                                        start=True, stop=True), [w])
            grd["lnrows"] = tb_
            out_v = None
            for kt in range(8):
                out_v = p.add("vector", lambda e, kt=kt: e.scalar_tensor_tensor(
                    dst_aps[kt], src_aps[kt], gcol[gains][:, kt:kt + 1], pb[1][:],
                    ALU.mult, ALU.mult),
                    wait=([tb_] if kt == 0 else ()), inc=[(sV, 1)])[0]
            grd["pb01"] = (sV, out_v)
            return (sV, out_v)

        for g in range(NG2):
            gl = g * G
            ta = TS(lambda e, gl=gl: e.matmul(
                pb[4][0:32, :], cst["inv_row"][:], rowB[0:1, gl:gl + G],
                start=True, stop=True), [dsel2, grd.get("pb4")])
            ea = VS(lambda e: e.tensor_copy(ang[:], pb[4][0:32, :]), [ta, grd.get("trig")])
            grd["pb4"] = ea
            ea = VS(lambda e: e.tensor_scalar(angk[:], ang[:], INV2PI, None, ALU.mult), [ea])
            ea = VS(lambda e: e.tensor_copy(angi[:], angk[:]), [ea])
            ea = VS(lambda e: e.tensor_copy(angk[:], angi[:]), [ea])
            ea = VS(lambda e: e.tensor_scalar(rtmp[0:32, :], angk[:], CW_HI, None, ALU.mult), [ea])
            ea = VS(lambda e: e.tensor_tensor(ang[:], ang[:], rtmp[0:32, :], ALU.subtract), [ea])
            ea = VS(lambda e: e.tensor_scalar(rtmp[0:32, :], angk[:], CW_MID, None, ALU.mult), [ea])
            ea = VS(lambda e: e.tensor_tensor(ang[:], ang[:], rtmp[0:32, :], ALU.subtract), [ea])
            ea = VS(lambda e: e.tensor_scalar(rtmp[0:32, :], angk[:], CW_LO, None, ALU.mult), [ea])
            ea = VS(lambda e: e.tensor_tensor(ang[:], ang[:], rtmp[0:32, :], ALU.subtract), [ea])
            es_ = SC(lambda e: e.activation(sinrep[0:32, :], ang[:], AF.Sin,
                                            bias=0.0, scale=1.0), [ea])
            ea = VS(lambda e: e.tensor_scalar(angk[:], ang[:], PI / 2, None, ALU.add), [ea])
            ea = VS(lambda e: e.tensor_scalar(rtmp[0:32, :], angk[:], PI, None, ALU.is_gt), [ea])
            ea = VS(lambda e: e.tensor_scalar(rtmp[0:32, :], rtmp[0:32, :], 2 * PI, None, ALU.mult), [ea])
            ea = VS(lambda e: e.tensor_tensor(angk[:], angk[:], rtmp[0:32, :], ALU.subtract), [ea])
            ec_ = SC(lambda e: e.activation(cosrep[0:32, :], angk[:], AF.Sin,
                                            bias=0.0, scale=1.0), [ea, es_])
            et = ec_
            for q in range(1, 4):
                et = VS(lambda e, q=q: e.tensor_copy(
                    cosrep[q * 32:(q + 1) * 32, :], cosrep[0:32, :]), [et])
                et = VS(lambda e, q=q: e.tensor_copy(
                    sinrep[q * 32:(q + 1) * 32, :], sinrep[0:32, :]), [et])
            v_trig = et

            v_r1_ = ln_rms([selT[i][:, gl:gl + G] for i in range(8)],
                           [h1bf[i][:] for i in range(8)], "g12",
                           grd.get("h1bf_free") or v_selT)

            for ri, (role, Wn, scl) in enumerate([("k", "wk2p", 1.0),
                                                  ("q", "wq2p", 0.125)]):
                for j4 in range(4):
                    pA, pB = ((2, 3) if (ri * 4 + j4) % 2 == 0 else (6, 7))
                    lwA = c.dma("sync", wtile[:], wtile_ap(wsc[Wn], 2 * j4),
                                wait=[v_wcast, grd.get("wtile")])
                    lwB = c.dma("sync", wcols[:, 0:1024], wtile_ap(wsc[Wn], 2 * j4 + 1),
                                wait=[v_wcast, grd.get("wcols")])
                    for i in range(8):
                        p.add("tensor", lambda e, i=i, pA=pA: e.matmul(
                            pb[pA][:], wtile[:, i * 128:(i + 1) * 128], h1bf[i][:],
                            start=(i == 0), stop=(i == 7)),
                            wait=([lwA, v_r1_, grd.get(f"bank{pA}")] if i == 0 else ()),
                            inc=([(sT, 1)] if i == 7 else ()))
                    tA = sT.v
                    for i in range(8):
                        p.add("tensor", lambda e, i=i, pB=pB: e.matmul(
                            pb[pB][:], wcols[:, i * 128:(i + 1) * 128], h1bf[i][:],
                            start=(i == 0), stop=(i == 7)),
                            wait=([lwB, grd.get(f"bank{pB}")] if i == 0 else ()),
                            inc=([(sT, 1)] if i == 7 else ()))
                    tB = sT.v
                    grd["wtile"] = (sT, tA)
                    grd["wcols"] = (sT, tB)
                    ew_ = VS(lambda e, pA=pA: e.tensor_tensor(
                        tmpf[0][:], pb[pA][:], cosrep[:], ALU.mult),
                        [(sT, tB), v_trig, grd.get("ropebuf")])
                    ew_ = VS(lambda e, pB=pB: e.tensor_tensor(
                        tmpf[1][:], pb[pB][:], sinrep[:], ALU.mult), [ew_])
                    ew_ = VS(lambda e: e.tensor_tensor(tmpf[0][:], tmpf[0][:], tmpf[1][:],
                                                       ALU.subtract), [ew_])
                    ew_ = VS(lambda e, pA=pA: e.tensor_tensor(
                        rtmp[:], pb[pA][:], sinrep[:], ALU.mult), [ew_])
                    ew_ = VS(lambda e, pB=pB: e.tensor_tensor(
                        tmpf[1][:], pb[pB][:], cosrep[:], ALU.mult), [ew_])
                    vrp = VS(lambda e: e.tensor_tensor(rtmp[:], rtmp[:], tmpf[1][:],
                                                       ALU.add), [ew_])
                    grd[f"bank{pA}"] = vrp
                    grd[f"bank{pB}"] = vrp
                    dst_ = QTa if role == "q" else kpiece
                    sv0 = None
                    for m in range(4):
                        hh = 4 * j4 + m
                        o0 = hh * G if role == "q" else m * G
                        w0 = ([vrp, grd.get("qta2_free" if role == "q" else "kp_free")]
                              if m == 0 else ())
                        sv0 = p.add("scalar", lambda e, m=m, o0=o0, dst_=dst_, scl=scl:
                                    e.activation(dst_[0:32, o0:o0 + G],
                                                 tmpf[0][32 * m:32 * m + 32, :],
                                                 AF.Identity, bias=0.0, scale=scl),
                                    wait=w0, inc=[(sS, 1)])[0]
                        sv0 = p.add("scalar", lambda e, m=m, o0=o0, dst_=dst_, scl=scl:
                                    e.activation(dst_[32:64, o0:o0 + G],
                                                 rtmp[32 * m:32 * m + 32, :],
                                                 AF.Identity, bias=0.0, scale=scl),
                                    inc=[(sS, 1)])[0]
                    grd["ropebuf"] = (sS, sv0)
                    if role == "k":
                        dk = c.dma("gpsimd",
                                   AP(KT2d, (4 * j4) * T2 + gl,
                                      [[H * T2, 64], [T2, 4], [1, G]]),
                                   AP(kpiece, 0, [[4 * G, 64], [G, 4], [1, G]]),
                                   wait=[(sS, sv0)])
                        grd["kp_free"] = dk
                        grd.setdefault("kst_toks", []).append(dk)
                if role == "q":
                    s_q2 = sS.v
            d_kst2 = c.join("gpsimd", grd.pop("kst_toks"))

            for half in range(2):
                lw = c.dma("sync", wcols[:],
                           AP(wsc["wv2"], half * 512, [[D, 128], [128 * D, 8], [1, 512]]),
                           wait=[v_wcast, grd.get("wcols")])
                for tt_ in range(4):
                    b = 2 + (half * 4 + tt_) % 2
                    for i in range(8):
                        p.add("tensor", lambda e, i=i, tt_=tt_, b=b: e.matmul(
                            pb[b][:], h1bf[i][:, tt_ * 128:(tt_ + 1) * 128],
                            wcols[:, i * 512:(i + 1) * 512],
                            start=(i == 0), stop=(i == 7)),
                            wait=([lw, v_r1_, grd.get(f"bank{b}")] if i == 0 else ()),
                            inc=([(sT, 1)] if i == 7 else ()))
                    t_v = sT.v
                    v_ev = VS(lambda e, tt_=tt_, half=half, b=b: e.tensor_copy(
                        AP(Vg, (half * 8 * 4 + tt_) * 65,
                           [[4 * H * 65, 128], [4 * 65, 8], [1, 64]]),
                        pb[b][:]), [(sT, t_v), grd.get("vg2_free"), v_init])
                    grd[f"bank{b}"] = v_ev
                grd["wcols"] = (sT, sT.v)
            v_v2 = (sV, sV.v)
            d_vst2 = c.dma("gpsimd",
                           AP(V2d, 4 * g * 65,
                              [[H * 8 * 65, 128], [8 * 65, H], [1, 260]]),
                           AP(Vg, 0, [[4 * H * 65, 128], [260, H], [1, 260]]),
                           wait=[v_v2])
            grd["vg2_free"] = d_vst2

            LAG = 3
            for hs in range(4):
                h0 = hs * 4
                av_started = [False] * 4
                pending = []

                def emit_av2(rec):
                    hl, kt, ppe, s_exp, lwv, first, last, strip_first = rec
                    p.add("tensor", lambda e, hl=hl, kt=kt, ppe=ppe,
                          first=first, last=last: e.matmul(
                              pb[hl][0:65, :],
                              AP(Vld, (hl * 4 + kt) * 65,
                                 [[4 * 4 * 65, 128], [1, 65]]),
                              expS[ppe][:],
                              start=first, stop=last),
                          wait=[(sS, s_exp)] + ([lwv] if strip_first else [])
                          + ([grd.get(f"av{hl}"), grd.get(f"bank{hl}")]
                             if first else []),
                          inc=[(sT, 1)])
                    grd[f"expS{ppe}"] = (sT, sT.v)

                tileno = 0
                for ks in range(g + 1):
                    lwk = c.dma("scalar", KTld[:],
                                AP(KT2d, h0 * T2 + ks * G,
                                   [[H * T2, 64], [T2, 4], [1, G]]),
                                wait=[d_kst2, grd.get("ktld2")])
                    lwv = c.dma("scalar", Vld[:],
                                AP(V2d, h0 * 8 * 65 + 4 * ks * 65,
                                   [[H * 8 * 65, 128], [8 * 65, 4], [1, 260]]),
                                wait=[d_vst2, grd.get("vld2")])
                    crossing = (ks == g)
                    strip_new = True
                    for hl in range(4):
                        for kt in range(4):
                            bqk = 4 + tileno % 4
                            wq_ = [grd.get(f"bank{bqk}")]
                            if strip_new:
                                wq_ = wq_ + [lwk]
                            if hs == 0 and tileno == 0:
                                wq_ = wq_ + [(sS, s_q2)]
                            if crossing:
                                p.add("tensor", lambda e, kt=kt, bqk=bqk: e.matmul(
                                    pb[bqk][:], cst["SUge"][:],
                                    cst["Sh4"][:, kt * G:(kt + 1) * G],
                                    start=True, stop=False), wait=wq_)
                            p.add("tensor", lambda e, hl=hl, kt=kt, bqk=bqk, h0=h0,
                                  crossing=crossing: e.matmul(
                                      pb[bqk][:],
                                      KTld[:, (hl * 4 + kt) * 128:(hl * 4 + kt + 1) * 128],
                                      QTa[:, (h0 + hl) * G:(h0 + hl + 1) * G],
                                      start=(not crossing), stop=True),
                                  wait=(wq_ if not crossing else ()),
                                  inc=[(sT, 1)])
                            t_qk = sT.v
                            ppe = tileno % 4
                            s_exp = p.add("scalar", lambda e, ppe=ppe, bqk=bqk:
                                          e.activation(expS[ppe][:], pb[bqk][:],
                                                       AF.Exp, bias=c0t[:], scale=1.0),
                                          wait=[(sT, t_qk),
                                                grd.get(f"expS{ppe}")]
                                          + ([v_init] if tileno == 0 else []),
                                          inc=[(sS, 1)])[0]
                            grd[f"bank{bqk}"] = (sS, s_exp)
                            first = not av_started[hl]
                            av_started[hl] = True
                            last = (ks == g and kt == 3)
                            pending.append((hl, kt, ppe, s_exp, lwv, first, last,
                                            strip_new))
                            strip_new = False
                            tileno += 1
                            if len(pending) > LAG:
                                emit_av2(pending.pop(0))
                    while pending:
                        emit_av2(pending.pop(0))
                    grd["ktld2"] = (sT, sT.v)
                    grd["vld2"] = (sT, sT.v)
                t_av_done = sT.v
                for hl in range(4):
                    h = h0 + hl
                    s3 = p.add("scalar", lambda e, hl=hl: e.activation(
                        rows["recip"][:], pb[hl][64:65, :], AF.Ln,
                        bias=0.0, scale=1.0),
                        wait=[(sT, t_av_done), grd.get("reciprow")],
                        inc=[(sS, 1)])[0]
                    v4 = p.add("scalar", lambda e: e.activation(
                        rows_bf["recip"][:], rows["recip"][:], AF.Exp,
                        bias=0.0, scale=-1.0), inc=[(sS, 1)])[0]
                    bb = 6 + (hl % 2)
                    t_bc = p.add("tensor", lambda e, bb=bb: e.matmul(
                        pb[bb][0:64, :], onesr[0:1, 0:64], rows_bf["recip"][0:1, :],
                        start=True, stop=True),
                        wait=[(sS, v4), grd.get(f"pbb{bb}"), grd.get(f"bank{bb}")],
                        inc=[(sT, 1)])[0]
                    grd["reciprow"] = (sT, t_bc)
                    v4b = p.add("vector", lambda e, bb=bb: e.tensor_copy(
                        recb[0:64, :], pb[bb][0:64, :]),
                        wait=[(sT, t_bc)], inc=[(sV, 1)])[0]
                    grd[f"pbb{bb}"] = (sV, v4b)
                    grd[f"bank{bb}"] = (sV, v4b)
                    v5 = p.add("vector", lambda e, h=h, hl=hl: e.tensor_tensor(
                        OTbf[h // 2][(h % 2) * 64:(h % 2) * 64 + 64, :],
                        pb[hl][0:64, :], recb[0:64, :], ALU.mult),
                        inc=[(sV, 1)])[0]
                    grd[f"av{hl}"] = (sV, v5)
                    grd[f"bank{hl}"] = (sV, v5)
            v_attn2 = (sV, sV.v)
            grd["qta2_free"] = (sT, sT.v)

            for f in range(8):
                b = 2 + f % 2
                lw = c.dma("sync", wtile[:], wtile_ap(wsc["wo2"], f),
                           wait=[grd.get("wtile")])
                for i in range(8):
                    p.add("tensor", lambda e, i=i, b=b: e.matmul(
                        pb[b][:], wtile[:, i * 128:(i + 1) * 128], OTbf[i][:],
                        start=(i == 0), stop=(i == 7)),
                        wait=([lw, v_attn2, grd.get(f"bank{b}")] if i == 0 else ()),
                        inc=([(sT, 1)] if i == 7 else ()))
                t_mm = sT.v
                grd["wtile"] = (sT, t_mm)
                s_ev = SC(lambda e, b=b: e.activation(tmpf[1][:], pb[b][:], AF.Identity,
                                                      bias=0.0, scale=1.0),
                          [(sT, t_mm), grd.get("tmpf1")])
                grd[f"bank{b}"] = s_ev
                v_x2 = VS(lambda e, f=f, gl=gl: e.tensor_tensor(
                    x2g[f][:], tmpf[1][:], selT[f][:, gl:gl + G], ALU.add),
                    [s_ev, grd.get("x2g")])
                grd["tmpf1"] = v_x2
            v_x2d = (sV, sV.v)

            tb = TS(lambda e, gl=gl: e.matmul(
                pb[6][:], cst["onesr_f"][:], srow[0:1, gl:gl + G],
                start=True, stop=True), [v_scores, grd.get("pbb6")])
            v_srep = VS(lambda e: e.tensor_copy(srep[:], pb[6][:]),
                        [tb, grd.get("srep_free")])
            grd["pbb6"] = v_srep

            v_r2_ = ln_rms([x2g[i][:] for i in range(8)],
                           [h1bf[i][:] for i in range(8)], "g22", v_x2d)

            for fo in range(32):
                bG = 2 if fo % 2 == 0 else 6
                bU = 3 if fo % 2 == 0 else 7
                lwg = c.dma("sync", wtile[:], wtile_ap(wsc["wg2"], fo),
                            wait=[v_wcast, grd.get("wtile")])
                lwu = c.dma("sync", wcols[:, 0:1024], wtile_ap(wsc["wu2"], fo),
                            wait=[v_wcast, grd.get("wcols")])
                for i in range(8):
                    p.add("tensor", lambda e, i=i, bG=bG: e.matmul(
                        pb[bG][:], wtile[:, i * 128:(i + 1) * 128], h1bf[i][:],
                        start=(i == 0), stop=(i == 7)),
                        wait=([lwg, v_r2_, grd.get(f"bank{bG}")] if i == 0 else ()),
                        inc=([(sT, 1)] if i == 7 else ()))
                tG = sT.v
                for i in range(8):
                    p.add("tensor", lambda e, i=i, bU=bU: e.matmul(
                        pb[bU][:], wcols[:, i * 128:(i + 1) * 128], h1bf[i][:],
                        start=(i == 0), stop=(i == 7)),
                        wait=([lwu, grd.get(f"bank{bU}")] if i == 0 else ()),
                        inc=([(sT, 1)] if i == 7 else ()))
                tU = sT.v
                grd["wtile"] = (sT, tG)
                grd["wcols"] = (sT, tU)
                s_si = SC(lambda e, bG=bG: e.activation(sgbf[:], pb[bG][:], AF.Silu,
                                                        bias=0.0, scale=1.0),
                          [(sT, tG), grd.get("sgbf")])
                v_sw = VS(lambda e, fo=fo, bU=bU: e.tensor_tensor(
                    swig[:, fo * G:(fo + 1) * G], sgbf[:], pb[bU][:], ALU.mult),
                    [s_si, (sT, tU), grd.get("swig_free")])
                grd["sgbf"] = v_sw
                grd[f"bank{bG}"] = s_si
                grd[f"bank{bU}"] = v_sw
            v_swig = (sV, sV.v)

            for f in range(8):
                b = 2 + f % 2
                lw = c.dma("sync", wcols[:],
                           AP(wsc["wd2"], f * 128, [[D, 128], [128 * D, 32], [1, 128]]),
                           wait=[grd.get("wcols")])
                for i in range(32):
                    p.add("tensor", lambda e, i=i, b=b: e.matmul(
                        pb[b][:], wcols[:, i * 128:(i + 1) * 128],
                        swig[:, i * G:(i + 1) * G],
                        start=(i == 0), stop=(i == 31)),
                        wait=([lw, v_swig, grd.get(f"bank{b}")] if i == 0 else ()),
                        inc=([(sT, 1)] if i == 31 else ()))
                t_mm = sT.v
                grd["wcols"] = (sT, t_mm)
                s_ev = SC(lambda e, b=b: e.activation(tmpf[1][:], pb[b][:], AF.Identity,
                                                      bias=0.0, scale=1.0),
                          [(sT, t_mm), grd.get("tmpf1")])
                grd[f"bank{b}"] = s_ev
                v_y = VS(lambda e, f=f: e.tensor_tensor(
                    rtmp[:], tmpf[1][:], x2g[f][:], ALU.add),
                    [s_ev, grd.get("rtmp_y")])
                grd["pb2"] = v_y
                grd["tmpf1"] = v_y
                dby = c.dma("gpsimd", AP(DBG_Y, f * 128 * T2 + gl, [[T2, 128], [1, G]]),
                            rtmp[:], wait=[v_y])
                c.outs.append(dby)
                vd = VS(lambda e, f=f, gl=gl: e.tensor_tensor(
                    rtmp[:], rtmp[:], selT[f][:, gl:gl + G], ALU.subtract),
                    [dby, v_srep])
                vd = VS(lambda e: e.tensor_tensor(
                    rtmp[:], rtmp[:], srep[:], ALU.mult), [vd])
                tw = None
                for q in range(4):
                    tw = TS(lambda e, q=q: e.transpose(
                        pb[5][:, q * 128:(q + 1) * 128],
                        rtmp[:, q * 128:(q + 1) * 128], cst["idf"][:]),
                        [vd, grd.get("tr5d")] if q == 0 else [])
                ev = None
                for q in range(4):
                    ev = VS(lambda e, q=q, g=g, f=f: e.tensor_copy(
                        deltaN[4 * g + q][:, f * 128:(f + 1) * 128],
                        pb[5][:, q * 128:(q + 1) * 128]), [tw] if q == 0 else [])
                grd["tr5d"] = ev
                grd["rtmp_y"] = ev
            grd["swig_free"] = (sT, sT.v)
            grd["h1bf_free"] = (sT, sT.v)
            grd["x2g"] = (sV, sV.v)
            grd["trig"] = (sT, sT.v)
            grd["srep_free"] = (sV, sV.v)
        v_dN = (sV, sV.v)

        oring = [tmpf[0], tmpf[1], cosrep, sinrep, rtmp, srep]
        NR = len(oring)
        octr = {"x": 0, "p": 0}

        def oload(kind, f, jc, queue):
            i = octr[kind]
            octr[kind] += 1
            buf = oring[(i % (NR // 2)) * 2 + (0 if kind == "x" else 1)]
            src = X1T if kind == "x" else PROCT
            tok = c.dma(queue, buf[:],
                        AP(src, f * 128 * T + jc * 512, [[T, 128], [1, 512]]),
                        wait=[v_dN, grd.get(f"ob{kind}{i % (NR // 2)}")])
            return buf, tok, i % (NR // 2)

        for jc in range(8):
            for it in range(8):
                eb1 = VS(lambda e, it=it, jc=jc: e.tensor_scalar(
                    col1[:], selTc[:, it:it + 1], 512.0 * jc, None, ALU.subtract),
                    [esc, v_dN, grd.get("pcol")])
                eb2 = VS(lambda e: e.tensor_scalar(
                    PTt[:, 0:512], cst["iota1024_2d"][:, 0:512], col1[:], None,
                    ALU.is_equal), [eb1, grd.get("ptt")])
                grd["pcol"] = eb2
                for f in range(8):
                    p.add("tensor", lambda e, it=it, f=f: e.matmul(
                        pb[f][:], deltaN[it][:, f * 128:(f + 1) * 128], PTt[:, 0:512],
                        start=(it == 0), stop=(it == 7)),
                        wait=([eb2] + ([grd.get(f"sc{f}")] if it == 0 else [])),
                        inc=([(sT, 1)] if f == 7 else ()))
                grd["ptt"] = (sT, sT.v)
            t_sc = (sT, sT.v)
            for f in range(8):
                bx, ldx, ix = oload("x", f, jc, "sync")
                bp, ldp2, ip = oload("p", f, jc, "scalar")
                ea1 = VS(lambda e, f=f, bx=bx: e.tensor_tensor(
                    bx[:], bx[:], pb[f][:], ALU.add), [ldx, t_sc])
                grd[f"sc{f}"] = ea1
                ea2 = VS(lambda e, f=f, bx=bx, bp=bp: e.tensor_tensor(
                    bx[:], bx[:], bp[:], ALU.add), [ldp2, ea1])
                grd[f"obp{ip}"] = ea2
                dso = c.dma("gpsimd",
                            AP(OUT, f * 128 * T + jc * 512, [[T, 128], [1, 512]]),
                            bx[:], wait=[ea2])
                grd[f"obx{ix}"] = dso
                c.outs.append(dso)

        c.join("gpsimd", c.outs)
        p.barrier("end")
        for s_ in p.sems:
            s_.h = c.es.enter_context(nc.semaphore(s_.name))
        import os
        if bool(int(os.environ.get("MK_SIM_EDGES", "0"))):
            _ENG = ENGINES; _Sem = Sem
            c.p.ord = {en: _Sem("ord_" + en) for en in _ENG}
            for s_ in c.p.ord.values():
                s_.h = c.es.enter_context(nc.semaphore(s_.name))
            c.p.emit(c.es.enter_context(nc.Block()), sim_edges=True)
        else:
            c.p.emit(c.es.enter_context(nc.Block()))
    return nc, c


def stage2_inputs(I, st1_out):
    colshape = lambda v: np.ascontiguousarray(v.reshape(8, 128).T)
    m = {
        "x1T": st1_out["x1T"], "procT": st1_out["procT"],
        "dst": st1_out["dst"], "dch": st1_out["dch"],
        "allsums": st1_out["allsums"],
        "wq2p": permute_w_rope(I["L0_ffn_wq2"]),
        "wk2p": permute_w_rope(I["L0_ffn_wk2"]),
        "wv2": I["L0_ffn_wv2"], "wo2": I["L0_ffn_wo2"],
        "wg2": I["L0_ffn_wg2"], "wu2": I["L0_ffn_wu2"], "wd2": I["L0_ffn_wd2"],
        "g12": colshape(I["L0_ffn_ln1_2"]), "g22": colshape(I["L0_ffn_ln2_2"]),
    }
    for n, v in host_consts2().items():
        m["c_" + n] = np.asarray(v)
    return m


from concourse.bass_utils import run_bass_kernel_spmd


def kernel(**inputs):
    I = {k: np.asarray(v) for k, v in inputs.items()}
    nc1, _c1 = build_stage1(n_groups=NG)
    maps1 = [stage1_inputs(I, b) for b in range(8)]
    res1 = run_bass_kernel_spmd(nc1, maps1, core_ids=list(range(8)))
    r1 = [res1.results[b] for b in range(8)]
    allsums = np.zeros((1, 16), np.float32)
    for b in range(8):
        allsums[0, b] = r1[b]["dsa"][0, 0]
        allsums[0, 8 + b] = r1[b]["dsb"][0, 0]
    nc2, _c2 = build_stage2()
    maps2 = []
    for b in range(8):
        st1 = dict(x1T=r1[b]["x1T"], procT=r1[b]["procT"],
                   dst=r1[b]["dst"], dch=r1[b]["dch"], allsums=allsums)
        maps2.append(stage2_inputs(I, st1))
    res2 = run_bass_kernel_spmd(nc2, maps2, core_ids=list(range(8)))
    out = np.empty((8, 4096, 1024), np.float32)
    for b in range(8):
        out[b] = res2.results[b]["outT"].T
    return out
